# version 25
# baseline (speedup 1.0000x reference)
import numpy as np
import concourse.bass as bass
import concourse.mybir as mybir
import concourse.bass_utils as _bu
from concourse.bass_utils import run_bass_kernel_spmd
from contextlib import ExitStack

LDWOPT = True

if LDWOPT and not getattr(_bu, "_ldwopt_patched", False):
    _orig_run_command = _bu.run_command

    def _run_command_ldwopt(argv, **kw):
        argv = ["--enable-ldw-opt=true" if a == "--enable-ldw-opt=false" else a
                for a in argv]
        return _orig_run_command(argv, **kw)

    _bu.run_command = _run_command_ldwopt
    _bu._ldwopt_patched = True

F32 = mybir.dt.float32
BF16 = mybir.dt.bfloat16
AF = mybir.ActivationFunctionType
ALU = mybir.AluOpType
AX = mybir.AxisListType

NCORES = 8
T = 1024
H = 4096
HC = H // 128
NH, NKV, HD = 32, 8, 128
KD = NKV * HD
II = 11008
ICN = II // 128
ICA = 43
NTC = T // 128
EPS_LN = 1e-5
EPS_ATT = 1e-10

DBG = False


class Sem:
    __slots__ = ("h", "n")

    def __init__(self, h):
        self.h = h
        self.n = 0

    def inc(self, instr, k=1):
        instr.then_inc(self.h, k)
        self.n += k
        return (self, self.n)


def wait(eng, *evts):
    best = {}
    for e in evts:
        if e is None:
            continue
        s, v = e
        if v > 0 and v > best.get(id(s), (s, 0))[1]:
            best[id(s)] = (s, v)
    for s, v in best.values():
        eng.wait_ge(s.h, v)


def build_nc(dbg=False):
    nc = bass.Bass("TRN2", target_bir_lowering=False)

    x_in = nc.dram_tensor("x", [T, H], F32, kind="ExternalInput").ap()
    wq = nc.dram_tensor("wq", [H, H], F32, kind="ExternalInput").ap()
    wk = nc.dram_tensor("wk", [KD, H], F32, kind="ExternalInput").ap()
    wv = nc.dram_tensor("wv", [KD, H], F32, kind="ExternalInput").ap()
    wo = nc.dram_tensor("wo", [H, H], F32, kind="ExternalInput").ap()
    wg = nc.dram_tensor("wg", [II, H], F32, kind="ExternalInput").ap()
    wu = nc.dram_tensor("wu", [II, H], F32, kind="ExternalInput").ap()
    wd = nc.dram_tensor("wd", [H, II], F32, kind="ExternalInput").ap()
    bq = nc.dram_tensor("bq", [128, HC], F32, kind="ExternalInput").ap()
    bk = nc.dram_tensor("bk", [128, NKV], F32, kind="ExternalInput").ap()
    bv = nc.dram_tensor("bv", [128, NKV], F32, kind="ExternalInput").ap()
    bo = nc.dram_tensor("bo", [128, HC], F32, kind="ExternalInput").ap()
    ident_in = nc.dram_tensor("ident_in", [128, 128], F32, kind="ExternalInput").ap()
    y_out = nc.dram_tensor("y", [T, H], F32, kind="ExternalOutput").ap()

    sck = "ExternalOutput" if dbg else "Internal"
    f_sc = nc.dram_tensor("f_sc", [H, T], F32, kind=sck).ap()
    g_sc = nc.dram_tensor("g_sc", [H, T], BF16, kind=sck).ap()
    p_sc = nc.dram_tensor("p_sc", [H, T], BF16, kind=sck).ap()
    xn_sc = nc.dram_tensor("xn_sc", [T, H], BF16, kind="Internal").ap()
    cc_in = nc.dram_tensor("cc_in", [128, KD + NKV], F32, kind="Internal").ap()
    cc_out = nc.dram_tensor("cc_out", [128, KD + NKV], F32, kind="Internal").ap()
    if dbg:
        cc_dump = nc.dram_tensor("cc_dump", [128, KD + NKV], F32, kind="ExternalOutput").ap()
        gu_dump = nc.dram_tensor("gu_dump", [128, ICA * T], BF16, kind="ExternalOutput").ap()
        xn_dump = nc.dram_tensor("xn_dump", [128, HC * T], BF16, kind="ExternalOutput").ap()
        q_dump = nc.dram_tensor("q_dump", [128, NH * T], BF16, kind="ExternalOutput").ap()
        ktm_dump = nc.dram_tensor("ktm_dump", [128, NTC * KD], BF16, kind="ExternalOutput").ap()
        vtm_dump = nc.dram_tensor("vtm_dump", [128, NTC * KD], BF16, kind="ExternalOutput").ap()
        kvg_dump = nc.dram_tensor("kvg_dump", [128, KD + NKV], F32, kind="ExternalOutput").ap()

    with ExitStack() as stack:
        sb_outer = lambda name, shape, dt: stack.enter_context(nc.sbuf_tensor(name, shape, dt))
        ident = sb_outer("ident", [128, 128], BF16)
        bq_sb = sb_outer("bq_sb", [128, HC], F32)
        bk_sb = sb_outer("bk_sb", [128, NKV], F32)
        bv_sb = sb_outer("bv_sb", [128, NKV], F32)
        bo_sb = sb_outer("bo_sb", [128, HC], F32)
        kvg_sb = sb_outer("kvg_sb", [128, KD + NKV], F32)
        kv_bf = sb_outer("kv_bf", [128, KD], BF16)
        rec_sb = sb_outer("rec_sb", [128, NKV], F32)
        kvpack = sb_outer("kvpack", [128, KD + NKV], F32)
        sb_outer("cachebust_ldwopt%d" % int(LDWOPT), [128, 1], F32)

        sW = Sem(stack.enter_context(nc.semaphore("sW")))
        sSY = Sem(stack.enter_context(nc.semaphore("sSY")))
        sPE = Sem(stack.enter_context(nc.semaphore("sPE")))
        sDVE = Sem(stack.enter_context(nc.semaphore("sDVE")))
        sACT = Sem(stack.enter_context(nc.semaphore("sACT")))
        sPL = Sem(stack.enter_context(nc.semaphore("sPL")))
        sCC = Sem(stack.enter_context(nc.semaphore("sCC")))
        sSC = Sem(stack.enter_context(nc.semaphore("sSC")))
        sAp = Sem(stack.enter_context(nc.semaphore("sAp")))
        sBp = Sem(stack.enter_context(nc.semaphore("sBp")))
        sCp = Sem(stack.enter_context(nc.semaphore("sCp")))

        sDp = Sem(stack.enter_context(nc.semaphore("sDp")))
        sYa = Sem(stack.enter_context(nc.semaphore("sYa")))
        sYb = Sem(stack.enter_context(nc.semaphore("sYb")))
        sXNa = Sem(stack.enter_context(nc.semaphore("sXNa")))
        sXNb = Sem(stack.enter_context(nc.semaphore("sXNb")))

        def ssem(i):
            return sAp if i % 2 == 0 else sBp

        def ssem2(i):
            return sCp if i % 2 == 0 else sDp
        ALLSEMS = [sW, sSY, sPE, sDVE, sACT, sPL, sSC, sAp, sBp, sCp, sDp, sYa, sYb,
                   sXNa, sXNb]

        def barrier():
            return [(s, s.n) for s in ALLSEMS]

        def vch(v, ins):
            e = sDVE.inc(ins, 1)
            v.wait_ge(sDVE.h, e[1])
            return e

        def ach(a, ins):
            e = sACT.inc(ins, 1)
            a.wait_ge(sACT.h, e[1])
            return e

        def twait(t, *evts):
            wait(t, *evts)
            t.nop(nofuse=True)

        with nc.Block() as block:
            TT, VV, AA, SY, GP = block.tensor, block.vector, block.scalar, block.sync, block.gpsimd

            evIdent = [None]

            @GP
            def _(g):
                evIdent[0] = sPL.inc(g.dma_start(out=ident[:, :], in_=ident_in), 16)
                sW.inc(g.dma_start(out=bq_sb[:, :], in_=bq), 16)
                sW.inc(g.dma_start(out=bk_sb[:, :], in_=bk), 16)
                sW.inc(g.dma_start(out=bv_sb[:, :], in_=bv), 16)
                sW.inc(g.dma_start(out=bo_sb[:, :], in_=bo), 16)

            with nc.sbuf_tensor("xn_fm", [128, HC * T], BF16) as xn_fm_t:
                xn_fm = xn_fm_t[:, :]
                evXnDone = [None]
                evXnH0 = [None]
                evXnH1 = [None]
                evXnFree = {}
                slot_of = {}
                evPtW = [None] * 4
                evPtR = [None] * 4
                tr = [0]

                with ExitStack() as st_tr:
                    pt = [st_tr.enter_context(nc.psum_tensor("pt%d" % i, [128, 1024], BF16))
                          for i in range(4)]
                    xn_st = st_tr.enter_context(nc.sbuf_tensor("xn_st", [128, 4 * H], BF16))
                    wk_db = st_tr.enter_context(nc.sbuf_tensor("wk_db", [128, 2 * H], BF16))
                    wv_db = st_tr.enter_context(nc.sbuf_tensor("wv_db", [128, 2 * H], BF16))

                    with ExitStack() as st0:
                        st0_keep = []
                        x_db = st0.enter_context(nc.sbuf_tensor("x_db", [128, 4 * H], F32))
                        bnst = st0.enter_context(nc.sbuf_tensor("bnst", [128, 48], F32))
                        p0t = st0.enter_context(nc.sbuf_tensor("p0t", [128, 8], F32))
                        rstd_all = st0.enter_context(nc.sbuf_tensor("rstd_all", [128, NTC], F32))
                        evX = {}
                        evFreeX = {}
                        evXsq = {}
                        evVar = {}
                        evRstd = {}
                        evXnStage = {}

                        xn_fm3 = xn_fm_t[:, :].rearrange("p (hcx tt) -> p hcx tt", tt=T)

                        def p0_transposes(tc):
                            sl = tc % 4
                            for gq in range(8):
                                @TT
                                def _(t, tc=tc, sl=sl, gq=gq):
                                    j = tr[0] % 4
                                    twait(t, evIdent[0], evXnStage[tc], evPtR[j])
                                    ins = None
                                    for k in range(4):
                                        hc = gq * 4 + k
                                        ins = t.transpose(
                                            pt[j][:, k * 128:(k + 1) * 128],
                                            xn_st[:, sl * H + hc * 128: sl * H + (hc + 1) * 128],
                                            ident[:, :])
                                    evPtW[j] = sPE.inc(ins, 1)
                                    slot_of[("p0g", tc, gq)] = j
                                    tr[0] += 1

                                @VV
                                def _(v, tc=tc, gq=gq):
                                    j = slot_of[("p0g", tc, gq)]
                                    wait(v, evPtW[j])
                                    evPtR[j] = sDVE.inc(v.tensor_copy(
                                        xn_fm3[:, gq * 4:(gq + 1) * 4, tc * 128:(tc + 1) * 128],
                                        pt[j].rearrange("p (k f) -> p k f", f=128)[:, 0:4, :]), 1)

                        @SY
                        def _(q):
                            for t0 in range(3):
                                evX[t0] = ssem(t0).inc(q.dma_start(
                                    out=x_db[:, t0 * H:(t0 + 1) * H],
                                    in_=x_in[t0 * 128:(t0 + 1) * 128, :]), 16)

                        for tc in range(NTC):
                            sl = tc % 4

                            @SY
                            def _(q, tc=tc):
                                if tc + 3 < NTC:
                                    s2 = (tc + 3) % 4
                                    wait(q, evFreeX.get(tc - 1))
                                    evX[tc + 3] = ssem(tc + 3).inc(q.dma_start(
                                        out=x_db[:, s2 * H:(s2 + 1) * H],
                                        in_=x_in[(tc + 3) * 128:(tc + 4) * 128, :]), 16)

                            @VV
                            def _(v, tc=tc, sl=sl):
                                wait(v, evX[tc])
                                for gq in range(8):
                                    ins = v.bn_stats(bnst[:, gq * 6:(gq + 1) * 6],
                                                     x_db[:, sl * H + gq * 512: sl * H + (gq + 1) * 512])
                                vch(v, ins)
                                vch(v, v.bn_aggr(p0t[:, 0:2], bnst[:, :]))
                                evVar[tc] = vch(v, v.tensor_single_scalar(
                                    p0t[:, 2:3], p0t[:, 1:2], EPS_LN, op=ALU.add))

                            @AA
                            def _(a, tc=tc):
                                wait(a, evVar[tc])
                                ach(a, a.activation(out=p0t[:, 3:4], in_=p0t[:, 2:3], func=AF.Ln))
                                evRstd[tc] = sACT.inc(a.activation(
                                    out=rstd_all[:, tc:tc + 1], in_=p0t[:, 3:4],
                                    func=AF.Exp, scale=-0.5), 1)

                            @VV
                            def _(v, tc=tc, sl=sl):
                                wait(v, evRstd[tc], evXnFree.get(tc - 4))
                                e = sDVE.inc(v.tensor_scalar(
                                    xn_st[:, sl * H:(sl + 1) * H], x_db[:, sl * H:(sl + 1) * H],
                                    p0t[:, 0:1], rstd_all[:, tc:tc + 1],
                                    op0=ALU.subtract, op1=ALU.mult), 1)
                                evXnStage[tc] = e
                                evFreeX[tc] = e

                            if tc < 4:
                                p0_transposes(tc)
                                evXnFree[tc] = (sPE, sPE.n)
                            if tc == 3:
                                evXnH0[0] = (sDVE, sDVE.n)



                    evKvpack = [None]
                    B1 = [(x, x.n) for x in ALLSEMS if x not in (sXNa, sXNb)]
                    with ExitStack() as st1:
                        k_tm = st1.enter_context(nc.sbuf_tensor("k_tm", [128, NTC * KD], BF16))
                        v_tm = st1.enter_context(nc.sbuf_tensor("v_tm", [128, NTC * KD], BF16))
                        ksum = st1.enter_context(nc.sbuf_tensor("ksum", [128, NKV], F32))
                        with ExitStack() as st1b:
                            kst = st1b.enter_context(nc.sbuf_tensor("kst", [128, 1024], F32))
                            est = st1b.enter_context(nc.sbuf_tensor("est", [128, 1024], F32))
                            rst = st1b.enter_context(nc.sbuf_tensor("rst", [128, 1024], F32))
                            kphiK = st1b.enter_context(nc.sbuf_tensor("kphiK", [128, 1024], BF16))
                            kphiV = st1b.enter_context(nc.sbuf_tensor("kphiV", [128, 1024], BF16))
                            ksum_h = st1b.enter_context(nc.sbuf_tensor("ksum_h", [128, 2 * NKV], F32))
                            pkK = [st1b.enter_context(nc.psum_tensor("pkK%d" % i, [128, 512], F32))
                                   for i in range(2)]
                            pkV = [st1b.enter_context(nc.psum_tensor("pkV%d" % i, [128, 512], F32))
                                   for i in range(2)]
                            evKmm = {}
                            evTsK = {}
                            evActK = {}
                            evPhi = {}
                            evWkv = {}
                            wbuf = [wk_db, wv_db]
                            wdr = [wk, wv]

                            @GP
                            def _(g):
                                wait(g, *B1)
                                for oc in (0, 1):
                                    for kvi in (0, 1):
                                        evWkv[(kvi, oc)] = (ssem(oc) if kvi == 0 else ssem2(oc)).inc(g.dma_start(
                                            out=wbuf[kvi][:, (oc % 2) * H:(oc % 2 + 1) * H],
                                            in_=wdr[kvi][oc * 128:(oc + 1) * 128, :]), 16)

                            k_tm3 = k_tm[:, :].rearrange("p (lc f) -> p lc f", f=KD)
                            v_tm3 = v_tm[:, :].rearrange("p (lc f) -> p lc f", f=KD)

                            def p1_transposes(oc, half):
                                for kvi, src, dst3 in ((0, kphiK, k_tm3), (1, kphiV, v_tm3)):
                                    @TT
                                    def _(t, oc=oc, kvi=kvi, src=src, half=half):
                                        j = tr[0] % 4
                                        twait(t, evPhi[(kvi, oc, half)], evPtR[j])
                                        ins = None
                                        for j4 in range(4):
                                            ins = t.transpose(
                                                pt[j][:, j4 * 128:(j4 + 1) * 128],
                                                src[:, half * 512 + j4 * 128: half * 512 + (j4 + 1) * 128],
                                                ident[:, :])
                                        evPtW[j] = sPE.inc(ins, 1)
                                        slot_of[("p1g", kvi, oc, half)] = j
                                        tr[0] += 1

                                    @VV
                                    def _(v, oc=oc, kvi=kvi, half=half, dst3=dst3):
                                        j = slot_of[("p1g", kvi, oc, half)]
                                        wait(v, evPtW[j])
                                        evPtR[j] = sDVE.inc(v.tensor_copy(
                                            dst3[:, half * 4:(half + 1) * 4, oc * 128:(oc + 1) * 128],
                                            pt[j].rearrange("p (k f) -> p k f", f=128)[:, 0:4, :]), 1)

                            first_pe = [True]
                            lastK = {}
                            lastV = {}
                            evWkvA = {}

                            @GP
                            def _(g):
                                for oc in (0, 1):
                                    for kvi in (0, 1):
                                        evWkvA[(kvi, 0, oc)] = (ssem(oc) if kvi == 0 else ssem2(oc)).inc(
                                            g.dma_start(
                                                out=wbuf[kvi][:, (oc % 2) * H:(oc % 2 + 1) * H],
                                                in_=wdr[kvi][oc * 128:(oc + 1) * 128, :]), 16)

                            for half in (0, 1):
                                for oc in range(NKV):
                                    gidx = half * NKV + oc
                                    sl = gidx % 2

                                    if oc > 0:
                                        p1_transposes(oc - 1, half)
                                    elif half == 1:
                                        p1_transposes(NKV - 1, 0)
                                    if half == 0 and 2 <= oc <= 5:
                                        p0_transposes(oc + 2)
                                        if oc == 5:
                                            evXnH1[0] = (sDVE, sDVE.n)

                                    @TT
                                    def _(t, oc=oc, sl=sl, half=half):
                                        bank = oc % 2
                                        twait(t, evWkvA[(0, half, oc)],
                                              evXnH0[0] if half == 0 else evXnH1[0],
                                              lastK.get(bank))
                                        last = None
                                        for hc in range(HC):
                                            last = t.matmul(
                                                pkK[bank][:, :],
                                                wk_db[:, sl * H + hc * 128: sl * H + (hc + 1) * 128],
                                                xn_fm[:, hc * T + half * 512: hc * T + half * 512 + 512],
                                                start=(hc == 0), stop=(hc == HC - 1))
                                        evKmm[(0, oc, half)] = sPE.inc(last, 1)

                                    @TT
                                    def _(t, oc=oc, sl=sl, half=half):
                                        bank = oc % 2
                                        twait(t, evWkvA[(1, half, oc)], lastV.get(bank))
                                        last = None
                                        for hc in range(HC):
                                            last = t.matmul(
                                                pkV[bank][:, :],
                                                wv_db[:, sl * H + hc * 128: sl * H + (hc + 1) * 128],
                                                xn_fm[:, hc * T + half * 512: hc * T + half * 512 + 512],
                                                start=(hc == 0), stop=(hc == HC - 1))
                                        evKmm[(1, oc, half)] = sPE.inc(last, 1)

                                    @VV
                                    def _(v, oc=oc, half=half):
                                        bank = oc % 2
                                        wait(v, evKmm[(0, oc, half)], evActK.get((0, oc - 2, half)))
                                        evTsK[(oc, half)] = sDVE.inc(v.tensor_scalar(
                                            kst[:, half * 512:(half + 1) * 512],
                                            pkK[bank][:, :],
                                            bk_sb[:, oc:oc + 1], 0.0,
                                            op0=ALU.add, op1=ALU.min), 1)

                                    @AA
                                    def _(a, oc=oc, half=half):
                                        bank = oc % 2
                                        wait(a, evTsK[(oc, half)],
                                             evPhi.get((0, oc - 2, half)))
                                        a.activation(out=est[:, half * 512:(half + 1) * 512],
                                                     in_=kst[:, half * 512:(half + 1) * 512],
                                                     func=AF.Exp)
                                        evActK[(0, oc, half)] = sACT.inc(a.activation(
                                            out=rst[:, half * 512:(half + 1) * 512],
                                            in_=pkK[bank][:, :],
                                            func=AF.Relu, bias=bk_sb[:, oc:oc + 1]), 1)
                                    lastK[oc % 2] = (sACT, sACT.n)

                                    @VV
                                    def _(v, oc=oc, half=half):
                                        wait(v, evActK[(0, oc, half)])
                                        evPhi[(0, oc, half)] = vch(v, v.tensor_add(
                                            kphiK[:, half * 512:(half + 1) * 512],
                                            rst[:, half * 512:(half + 1) * 512],
                                            est[:, half * 512:(half + 1) * 512]))
                                        vch(v, v.reduce_sum(
                                            ksum_h[:, half * NKV + oc: half * NKV + oc + 1],
                                            kphiK[:, half * 512:(half + 1) * 512],
                                            axis=AX.X))
                                        if half == 1:
                                            vch(v, v.tensor_add(
                                                ksum[:, oc:oc + 1],
                                                ksum_h[:, oc:oc + 1],
                                                ksum_h[:, NKV + oc:NKV + oc + 1]))

                                    @VV
                                    def _(v, oc=oc, half=half):
                                        bank = oc % 2
                                        wait(v, evKmm[(1, oc, half)])
                                        evPhi[(1, oc, half)] = sDVE.inc(v.tensor_scalar(
                                            kphiV[:, half * 512:(half + 1) * 512],
                                            pkV[bank][:, :],
                                            bv_sb[:, oc:oc + 1], None, op0=ALU.add), 1)
                                    lastV[oc % 2] = (sDVE, sDVE.n)

                                    @GP
                                    def _(g, oc=oc, half=half, gidx=gidx):
                                        if gidx + 2 < 2 * NKV:
                                            g2 = gidx + 2
                                            h2, oc2 = divmod(g2, NKV)
                                            for kvi in (0, 1):
                                                wait(g, evKmm[(kvi, oc, half)])
                                                evWkvA[(kvi, h2, oc2)] = (ssem(g2) if kvi == 0 else ssem2(g2)).inc(
                                                    g.dma_start(
                                                        out=wbuf[kvi][:, (g2 % 2) * H:(g2 % 2 + 1) * H],
                                                        in_=wdr[kvi][oc2 * 128:(oc2 + 1) * 128, :]), 16)

                            p1_transposes(NKV - 1, 1)

                            evKVtmDone = (sDVE, sDVE.n)

                        with nc.psum_tensor("pkv0", [128, 512], F32) as pkv0, \
                             nc.psum_tensor("pkv1", [128, 512], F32) as pkv1:
                            pkv = [pkv0, pkv1]
                            evKVp = [None]

                            @TT
                            def _(t):
                                twait(t, evKVtmDone)
                                last = None
                                for h in range(NKV):
                                    for lc in range(NTC):
                                        last = t.matmul(
                                            pkv[h // 4][:, (h % 4) * 128:(h % 4 + 1) * 128],
                                            k_tm[:, lc * KD + h * 128:lc * KD + (h + 1) * 128],
                                            v_tm[:, lc * KD + h * 128:lc * KD + (h + 1) * 128],
                                            start=(lc == 0), stop=(lc == NTC - 1))
                                evKVp[0] = sPE.inc(last, 1)

                            @VV
                            def _(v):
                                wait(v, evKVp[0])
                                v.tensor_copy(kvpack[:, 0:512], pkv0[:, :])
                                v.tensor_copy(kvpack[:, 512:1024], pkv1[:, :])
                                evKvpack[0] = sDVE.inc(v.tensor_copy(
                                    kvpack[:, 1024:1024 + NKV], ksum[:, :]), 1)

                            if dbg:
                                @SY
                                def _(q):
                                    wait(q, evXnH1[0])
                                    sSY.inc(q.dma_start(out=xn_dump, in_=xn_fm), 16)
                                    wait(q, evKVtmDone)
                                    sSY.inc(q.dma_start(out=ktm_dump, in_=k_tm[:, :]), 16)
                                    sSY.inc(q.dma_start(out=vtm_dump, in_=v_tm[:, :]), 16)

                @GP
                def _(g):
                    wait(g, evKvpack[0])
                    sW.inc(g.dma_start(out=cc_in, in_=kvpack[:, :]), 16)
                    g.wait_ge(sW.h, sW.n)
                    ins = g.collective_compute(
                        "AllReduce", ALU.add,
                        replica_groups=[list(range(NCORES))],
                        ins=[cc_in], outs=[cc_out])
                    ins.then_inc(sCC.h, 1)
                    sCC.n += 1

                evGate = {}
                evWgH = {}
                evWdH = {}
                evUp = {}
                evSilu = {}
                evGuEvac = {}
                evWg = {}
                evWu = {}
                evF = {}
                evFEvac = {}
                evFSt = {}
                evFRead = {}
                evAdd = {}
                evG = {}
                evGSt = {}
                evWd = {}

                with nc.sbuf_tensor("gu", [128, ICA * T], BF16) as gu_t:
                    gu = gu_t[:, :]

                    def gateup(mega):
                        icl = list(range(0, ICA)) if mega == 0 else list(range(ICA, ICN))
                        BB = barrier()
                        with ExitStack() as ctx:
                            wg_db = ctx.enter_context(nc.sbuf_tensor("wg_db%d" % mega, [128, 2 * H], BF16))
                            wu_db = ctx.enter_context(nc.sbuf_tensor("wu_db%d" % mega, [128, 2 * H], BF16))
                            sg_st = ctx.enter_context(nc.sbuf_tensor("sg_st%d" % mega, [128, T], F32))
                            pg = [ctx.enter_context(nc.psum_tensor("pg%d_%d" % (i, mega), [128, 1024], F32))
                                  for i in range(2)]
                            pu = [ctx.enter_context(nc.psum_tensor("pu%d_%d" % (i, mega), [128, 1024], F32))
                                  for i in range(2)]

                            @GP
                            def _(g):
                                wait(g, *BB)
                                ic0 = icl[0]
                                sl0 = ic0 % 2
                                evWgH[ic0] = ssem(ic0).inc(g.dma_start(
                                    out=wg_db[:, sl0 * H: sl0 * H + H // 2],
                                    in_=wg[ic0 * 128:(ic0 + 1) * 128, 0:H // 2]), 16)
                                evWg[ic0] = sW.inc(g.dma_start(
                                    out=wg_db[:, sl0 * H + H // 2: sl0 * H + H],
                                    in_=wg[ic0 * 128:(ic0 + 1) * 128, H // 2:H]), 16)
                                evWu[ic0] = ssem2(ic0).inc(g.dma_start(
                                    out=wu_db[:, sl0 * H:(sl0 + 1) * H],
                                    in_=wu[ic0 * 128:(ic0 + 1) * 128, :]), 16)
                                for ic in icl[1:2]:
                                    evWg[ic] = ssem(ic).inc(g.dma_start(
                                        out=wg_db[:, (ic % 2) * H:(ic % 2 + 1) * H],
                                        in_=wg[ic * 128:(ic + 1) * 128, :]), 16)
                                    evWu[ic] = ssem2(ic).inc(g.dma_start(
                                        out=wu_db[:, (ic % 2) * H:(ic % 2 + 1) * H],
                                        in_=wu[ic * 128:(ic + 1) * 128, :]), 16)

                            first = [True, True, True]
                            for idx, ic in enumerate(icl):
                                sl = ic % 2

                                @TT
                                def _(t, ic=ic, sl=sl, idx=idx):
                                    if first[0]:
                                        twait(t, *BB)
                                        first[0] = False
                                    if idx == 0:
                                        twait(t, evWgH[ic])
                                    else:
                                        twait(t, evWg[ic])
                                    if idx >= 2:
                                        prev = icl[idx - 2]
                                        twait(t, evSilu.get(prev), evGuEvac.get(prev))
                                    last = None
                                    for hc in range(HC):
                                        if idx == 0 and hc == HC // 2:
                                            twait(t, evWg[ic])
                                        for half in (0, 1):
                                            last = t.matmul(
                                                pg[sl][:, half * 512:(half + 1) * 512],
                                                wg_db[:, sl * H + hc * 128: sl * H + (hc + 1) * 128],
                                                xn_fm[:, hc * T + half * 512: hc * T + half * 512 + 512],
                                                start=(hc == 0), stop=(hc == HC - 1))
                                    evGate[ic] = sPE.inc(last, 1)
                                    twait(t, evWu[ic])
                                    for hc in range(HC):
                                        for half in (0, 1):
                                            last = t.matmul(
                                                pu[sl][:, half * 512:(half + 1) * 512],
                                                wu_db[:, sl * H + hc * 128: sl * H + (hc + 1) * 128],
                                                xn_fm[:, hc * T + half * 512: hc * T + half * 512 + 512],
                                                start=(hc == 0), stop=(hc == HC - 1))
                                    evUp[ic] = sPE.inc(last, 1)

                                @GP
                                def _(g, idx=idx):
                                    if idx + 2 < len(icl):
                                        ic2 = icl[idx + 2]
                                        wait(g, evGate[icl[idx]], evUp[icl[idx]])
                                        evWg[ic2] = ssem(ic2).inc(g.dma_start(
                                            out=wg_db[:, (ic2 % 2) * H:(ic2 % 2 + 1) * H],
                                            in_=wg[ic2 * 128:(ic2 + 1) * 128, :]), 16)
                                        evWu[ic2] = ssem2(ic2).inc(g.dma_start(
                                            out=wu_db[:, (ic2 % 2) * H:(ic2 % 2 + 1) * H],
                                            in_=wu[ic2 * 128:(ic2 + 1) * 128, :]), 16)

                                @AA
                                def _(a, ic=ic, sl=sl, idx=idx):
                                    if first[1]:
                                        wait(a, *BB)
                                        first[1] = False
                                    wait(a, evGate[ic])
                                    if idx >= 1:
                                        wait(a, evGuEvac.get(icl[idx - 1]))
                                    evSilu[ic] = sACT.inc(a.activation(
                                        out=sg_st[:, :], in_=pg[sl][:, :], func=AF.Silu), 1)

                                @VV
                                def _(v, ic=ic, sl=sl, idx=idx):
                                    if first[2]:
                                        wait(v, *BB)
                                        first[2] = False
                                    wait(v, evSilu[ic], evUp[ic])
                                    evGuEvac[ic] = sDVE.inc(v.tensor_mul(
                                        gu[:, idx * T:(idx + 1) * T], sg_st[:, :], pu[sl][:, :]), 1)

                    def down_phase(mega, g_write):
                        icl = list(range(0, ICA)) if mega == 0 else list(range(ICA, ICN))
                        nIC = len(icl)
                        CW = nIC * 128
                        lastGu = evGuEvac[icl[-1]]
                        BB = barrier()
                        with ExitStack() as ctx:
                            wd_db = ctx.enter_context(nc.sbuf_tensor("wd_db%d" % mega, [128, 2 * CW], BF16))
                            if g_write:
                                frd = ctx.enter_context(nc.sbuf_tensor("frd", [128, 2 * T], F32))
                                tmpf = ctx.enter_context(nc.sbuf_tensor("tmpf", [128, 2 * T], F32))
                                g_st = ctx.enter_context(nc.sbuf_tensor("g_st", [128, 2 * T], BF16))
                            else:
                                f_st = ctx.enter_context(nc.sbuf_tensor("f_st%d" % mega, [128, 2 * T], F32))
                            pf = [ctx.enter_context(nc.psum_tensor("pf%d_%d" % (i, mega), [128, 1024], F32))
                                  for i in range(4)]

                            @GP
                            def _(g):
                                wait(g, *BB)
                                cw2 = CW // 2
                                evWdH[mega] = ssem(0).inc(g.dma_start(
                                    out=wd_db[:, 0:cw2],
                                    in_=wd[0:128, mega * ICA * 128: mega * ICA * 128 + cw2]), 16)
                                evWd[(mega, 0)] = sW.inc(g.dma_start(
                                    out=wd_db[:, cw2:CW],
                                    in_=wd[0:128, mega * ICA * 128 + cw2: mega * ICA * 128 + CW]), 16)
                                evWd[(mega, 1)] = ssem(1).inc(g.dma_start(
                                    out=wd_db[:, CW:2 * CW],
                                    in_=wd[128:256, mega * ICA * 128: mega * ICA * 128 + CW]), 16)

                            if g_write:
                                @SY
                                def _(q):
                                    wait(q, *BB)
                                    for oc in (0, 1):
                                        wait(q, evFSt[oc])
                                        evFRead[oc] = ssem2(oc).inc(q.dma_start(
                                            out=frd[:, (oc % 2) * T:(oc % 2 + 1) * T],
                                            in_=f_sc[oc * 128:(oc + 1) * 128, :]), 16)

                            first = [True, True, True, True]
                            for oc in range(HC):
                                sl = oc % 2

                                @TT
                                def _(t, oc=oc, sl=sl):
                                    if first[0]:
                                        twait(t, *BB)
                                        first[0] = False
                                    if oc == 0:
                                        twait(t, evWdH[mega], lastGu)
                                    else:
                                        twait(t, evWd[(mega, oc)], lastGu)
                                    if g_write:
                                        twait(t, evAdd.get(oc - 4))
                                    else:
                                        twait(t, evFEvac.get(oc - 4))
                                    pj = pf[oc % 4]
                                    last = None
                                    for li in range(nIC):
                                        if oc == 0 and li == nIC // 2:
                                            twait(t, evWd[(mega, 0)])
                                        for half in (0, 1):
                                            last = t.matmul(
                                                pj[:, half * 512:(half + 1) * 512],
                                                wd_db[:, sl * CW + li * 128: sl * CW + (li + 1) * 128],
                                                gu[:, li * T + half * 512: li * T + half * 512 + 512],
                                                start=(li == 0), stop=(li == nIC - 1))
                                    evF[(mega, oc)] = sPE.inc(last, 1)

                                @GP
                                def _(g, oc=oc):
                                    if oc + 2 < HC:
                                        wait(g, evF[(mega, oc)])
                                        s2 = (oc + 2) % 2
                                        evWd[(mega, oc + 2)] = ssem(oc).inc(g.dma_start(
                                            out=wd_db[:, s2 * CW:(s2 + 1) * CW],
                                            in_=wd[(oc + 2) * 128:(oc + 3) * 128,
                                                   mega * ICA * 128: mega * ICA * 128 + CW]), 16)

                                if not g_write:
                                    @VV
                                    def _(v, oc=oc, sl=sl):
                                        if first[1]:
                                            wait(v, *BB)
                                            first[1] = False
                                        wait(v, evF[(0, oc)], evFSt.get(oc - 2))
                                        evFEvac[oc] = sDVE.inc(v.tensor_copy(
                                            f_st[:, sl * T:(sl + 1) * T], pf[oc % 4][:, :]), 1)

                                    @SY
                                    def _(q, oc=oc, sl=sl):
                                        if first[3]:
                                            wait(q, *BB)
                                            first[3] = False
                                        wait(q, evFEvac[oc])
                                        evFSt[oc] = sSY.inc(q.dma_start(
                                            out=f_sc[oc * 128:(oc + 1) * 128, :],
                                            in_=f_st[:, sl * T:(sl + 1) * T]), 16)
                                else:
                                    @VV
                                    def _(v, oc=oc, sl=sl):
                                        if first[1]:
                                            wait(v, *BB)
                                            first[1] = False
                                        wait(v, evF[(1, oc)], evFRead[oc], evG.get(oc - 2))
                                        evAdd[oc] = sDVE.inc(v.tensor_add(
                                            tmpf[:, sl * T:(sl + 1) * T],
                                            pf[oc % 4][:, :], frd[:, sl * T:(sl + 1) * T]), 1)

                                    @AA
                                    def _(a, oc=oc, sl=sl):
                                        if first[2]:
                                            wait(a, *BB)
                                            first[2] = False
                                        wait(a, evAdd[oc], evGSt.get(oc - 2))
                                        evG[oc] = sACT.inc(a.activation(
                                            out=g_st[:, sl * T:(sl + 1) * T],
                                            in_=tmpf[:, sl * T:(sl + 1) * T], func=AF.Silu), 1)

                                    @SY
                                    def _(q, oc=oc, sl=sl):
                                        if first[3]:
                                            wait(q, *BB)
                                            first[3] = False
                                        wait(q, evG[oc])
                                        evGSt[oc] = sSY.inc(q.dma_start(
                                            out=g_sc[oc * 128:(oc + 1) * 128, :],
                                            in_=g_st[:, sl * T:(sl + 1) * T]), 16)
                                        if oc + 2 < HC:
                                            wait(q, evAdd[oc])
                                            evFRead[oc + 2] = ssem2(oc).inc(q.dma_start(
                                                out=frd[:, sl * T:(sl + 1) * T],
                                                in_=f_sc[(oc + 2) * 128:(oc + 3) * 128, :]), 16)

                    gateup(0)
                    if dbg:
                        @SY
                        def _(q):
                            wait(q, evGuEvac[ICA - 1])
                            sSY.inc(q.dma_start(out=gu_dump, in_=gu_t[:, :]), 16)
                    down_phase(0, g_write=False)
                    gateup(1)
                    down_phase(1, g_write=True)

                evQmm = {}
                evWqH = {}
                evWoH = {}
                evTsQ = {}
                evActQ = {}
                evQbf = {}
                evWq = {}
                evKvg = [None]
                evRec = [None]
                evAmm = {}
                evAsc = {}
                evWo = {}
                evSmm = {}
                evP = {}
                evPSt = {}
                evGload = {}

                p_tm4 = nc.alloc_sbuf_tensor("p_tm4", [128, 5 * H], BF16, side="right")
                evPTw1 = [None]

                with nc.sbuf_tensor("q_bf", [128, NH * T], BF16) as q_bf_t:
                    q_bf = q_bf_t[:, :]
                    BQ = barrier()
                    with ExitStack() as ctq:
                        wq_db = ctq.enter_context(nc.sbuf_tensor("wq_db", [128, 2 * H], BF16))
                        qtmin = ctq.enter_context(nc.sbuf_tensor("qtmin", [128, 1024], F32))
                        qe = ctq.enter_context(nc.sbuf_tensor("qe", [128, 1024], F32))
                        qr = ctq.enter_context(nc.sbuf_tensor("qr", [128, 1024], F32))
                        pq = [ctq.enter_context(nc.psum_tensor("pq%d" % i, [128, 512], F32))
                              for i in range(4)]

                        @GP
                        def _(g):
                            wait(g, *BQ)
                            evWqH[0] = ssem(0).inc(g.dma_start(
                                out=wq_db[:, 0:H // 2],
                                in_=wq[0:128, 0:H // 2]), 16)
                            evWq[0] = sW.inc(g.dma_start(
                                out=wq_db[:, H // 2:H],
                                in_=wq[0:128, H // 2:H]), 16)
                            evWq[1] = ssem(1).inc(g.dma_start(
                                out=wq_db[:, H:2 * H],
                                in_=wq[128:256, :]), 16)

                        first = [True, True, True]
                        for oc in range(NH):
                            sl = oc % 2

                            @TT
                            def _(t, oc=oc, sl=sl):
                                if first[0]:
                                    twait(t, *BQ)
                                    first[0] = False
                                twait(t, evWqH[0] if oc == 0 else evWq[oc])
                                for b in (0, 1):
                                    twait(t, evActQ.get((oc - 2, b)))
                                lastmm = {}
                                for hc in range(HC):
                                    if oc == 0 and hc == HC // 2:
                                        twait(t, evWq[0])
                                    for half in (0, 1):
                                        bank = (oc % 2) * 2 + half
                                        lastmm[half] = t.matmul(
                                            pq[bank][:, :],
                                            wq_db[:, sl * H + hc * 128: sl * H + (hc + 1) * 128],
                                            xn_fm[:, hc * T + half * 512: hc * T + half * 512 + 512],
                                            start=(hc == 0), stop=(hc == HC - 1))
                                evQmm[(oc, 0)] = sPE.inc(lastmm[0], 1)
                                evQmm[(oc, 1)] = sPE.inc(lastmm[1], 1)

                            @GP
                            def _(g, oc=oc):
                                if oc + 2 < NH:
                                    wait(g, evQmm[(oc, 1)])
                                    s2 = (oc + 2) % 2
                                    evWq[oc + 2] = ssem(oc).inc(g.dma_start(
                                        out=wq_db[:, s2 * H:(s2 + 1) * H],
                                        in_=wq[(oc + 2) * 128:(oc + 3) * 128, :]), 16)

                            @VV
                            def _(v, oc=oc):
                                if first[1]:
                                    wait(v, *BQ)
                                    first[1] = False
                                for half in (0, 1):
                                    bank = (oc % 2) * 2 + half
                                    wait(v, evQmm[(oc, half)], evActQ.get((oc - 1, half)))
                                    evTsQ[(oc, half)] = sDVE.inc(v.tensor_scalar(
                                        qtmin[:, half * 512:(half + 1) * 512], pq[bank][:, :],
                                        bq_sb[:, oc:oc + 1], 0.0, op0=ALU.add, op1=ALU.min), 1)

                            @AA
                            def _(a, oc=oc):
                                if first[2]:
                                    wait(a, *BQ)
                                    first[2] = False
                                for half in (0, 1):
                                    bank = (oc % 2) * 2 + half
                                    wait(a, evTsQ[(oc, half)], evQbf.get((oc - 1, half)))
                                    a.activation(out=qe[:, half * 512:(half + 1) * 512],
                                                 in_=qtmin[:, half * 512:(half + 1) * 512],
                                                 func=AF.Exp)
                                    evActQ[(oc, half)] = sACT.inc(a.activation(
                                        out=qr[:, half * 512:(half + 1) * 512],
                                        in_=pq[bank][:, :], func=AF.Relu,
                                        bias=bq_sb[:, oc:oc + 1]), 1)

                            @VV
                            def _(v, oc=oc):
                                for half in (0, 1):
                                    wait(v, evActQ[(oc, half)])
                                    evQbf[(oc, half)] = sDVE.inc(v.tensor_add(
                                        q_bf[:, oc * T + half * 512: oc * T + half * 512 + 512],
                                        qr[:, half * 512:(half + 1) * 512],
                                        qe[:, half * 512:(half + 1) * 512]), 1)

                    @GP
                    def _(g):
                        g.wait_ge(sCC.h, 1)
                        evKvg[0] = sSC.inc(g.dma_start(out=kvg_sb[:, :], in_=cc_out), 16)

                    @VV
                    def _(v):
                        wait(v, evKvg[0])
                        v.tensor_copy(kv_bf[:, :], kvg_sb[:, 0:KD])
                        vch(v, v.tensor_single_scalar(kvg_sb[:, KD:KD + NKV],
                                                      kvg_sb[:, KD:KD + NKV],
                                                      EPS_ATT, op=ALU.add))
                        evRec[0] = vch(v, v.reciprocal(rec_sb[:, :], kvg_sb[:, KD:KD + NKV]))

                    if dbg:
                        @SY
                        def _(q):
                            wait(q, evRec[0])
                            sSY.inc(q.dma_start(out=kvg_dump, in_=kvg_sb[:, :]), 16)
                            wait(q, evKvpack[0])
                            sSY.inc(q.dma_start(out=cc_dump, in_=kvpack[:, :]), 16)

                    BA = barrier()
                    with ExitStack() as cta:
                        pa = [cta.enter_context(nc.psum_tensor("pa%d" % i, [128, 512], F32))
                              for i in range(4)]
                        first = [True, True]
                        for h in range(NH):
                            hk = h // 4

                            @TT
                            def _(t, h=h, hk=hk):
                                if first[0]:
                                    twait(t, *BA)
                                    first[0] = False
                                twait(t, evRec[0], evQbf.get((NH - 1, 1)))
                                for half in (0, 1):
                                    bank = (h % 2) * 2 + half
                                    twait(t, evAsc.get((h - 2, half)))
                                    evAmm[(h, half)] = sPE.inc(t.matmul(
                                        pa[bank][:, :],
                                        kv_bf[:, hk * 128:(hk + 1) * 128],
                                        q_bf[:, h * T + half * 512: h * T + half * 512 + 512],
                                        start=True, stop=True), 1)

                            @VV
                            def _(v, h=h, hk=hk):
                                if first[1]:
                                    wait(v, *BA)
                                    first[1] = False
                                for half in (0, 1):
                                    bank = (h % 2) * 2 + half
                                    wait(v, evAmm[(h, half)])
                                    evAsc[(h, half)] = sDVE.inc(v.tensor_scalar(
                                        q_bf[:, h * T + half * 512: h * T + half * 512 + 512],
                                        pa[bank][:, :], rec_sb[:, hk:hk + 1], None,
                                        op0=ALU.mult), 1)

                    if dbg:
                        @SY
                        def _(q):
                            wait(q, evAsc[(NH - 1, 1)])
                            sSY.inc(q.dma_start(out=q_dump, in_=q_bf), 16)

                    BO = barrier()
                    with ExitStack() as cto:
                        wo_db = cto.enter_context(nc.sbuf_tensor("wo_db", [128, 2 * H], BF16))
                        g_buf = cto.enter_context(nc.sbuf_tensor("g_buf", [128, 2 * T], BF16))
                        p_st = cto.enter_context(nc.sbuf_tensor("p_st", [128, 2 * T], BF16))
                        ps = [cto.enter_context(nc.psum_tensor("ps%d" % i, [128, 512], F32))
                              for i in range(4)]

                        @GP
                        def _(g):
                            wait(g, *BO)
                            evWoH[0] = ssem(0).inc(g.dma_start(
                                out=wo_db[:, 0:H // 2],
                                in_=wo[0:128, 0:H // 2]), 16)
                            evWo[0] = sW.inc(g.dma_start(
                                out=wo_db[:, H // 2:H],
                                in_=wo[0:128, H // 2:H]), 16)
                            evWo[1] = ssem(1).inc(g.dma_start(
                                out=wo_db[:, H:2 * H],
                                in_=wo[128:256, :]), 16)

                        @SY
                        def _(q):
                            wait(q, *BO)
                            for oc in (0, 1):
                                wait(q, evGSt[oc])
                                evGload[oc] = ssem2(oc).inc(q.dma_start(
                                    out=g_buf[:, (oc % 2) * T:(oc % 2 + 1) * T],
                                    in_=g_sc[oc * 128:(oc + 1) * 128, :]), 16)

                        first = [True, True, True]
                        for oc in range(HC):
                            sl = oc % 2

                            @TT
                            def _(t, oc=oc, sl=sl):
                                if first[0]:
                                    twait(t, *BO)
                                    first[0] = False
                                twait(t, evWoH[0] if oc == 0 else evWo[oc],
                                      evAsc.get((NH - 1, 1)))
                                for b in (0, 1):
                                    twait(t, evP.get((oc - 2, b)))
                                lastmm = {}
                                for vc in range(HC):
                                    if oc == 0 and vc == HC // 2:
                                        twait(t, evWo[0])
                                    for half in (0, 1):
                                        bank = (oc % 2) * 2 + half
                                        lastmm[half] = t.matmul(
                                            ps[bank][:, :],
                                            wo_db[:, sl * H + vc * 128: sl * H + (vc + 1) * 128],
                                            q_bf[:, vc * T + half * 512: vc * T + half * 512 + 512],
                                            start=(vc == 0), stop=(vc == HC - 1))
                                evSmm[(oc, 0)] = sPE.inc(lastmm[0], 1)
                                evSmm[(oc, 1)] = sPE.inc(lastmm[1], 1)

                            @GP
                            def _(g, oc=oc):
                                if oc + 2 < HC:
                                    wait(g, evSmm[(oc, 1)])
                                    s2 = (oc + 2) % 2
                                    evWo[oc + 2] = ssem(oc).inc(g.dma_start(
                                        out=wo_db[:, s2 * H:(s2 + 1) * H],
                                        in_=wo[(oc + 2) * 128:(oc + 3) * 128, :]), 16)

                            @VV
                            def _(v, oc=oc, sl=sl):
                                if first[1]:
                                    wait(v, *BO)
                                    first[1] = False
                                for half in (0, 1):
                                    bank = (oc % 2) * 2 + half
                                    wait(v, evSmm[(oc, half)], evGload[oc], evPSt.get(oc - 2))
                                    evP[(oc, half)] = sDVE.inc(v.scalar_tensor_tensor(
                                        p_st[:, sl * T + half * 512: sl * T + half * 512 + 512],
                                        ps[bank][:, :], bo_sb[:, oc:oc + 1],
                                        g_buf[:, sl * T + half * 512: sl * T + half * 512 + 512],
                                        op0=ALU.add, op1=ALU.mult), 1)

                            @SY
                            def _(q, oc=oc, sl=sl):
                                if first[2]:
                                    wait(q, *BO)
                                    first[2] = False
                                wait(q, evP[(oc, 1)])
                                evPSt[oc] = sSY.inc(q.dma_start(
                                    out=p_sc[oc * 128:(oc + 1) * 128, :],
                                    in_=p_st[:, sl * T:(sl + 1) * T]), 16)
                                if oc + 2 < HC:
                                    wait(q, evGSt.get(oc + 2))
                                    s2 = (oc + 2) % 2
                                    evGload[oc + 2] = ssem2(oc).inc(q.dma_start(
                                        out=g_buf[:, s2 * T:(s2 + 1) * T],
                                        in_=g_sc[(oc + 2) * 128:(oc + 3) * 128, :]), 16)
                                if oc % 4 == 3:
                                    wait(q, evPSt[oc])
                                    lo = (oc - 3) * 128
                                    hi = (oc + 1) * 128
                                    for tc4 in range(5):
                                        sSY.inc(q.dma_start(
                                            out=p_tm4[:, tc4 * H + lo: tc4 * H + hi],
                                            in_=p_sc[lo:hi, tc4 * 128:(tc4 + 1) * 128],
                                            transpose=True), 16)
                                    if oc == HC - 1:
                                        evPTw1[0] = (sSY, sSY.n)

            BE = barrier()
            evPTw1v = evPTw1[0]
            with ExitStack() as cte:
                x2 = cte.enter_context(nc.sbuf_tensor("x2", [128, 2 * H], F32))
                outb = cte.enter_context(nc.sbuf_tensor("outb", [128, 2 * H], F32))
                evPT = {}
                evPT2 = {}
                evX2 = {}
                evEpi = {}
                evYSt = {}

                @SY
                def _(q):
                    wait(q, *BE)

                @VV
                def _(v):
                    wait(v, *BE)

                @GP
                def _(g):
                    wait(g, *BE)
                    for tc in (0, 1):
                        evX2[tc] = ssem(tc).inc(g.dma_start(
                            out=x2[:, (tc % 2) * H:((tc % 2) + 1) * H],
                            in_=x_in[tc * 128:(tc + 1) * 128, :]), 16)

                for tc in range(NTC):
                    sl = tc % 2
                    slot = tc % 5
                    if tc >= 5:
                        @SY
                        def _(q, tc=tc, slot=slot):
                            wait(q, evEpi[tc - 5])
                            wsem = [sCp, sDp, sSC][slot]
                            evPT[tc] = wsem.inc(q.dma_start(
                                out=p_tm4[:, slot * H:(slot + 1) * H],
                                in_=p_sc[:, tc * 128:(tc + 1) * 128],
                                transpose=True), 16)

                    @VV
                    def _(v, tc=tc, sl=sl, slot=slot):
                        wait(v, evX2[tc], evPT.get(tc, evPTw1v), evYSt.get(tc - 2))
                        evEpi[tc] = sDVE.inc(v.scalar_tensor_tensor(
                            outb[:, sl * H:(sl + 1) * H],
                            p_tm4[:, slot * H:(slot + 1) * H], 1.0,
                            x2[:, sl * H:(sl + 1) * H],
                            op0=ALU.mult, op1=ALU.add), 1)

                    @SY
                    def _(q, tc=tc, sl=sl):
                        wait(q, evEpi[tc])
                        evYSt[tc] = (sYa if tc % 2 == 0 else sYb).inc(q.dma_start(
                            out=y_out[tc * 128:(tc + 1) * 128, :],
                            in_=outb[:, sl * H:(sl + 1) * H]), 16)

                    @GP
                    def _(g, tc=tc):
                        if tc + 2 < NTC:
                            wait(g, evEpi[tc])
                            s2 = (tc + 2) % 2
                            evX2[tc + 2] = ssem(tc).inc(g.dma_start(
                                out=x2[:, s2 * H:(s2 + 1) * H],
                                in_=x_in[(tc + 2) * 128:(tc + 3) * 128, :]), 16)

                @SY
                def _(q):
                    wait(q, evYSt[NTC - 2], evYSt[NTC - 1])

    return nc



def _wtile(w):
    O, Hin = w.shape
    wt = np.ascontiguousarray(w.T)
    wt = wt.reshape(Hin // 128, 128, O // 128, 128)
    wt = wt.transpose(2, 1, 0, 3)
    return np.ascontiguousarray(wt).reshape(O, Hin)


def _btile(b):
    return np.ascontiguousarray(np.asarray(b, np.float32).reshape(-1, 128).T)


def kernel(L0_attn_hidden_states, L0_attn_Wq, L0_attn_bq, L0_attn_Wk, L0_attn_bk,
           L0_attn_Wv, L0_attn_bv, L0_attn_Wo, L0_attn_bo,
           L0_ffn_w_gate, L0_ffn_w_up, L0_ffn_w_down):
    x = np.ascontiguousarray(np.asarray(L0_attn_hidden_states, np.float32).reshape(8192, 4096))
    w_host = {
        "wq": _wtile(np.asarray(L0_attn_Wq, np.float32)),
        "wk": _wtile(np.asarray(L0_attn_Wk, np.float32)),
        "wv": _wtile(np.asarray(L0_attn_Wv, np.float32)),
        "wo": _wtile(np.asarray(L0_attn_Wo, np.float32)),
        "wg": _wtile(np.asarray(L0_ffn_w_gate, np.float32)),
        "wu": _wtile(np.asarray(L0_ffn_w_up, np.float32)),
        "wd": _wtile(np.asarray(L0_ffn_w_down, np.float32)),
        "bq": _btile(L0_attn_bq),
        "bk": _btile(L0_attn_bk),
        "bv": _btile(L0_attn_bv),
        "bo": _btile(L0_attn_bo),
        "ident_in": np.eye(128, dtype=np.float32),
    }
    nc = build_nc(dbg=DBG)
    in_maps = []
    for c in range(NCORES):
        m = {"x": np.ascontiguousarray(x[c * T:(c + 1) * T])}
        m.update(w_host)
        in_maps.append(m)
    res = run_bass_kernel_spmd(nc, in_maps, core_ids=list(range(NCORES)))
    out = np.concatenate([r["y"] for r in res.results], axis=0)
    if DBG:
        kernel.last_results = res.results
    return out.reshape(1, 8192, 4096)


if __name__ == "__main__":
    import time
    t0 = time.time()
    build_nc()
    print("build ok in %.1fs" % (time.time() - t0))


# version 27
# speedup vs baseline: 1.0077x; 1.0077x over previous
import numpy as np
import concourse.bass as bass
import concourse.mybir as mybir
import concourse.bass_utils as _bu
from concourse.bass_utils import run_bass_kernel_spmd
from contextlib import ExitStack

LDWOPT = True

if LDWOPT and not getattr(_bu, "_ldwopt_patched", False):
    _orig_run_command = _bu.run_command

    def _run_command_ldwopt(argv, **kw):
        argv = ["--enable-ldw-opt=true" if a == "--enable-ldw-opt=false" else a
                for a in argv]
        return _orig_run_command(argv, **kw)

    _bu.run_command = _run_command_ldwopt
    _bu._ldwopt_patched = True

F32 = mybir.dt.float32
BF16 = mybir.dt.bfloat16
AF = mybir.ActivationFunctionType
ALU = mybir.AluOpType
AX = mybir.AxisListType

NCORES = 8
T = 1024
H = 4096
HC = H // 128
NH, NKV, HD = 32, 8, 128
KD = NKV * HD
II = 11008
ICN = II // 128
ICA = 43
NTC = T // 128
EPS_LN = 1e-5
EPS_ATT = 1e-10

DBG = False


class Sem:
    __slots__ = ("h", "n")

    def __init__(self, h):
        self.h = h
        self.n = 0

    def inc(self, instr, k=1):
        instr.then_inc(self.h, k)
        self.n += k
        return (self, self.n)


def wait(eng, *evts):
    best = {}
    for e in evts:
        if e is None:
            continue
        s, v = e
        if v > 0 and v > best.get(id(s), (s, 0))[1]:
            best[id(s)] = (s, v)
    for s, v in best.values():
        eng.wait_ge(s.h, v)


def build_nc(dbg=False):
    nc = bass.Bass("TRN2", target_bir_lowering=False)

    x_in = nc.dram_tensor("x", [T, H], F32, kind="ExternalInput").ap()
    wq = nc.dram_tensor("wq", [H, H], F32, kind="ExternalInput").ap()
    wk = nc.dram_tensor("wk", [KD, H], F32, kind="ExternalInput").ap()
    wv = nc.dram_tensor("wv", [KD, H], F32, kind="ExternalInput").ap()
    wo = nc.dram_tensor("wo", [H, H], F32, kind="ExternalInput").ap()
    wg = nc.dram_tensor("wg", [II, H], F32, kind="ExternalInput").ap()
    wu = nc.dram_tensor("wu", [II, H], F32, kind="ExternalInput").ap()
    wd = nc.dram_tensor("wd", [H, II], F32, kind="ExternalInput").ap()
    bq = nc.dram_tensor("bq", [128, HC], F32, kind="ExternalInput").ap()
    bk = nc.dram_tensor("bk", [128, NKV], F32, kind="ExternalInput").ap()
    bv = nc.dram_tensor("bv", [128, NKV], F32, kind="ExternalInput").ap()
    bo = nc.dram_tensor("bo", [128, HC], F32, kind="ExternalInput").ap()
    ident_in = nc.dram_tensor("ident_in", [128, 128], F32, kind="ExternalInput").ap()
    y_out = nc.dram_tensor("y", [T, H], F32, kind="ExternalOutput").ap()

    sck = "ExternalOutput" if dbg else "Internal"
    f_sc = nc.dram_tensor("f_sc", [H, T], F32, kind=sck).ap()
    g_sc = nc.dram_tensor("g_sc", [H, T], BF16, kind=sck).ap()
    p_sc = nc.dram_tensor("p_sc", [H, T], BF16, kind=sck).ap()
    xn_sc = nc.dram_tensor("xn_sc", [T, H], BF16, kind="Internal").ap()
    cc_in = nc.dram_tensor("cc_in", [128, KD + NKV], F32, kind="Internal").ap()
    cc_out = nc.dram_tensor("cc_out", [128, KD + NKV], F32, kind="Internal").ap()
    if dbg:
        cc_dump = nc.dram_tensor("cc_dump", [128, KD + NKV], F32, kind="ExternalOutput").ap()
        gu_dump = nc.dram_tensor("gu_dump", [128, ICA * T], BF16, kind="ExternalOutput").ap()
        xn_dump = nc.dram_tensor("xn_dump", [128, HC * T], BF16, kind="ExternalOutput").ap()
        q_dump = nc.dram_tensor("q_dump", [128, NH * T], BF16, kind="ExternalOutput").ap()
        ktm_dump = nc.dram_tensor("ktm_dump", [128, NTC * KD], BF16, kind="ExternalOutput").ap()
        vtm_dump = nc.dram_tensor("vtm_dump", [128, NTC * KD], BF16, kind="ExternalOutput").ap()
        kvg_dump = nc.dram_tensor("kvg_dump", [128, KD + NKV], F32, kind="ExternalOutput").ap()

    with ExitStack() as stack:
        sb_outer = lambda name, shape, dt: stack.enter_context(nc.sbuf_tensor(name, shape, dt))
        ident = sb_outer("ident", [128, 128], BF16)
        bq_sb = sb_outer("bq_sb", [128, HC], F32)
        bk_sb = sb_outer("bk_sb", [128, NKV], F32)
        bv_sb = sb_outer("bv_sb", [128, NKV], F32)
        bo_sb = sb_outer("bo_sb", [128, HC], F32)
        kvg_sb = sb_outer("kvg_sb", [128, KD + NKV], F32)
        kv_bf = sb_outer("kv_bf", [128, KD], BF16)
        rec_sb = sb_outer("rec_sb", [128, NKV], F32)
        kvpack = sb_outer("kvpack", [128, KD + NKV], F32)
        sb_outer("cachebust_ldwopt%d" % int(LDWOPT), [128, 1], F32)

        sW = Sem(stack.enter_context(nc.semaphore("sW")))
        sSY = Sem(stack.enter_context(nc.semaphore("sSY")))
        sPE = Sem(stack.enter_context(nc.semaphore("sPE")))
        sDVE = Sem(stack.enter_context(nc.semaphore("sDVE")))
        sACT = Sem(stack.enter_context(nc.semaphore("sACT")))
        sPL = Sem(stack.enter_context(nc.semaphore("sPL")))
        sCC = Sem(stack.enter_context(nc.semaphore("sCC")))
        sSC = Sem(stack.enter_context(nc.semaphore("sSC")))
        sAp = Sem(stack.enter_context(nc.semaphore("sAp")))
        sBp = Sem(stack.enter_context(nc.semaphore("sBp")))
        sCp = Sem(stack.enter_context(nc.semaphore("sCp")))

        sDp = Sem(stack.enter_context(nc.semaphore("sDp")))
        sYa = Sem(stack.enter_context(nc.semaphore("sYa")))
        sYb = Sem(stack.enter_context(nc.semaphore("sYb")))
        sXNa = Sem(stack.enter_context(nc.semaphore("sXNa")))
        sXNb = Sem(stack.enter_context(nc.semaphore("sXNb")))
        sXa = Sem(stack.enter_context(nc.semaphore("sXa")))
        sXb = Sem(stack.enter_context(nc.semaphore("sXb")))

        def ssem(i):
            return sAp if i % 2 == 0 else sBp

        def ssem2(i):
            return sCp if i % 2 == 0 else sDp
        ALLSEMS = [sW, sSY, sPE, sDVE, sACT, sPL, sSC, sAp, sBp, sCp, sDp, sYa, sYb,
                   sXNa, sXNb, sXa, sXb]

        def barrier():
            return [(s, s.n) for s in ALLSEMS]

        def vch(v, ins):
            e = sDVE.inc(ins, 1)
            v.wait_ge(sDVE.h, e[1])
            return e

        def ach(a, ins):
            e = sACT.inc(ins, 1)
            a.wait_ge(sACT.h, e[1])
            return e

        def twait(t, *evts):
            wait(t, *evts)
            t.nop(nofuse=True)

        with nc.Block() as block:
            TT, VV, AA, SY, GP = block.tensor, block.vector, block.scalar, block.sync, block.gpsimd

            evIdent = [None]

            @GP
            def _(g):
                evIdent[0] = sPL.inc(g.dma_start(out=ident[:, :], in_=ident_in), 16)
                sW.inc(g.dma_start(out=bq_sb[:, :], in_=bq), 16)
                sW.inc(g.dma_start(out=bk_sb[:, :], in_=bk), 16)
                sW.inc(g.dma_start(out=bv_sb[:, :], in_=bv), 16)
                sW.inc(g.dma_start(out=bo_sb[:, :], in_=bo), 16)

            with nc.sbuf_tensor("xn_fm", [128, HC * T], BF16) as xn_fm_t:
                xn_fm = xn_fm_t[:, :]
                evXnDone = [None]
                evXnH0 = [None]
                evXnH1 = [None]
                evXnFree = {}
                slot_of = {}
                evPtW = [None] * 4
                evPtR = [None] * 4
                tr = [0]

                with ExitStack() as st_tr:
                    pt = [st_tr.enter_context(nc.psum_tensor("pt%d" % i, [128, 1024], BF16))
                          for i in range(4)]
                    xn_st = st_tr.enter_context(nc.sbuf_tensor("xn_st", [128, 4 * H], BF16))
                    wk_db = st_tr.enter_context(nc.sbuf_tensor("wk_db", [128, 2 * H], BF16))
                    wv_db = st_tr.enter_context(nc.sbuf_tensor("wv_db", [128, 2 * H], BF16))

                    with ExitStack() as st0:
                        st0_keep = []
                        x_db = st0.enter_context(nc.sbuf_tensor("x_db", [128, 4 * H], F32))
                        bnst = st0.enter_context(nc.sbuf_tensor("bnst", [128, 48], F32))
                        p0t = st0.enter_context(nc.sbuf_tensor("p0t", [128, 8], F32))
                        rstd_all = st0.enter_context(nc.sbuf_tensor("rstd_all", [128, NTC], F32))
                        evX = {}
                        evFreeX = {}
                        evXsq = {}
                        evVar = {}
                        evRstd = {}
                        evXnStage = {}

                        xn_fm3 = xn_fm_t[:, :].rearrange("p (hcx tt) -> p hcx tt", tt=T)

                        def p0_transposes(tc):
                            sl = tc % 4
                            for gq in range(8):
                                @TT
                                def _(t, tc=tc, sl=sl, gq=gq):
                                    j = tr[0] % 4
                                    twait(t, evIdent[0], evXnStage[tc], evPtR[j])
                                    ins = None
                                    for k in range(4):
                                        hc = gq * 4 + k
                                        ins = t.transpose(
                                            pt[j][:, k * 128:(k + 1) * 128],
                                            xn_st[:, sl * H + hc * 128: sl * H + (hc + 1) * 128],
                                            ident[:, :])
                                    evPtW[j] = sPE.inc(ins, 1)
                                    slot_of[("p0g", tc, gq)] = j
                                    tr[0] += 1

                                @VV
                                def _(v, tc=tc, gq=gq):
                                    j = slot_of[("p0g", tc, gq)]
                                    wait(v, evPtW[j])
                                    evPtR[j] = sDVE.inc(v.tensor_copy(
                                        xn_fm3[:, gq * 4:(gq + 1) * 4, tc * 128:(tc + 1) * 128],
                                        pt[j].rearrange("p (k f) -> p k f", f=128)[:, 0:4, :]), 1)

                        @SY
                        def _(q):
                            evX[0] = sXa.inc(q.dma_start(
                                out=x_db[:, 0:H], in_=x_in[0:128, :]), 16)
                            evX[3] = sYb.inc(q.dma_start(
                                out=x_db[:, 3 * H:4 * H], in_=x_in[384:512, :]), 16)

                        @AA
                        def _(a):
                            evX[1] = sXNa.inc(a.dma_start(
                                out=x_db[:, H:2 * H], in_=x_in[128:256, :]), 16)
                            evX[2] = sYa.inc(a.dma_start(
                                out=x_db[:, 2 * H:3 * H], in_=x_in[256:384, :]), 16)

                        for tc in range(NTC):
                            sl = tc % 4

                            @VV
                            def _(v, tc=tc, sl=sl):
                                wait(v, evX[tc])
                                for gq in range(8):
                                    ins = v.bn_stats(bnst[:, gq * 6:(gq + 1) * 6],
                                                     x_db[:, sl * H + gq * 512: sl * H + (gq + 1) * 512])
                                vch(v, ins)
                                vch(v, v.bn_aggr(p0t[:, 0:2], bnst[:, :]))
                                evVar[tc] = vch(v, v.tensor_single_scalar(
                                    p0t[:, 2:3], p0t[:, 1:2], EPS_LN, op=ALU.add))

                            @AA
                            def _(a, tc=tc):
                                wait(a, evVar[tc])
                                ach(a, a.activation(out=p0t[:, 3:4], in_=p0t[:, 2:3], func=AF.Ln))
                                evRstd[tc] = sACT.inc(a.activation(
                                    out=rstd_all[:, tc:tc + 1], in_=p0t[:, 3:4],
                                    func=AF.Exp, scale=-0.5), 1)

                            @VV
                            def _(v, tc=tc, sl=sl):
                                wait(v, evRstd[tc], evXnFree.get(tc - 4))
                                e = sDVE.inc(v.tensor_scalar(
                                    xn_st[:, sl * H:(sl + 1) * H], x_db[:, sl * H:(sl + 1) * H],
                                    p0t[:, 0:1], rstd_all[:, tc:tc + 1],
                                    op0=ALU.subtract, op1=ALU.mult), 1)
                                evXnStage[tc] = e
                                evFreeX[tc] = e

                            @SY
                            def _(q, tc=tc, sl=sl):
                                if tc + 4 < NTC:
                                    wait(q, evFreeX[tc])
                                    evX[tc + 4] = (sXa if tc % 2 else sXb).inc(q.dma_start(
                                        out=x_db[:, sl * H:(sl + 1) * H],
                                        in_=x_in[(tc + 4) * 128:(tc + 5) * 128, :]), 16)

                            if tc < 4:
                                p0_transposes(tc)
                                evXnFree[tc] = (sPE, sPE.n)
                            if tc == 3:
                                evXnH0[0] = (sDVE, sDVE.n)



                    evKvpack = [None]
                    evP0stage = evXnStage[NTC - 1]
                    with ExitStack() as st1:
                        k_tm = st1.enter_context(nc.sbuf_tensor("k_tm", [128, NTC * KD], BF16))
                        v_tm = st1.enter_context(nc.sbuf_tensor("v_tm", [128, NTC * KD], BF16))
                        ksum = st1.enter_context(nc.sbuf_tensor("ksum", [128, NKV], F32))
                        with ExitStack() as st1b:
                            kst = st1b.enter_context(nc.sbuf_tensor("kst", [128, 1024], F32))
                            est = st1b.enter_context(nc.sbuf_tensor("est", [128, 1024], F32))
                            rst = st1b.enter_context(nc.sbuf_tensor("rst", [128, 1024], F32))
                            kphiK = st1b.enter_context(nc.sbuf_tensor("kphiK", [128, 1024], BF16))
                            kphiV = st1b.enter_context(nc.sbuf_tensor("kphiV", [128, 1024], BF16))
                            ksum_h = st1b.enter_context(nc.sbuf_tensor("ksum_h", [128, 2 * NKV], F32))
                            pkK = [st1b.enter_context(nc.psum_tensor("pkK%d" % i, [128, 512], F32))
                                   for i in range(2)]
                            pkV = [st1b.enter_context(nc.psum_tensor("pkV%d" % i, [128, 512], F32))
                                   for i in range(2)]
                            evKmm = {}
                            evTsK = {}
                            evActK = {}
                            evPhi = {}
                            wbuf = [wk_db, wv_db]
                            wdr = [wk, wv]

                            k_tm3 = k_tm[:, :].rearrange("p (lc f) -> p lc f", f=KD)
                            v_tm3 = v_tm[:, :].rearrange("p (lc f) -> p lc f", f=KD)

                            def p1_transposes(oc, half):
                                for kvi, src, dst3 in ((0, kphiK, k_tm3), (1, kphiV, v_tm3)):
                                    @TT
                                    def _(t, oc=oc, kvi=kvi, src=src, half=half):
                                        j = tr[0] % 4
                                        twait(t, evPhi[(kvi, oc, half)], evPtR[j])
                                        ins = None
                                        for j4 in range(4):
                                            ins = t.transpose(
                                                pt[j][:, j4 * 128:(j4 + 1) * 128],
                                                src[:, half * 512 + j4 * 128: half * 512 + (j4 + 1) * 128],
                                                ident[:, :])
                                        evPtW[j] = sPE.inc(ins, 1)
                                        slot_of[("p1g", kvi, oc, half)] = j
                                        tr[0] += 1

                                    @VV
                                    def _(v, oc=oc, kvi=kvi, half=half, dst3=dst3):
                                        j = slot_of[("p1g", kvi, oc, half)]
                                        wait(v, evPtW[j])
                                        evPtR[j] = sDVE.inc(v.tensor_copy(
                                            dst3[:, half * 4:(half + 1) * 4, oc * 128:(oc + 1) * 128],
                                            pt[j].rearrange("p (k f) -> p k f", f=128)[:, 0:4, :]), 1)

                            first_pe = [True]
                            lastK = {}
                            lastV = {}
                            evWkvA = {}

                            @GP
                            def _(g):
                                for oc in (0, 1):
                                    for kvi in (0, 1):
                                        evWkvA[(kvi, 0, oc)] = (ssem(oc) if kvi == 0 else ssem2(oc)).inc(
                                            g.dma_start(
                                                out=wbuf[kvi][:, (oc % 2) * H:(oc % 2 + 1) * H],
                                                in_=wdr[kvi][oc * 128:(oc + 1) * 128, :]), 16)

                            for half in (0, 1):
                                for oc in range(NKV):
                                    gidx = half * NKV + oc
                                    sl = gidx % 2

                                    if oc > 0:
                                        p1_transposes(oc - 1, half)
                                    elif half == 1:
                                        p1_transposes(NKV - 1, 0)
                                    if half == 0 and 2 <= oc <= 5:
                                        p0_transposes(oc + 2)
                                        if oc == 5:
                                            evXnH1[0] = (sDVE, sDVE.n)

                                    @TT
                                    def _(t, oc=oc, sl=sl, half=half):
                                        bank = oc % 2
                                        twait(t, evWkvA[(0, half, oc)],
                                              evXnH0[0] if half == 0 else evXnH1[0],
                                              lastK.get(bank))
                                        last = None
                                        for hc in range(HC):
                                            last = t.matmul(
                                                pkK[bank][:, :],
                                                wk_db[:, sl * H + hc * 128: sl * H + (hc + 1) * 128],
                                                xn_fm[:, hc * T + half * 512: hc * T + half * 512 + 512],
                                                start=(hc == 0), stop=(hc == HC - 1))
                                        evKmm[(0, oc, half)] = sPE.inc(last, 1)

                                    @TT
                                    def _(t, oc=oc, sl=sl, half=half):
                                        bank = oc % 2
                                        twait(t, evWkvA[(1, half, oc)], lastV.get(bank))
                                        last = None
                                        for hc in range(HC):
                                            last = t.matmul(
                                                pkV[bank][:, :],
                                                wv_db[:, sl * H + hc * 128: sl * H + (hc + 1) * 128],
                                                xn_fm[:, hc * T + half * 512: hc * T + half * 512 + 512],
                                                start=(hc == 0), stop=(hc == HC - 1))
                                        evKmm[(1, oc, half)] = sPE.inc(last, 1)

                                    @VV
                                    def _(v, oc=oc, half=half):
                                        bank = oc % 2
                                        if half == 0 and oc == 0:
                                            wait(v, evP0stage)
                                        wait(v, evKmm[(0, oc, half)], evActK.get((0, oc - 2, half)))
                                        evTsK[(oc, half)] = sDVE.inc(v.tensor_scalar(
                                            kst[:, half * 512:(half + 1) * 512],
                                            pkK[bank][:, :],
                                            bk_sb[:, oc:oc + 1], 0.0,
                                            op0=ALU.add, op1=ALU.min), 1)

                                    @AA
                                    def _(a, oc=oc, half=half):
                                        bank = oc % 2
                                        wait(a, evTsK[(oc, half)],
                                             evPhi.get((0, oc - 2, half)))
                                        a.activation(out=est[:, half * 512:(half + 1) * 512],
                                                     in_=kst[:, half * 512:(half + 1) * 512],
                                                     func=AF.Exp)
                                        evActK[(0, oc, half)] = sACT.inc(a.activation(
                                            out=rst[:, half * 512:(half + 1) * 512],
                                            in_=pkK[bank][:, :],
                                            func=AF.Relu, bias=bk_sb[:, oc:oc + 1]), 1)
                                    lastK[oc % 2] = (sACT, sACT.n)

                                    @VV
                                    def _(v, oc=oc, half=half):
                                        wait(v, evActK[(0, oc, half)])
                                        evPhi[(0, oc, half)] = vch(v, v.tensor_add(
                                            kphiK[:, half * 512:(half + 1) * 512],
                                            rst[:, half * 512:(half + 1) * 512],
                                            est[:, half * 512:(half + 1) * 512]))
                                        vch(v, v.reduce_sum(
                                            ksum_h[:, half * NKV + oc: half * NKV + oc + 1],
                                            kphiK[:, half * 512:(half + 1) * 512],
                                            axis=AX.X))
                                        if half == 1:
                                            vch(v, v.tensor_add(
                                                ksum[:, oc:oc + 1],
                                                ksum_h[:, oc:oc + 1],
                                                ksum_h[:, NKV + oc:NKV + oc + 1]))

                                    @VV
                                    def _(v, oc=oc, half=half):
                                        bank = oc % 2
                                        wait(v, evKmm[(1, oc, half)])
                                        evPhi[(1, oc, half)] = sDVE.inc(v.tensor_scalar(
                                            kphiV[:, half * 512:(half + 1) * 512],
                                            pkV[bank][:, :],
                                            bv_sb[:, oc:oc + 1], None, op0=ALU.add), 1)
                                    lastV[oc % 2] = (sDVE, sDVE.n)

                                    @GP
                                    def _(g, oc=oc, half=half, gidx=gidx):
                                        if gidx + 2 < 2 * NKV:
                                            g2 = gidx + 2
                                            h2, oc2 = divmod(g2, NKV)
                                            for kvi in (0, 1):
                                                wait(g, evKmm[(kvi, oc, half)])
                                                evWkvA[(kvi, h2, oc2)] = (ssem(g2) if kvi == 0 else ssem2(g2)).inc(
                                                    g.dma_start(
                                                        out=wbuf[kvi][:, (g2 % 2) * H:(g2 % 2 + 1) * H],
                                                        in_=wdr[kvi][oc2 * 128:(oc2 + 1) * 128, :]), 16)

                            p1_transposes(NKV - 1, 1)

                            evKVtmDone = (sDVE, sDVE.n)

                        with nc.psum_tensor("pkv0", [128, 512], F32) as pkv0, \
                             nc.psum_tensor("pkv1", [128, 512], F32) as pkv1:
                            pkv = [pkv0, pkv1]
                            evKVp = [None]

                            @TT
                            def _(t):
                                twait(t, evKVtmDone)
                                last = None
                                for h in range(NKV):
                                    for lc in range(NTC):
                                        last = t.matmul(
                                            pkv[h // 4][:, (h % 4) * 128:(h % 4 + 1) * 128],
                                            k_tm[:, lc * KD + h * 128:lc * KD + (h + 1) * 128],
                                            v_tm[:, lc * KD + h * 128:lc * KD + (h + 1) * 128],
                                            start=(lc == 0), stop=(lc == NTC - 1))
                                evKVp[0] = sPE.inc(last, 1)

                            @VV
                            def _(v):
                                wait(v, evKVp[0])
                                v.tensor_copy(kvpack[:, 0:512], pkv0[:, :])
                                v.tensor_copy(kvpack[:, 512:1024], pkv1[:, :])
                                evKvpack[0] = sDVE.inc(v.tensor_copy(
                                    kvpack[:, 1024:1024 + NKV], ksum[:, :]), 1)

                            if dbg:
                                @SY
                                def _(q):
                                    wait(q, evXnH1[0])
                                    sSY.inc(q.dma_start(out=xn_dump, in_=xn_fm), 16)
                                    wait(q, evKVtmDone)
                                    sSY.inc(q.dma_start(out=ktm_dump, in_=k_tm[:, :]), 16)
                                    sSY.inc(q.dma_start(out=vtm_dump, in_=v_tm[:, :]), 16)

                @GP
                def _(g):
                    wait(g, evKvpack[0])
                    sW.inc(g.dma_start(out=cc_in, in_=kvpack[:, :]), 16)
                    g.wait_ge(sW.h, sW.n)
                    ins = g.collective_compute(
                        "AllReduce", ALU.add,
                        replica_groups=[list(range(NCORES))],
                        ins=[cc_in], outs=[cc_out])
                    ins.then_inc(sCC.h, 1)
                    sCC.n += 1

                evGate = {}
                evWgH = {}
                evWdH = {}
                evUp = {}
                evSilu = {}
                evGuEvac = {}
                evWg = {}
                evWu = {}
                evF = {}
                evFEvac = {}
                evFSt = {}
                evFRead = {}
                evAdd = {}
                evG = {}
                evGSt = {}
                evWd = {}

                with nc.sbuf_tensor("gu", [128, ICA * T], BF16) as gu_t:
                    gu = gu_t[:, :]

                    def gateup(mega):
                        icl = list(range(0, ICA)) if mega == 0 else list(range(ICA, ICN))
                        BB = barrier()
                        with ExitStack() as ctx:
                            wg_db = ctx.enter_context(nc.sbuf_tensor("wg_db%d" % mega, [128, 2 * H], BF16))
                            wu_db = ctx.enter_context(nc.sbuf_tensor("wu_db%d" % mega, [128, 2 * H], BF16))
                            sg_st = ctx.enter_context(nc.sbuf_tensor("sg_st%d" % mega, [128, T], F32))
                            pg = [ctx.enter_context(nc.psum_tensor("pg%d_%d" % (i, mega), [128, 1024], F32))
                                  for i in range(2)]
                            pu = [ctx.enter_context(nc.psum_tensor("pu%d_%d" % (i, mega), [128, 1024], F32))
                                  for i in range(2)]

                            @GP
                            def _(g):
                                wait(g, *BB)
                                ic0 = icl[0]
                                sl0 = ic0 % 2
                                evWgH[ic0] = ssem(ic0).inc(g.dma_start(
                                    out=wg_db[:, sl0 * H: sl0 * H + H // 2],
                                    in_=wg[ic0 * 128:(ic0 + 1) * 128, 0:H // 2]), 16)
                                evWg[ic0] = sW.inc(g.dma_start(
                                    out=wg_db[:, sl0 * H + H // 2: sl0 * H + H],
                                    in_=wg[ic0 * 128:(ic0 + 1) * 128, H // 2:H]), 16)
                                evWu[ic0] = ssem2(ic0).inc(g.dma_start(
                                    out=wu_db[:, sl0 * H:(sl0 + 1) * H],
                                    in_=wu[ic0 * 128:(ic0 + 1) * 128, :]), 16)
                                for ic in icl[1:2]:
                                    evWg[ic] = ssem(ic).inc(g.dma_start(
                                        out=wg_db[:, (ic % 2) * H:(ic % 2 + 1) * H],
                                        in_=wg[ic * 128:(ic + 1) * 128, :]), 16)
                                    evWu[ic] = ssem2(ic).inc(g.dma_start(
                                        out=wu_db[:, (ic % 2) * H:(ic % 2 + 1) * H],
                                        in_=wu[ic * 128:(ic + 1) * 128, :]), 16)

                            first = [True, True, True]
                            for idx, ic in enumerate(icl):
                                sl = ic % 2

                                @TT
                                def _(t, ic=ic, sl=sl, idx=idx):
                                    if first[0]:
                                        twait(t, *BB)
                                        first[0] = False
                                    if idx == 0:
                                        twait(t, evWgH[ic])
                                    else:
                                        twait(t, evWg[ic])
                                    if idx >= 2:
                                        prev = icl[idx - 2]
                                        twait(t, evSilu.get(prev), evGuEvac.get(prev))
                                    last = None
                                    for hc in range(HC):
                                        if idx == 0 and hc == HC // 2:
                                            twait(t, evWg[ic])
                                        for half in (0, 1):
                                            last = t.matmul(
                                                pg[sl][:, half * 512:(half + 1) * 512],
                                                wg_db[:, sl * H + hc * 128: sl * H + (hc + 1) * 128],
                                                xn_fm[:, hc * T + half * 512: hc * T + half * 512 + 512],
                                                start=(hc == 0), stop=(hc == HC - 1))
                                    evGate[ic] = sPE.inc(last, 1)
                                    twait(t, evWu[ic])
                                    for hc in range(HC):
                                        for half in (0, 1):
                                            last = t.matmul(
                                                pu[sl][:, half * 512:(half + 1) * 512],
                                                wu_db[:, sl * H + hc * 128: sl * H + (hc + 1) * 128],
                                                xn_fm[:, hc * T + half * 512: hc * T + half * 512 + 512],
                                                start=(hc == 0), stop=(hc == HC - 1))
                                    evUp[ic] = sPE.inc(last, 1)

                                @GP
                                def _(g, idx=idx):
                                    if idx + 2 < len(icl):
                                        ic2 = icl[idx + 2]
                                        wait(g, evGate[icl[idx]], evUp[icl[idx]])
                                        evWg[ic2] = ssem(ic2).inc(g.dma_start(
                                            out=wg_db[:, (ic2 % 2) * H:(ic2 % 2 + 1) * H],
                                            in_=wg[ic2 * 128:(ic2 + 1) * 128, :]), 16)
                                        evWu[ic2] = ssem2(ic2).inc(g.dma_start(
                                            out=wu_db[:, (ic2 % 2) * H:(ic2 % 2 + 1) * H],
                                            in_=wu[ic2 * 128:(ic2 + 1) * 128, :]), 16)

                                @AA
                                def _(a, ic=ic, sl=sl, idx=idx):
                                    if first[1]:
                                        wait(a, *BB)
                                        first[1] = False
                                    wait(a, evGate[ic])
                                    if idx >= 1:
                                        wait(a, evGuEvac.get(icl[idx - 1]))
                                    evSilu[ic] = sACT.inc(a.activation(
                                        out=sg_st[:, :], in_=pg[sl][:, :], func=AF.Silu), 1)

                                @VV
                                def _(v, ic=ic, sl=sl, idx=idx):
                                    if first[2]:
                                        wait(v, *BB)
                                        first[2] = False
                                    wait(v, evSilu[ic], evUp[ic])
                                    evGuEvac[ic] = sDVE.inc(v.tensor_mul(
                                        gu[:, idx * T:(idx + 1) * T], sg_st[:, :], pu[sl][:, :]), 1)

                    def down_phase(mega, g_write):
                        icl = list(range(0, ICA)) if mega == 0 else list(range(ICA, ICN))
                        nIC = len(icl)
                        CW = nIC * 128
                        lastGu = evGuEvac[icl[-1]]
                        BB = barrier()
                        with ExitStack() as ctx:
                            wd_db = ctx.enter_context(nc.sbuf_tensor("wd_db%d" % mega, [128, 2 * CW], BF16))
                            if g_write:
                                frd = ctx.enter_context(nc.sbuf_tensor("frd", [128, 2 * T], F32))
                                tmpf = ctx.enter_context(nc.sbuf_tensor("tmpf", [128, 2 * T], F32))
                                g_st = ctx.enter_context(nc.sbuf_tensor("g_st", [128, 2 * T], BF16))
                            else:
                                f_st = ctx.enter_context(nc.sbuf_tensor("f_st%d" % mega, [128, 2 * T], F32))
                            pf = [ctx.enter_context(nc.psum_tensor("pf%d_%d" % (i, mega), [128, 1024], F32))
                                  for i in range(4)]

                            @GP
                            def _(g):
                                wait(g, *BB)
                                cw2 = CW // 2
                                evWdH[mega] = ssem(0).inc(g.dma_start(
                                    out=wd_db[:, 0:cw2],
                                    in_=wd[0:128, mega * ICA * 128: mega * ICA * 128 + cw2]), 16)
                                evWd[(mega, 0)] = sW.inc(g.dma_start(
                                    out=wd_db[:, cw2:CW],
                                    in_=wd[0:128, mega * ICA * 128 + cw2: mega * ICA * 128 + CW]), 16)
                                evWd[(mega, 1)] = ssem(1).inc(g.dma_start(
                                    out=wd_db[:, CW:2 * CW],
                                    in_=wd[128:256, mega * ICA * 128: mega * ICA * 128 + CW]), 16)

                            if g_write:
                                @SY
                                def _(q):
                                    wait(q, *BB)
                                    for oc in (0, 1):
                                        wait(q, evFSt[oc])
                                        evFRead[oc] = ssem2(oc).inc(q.dma_start(
                                            out=frd[:, (oc % 2) * T:(oc % 2 + 1) * T],
                                            in_=f_sc[oc * 128:(oc + 1) * 128, :]), 16)

                            first = [True, True, True, True]
                            for oc in range(HC):
                                sl = oc % 2

                                @TT
                                def _(t, oc=oc, sl=sl):
                                    if first[0]:
                                        twait(t, *BB)
                                        first[0] = False
                                    if oc == 0:
                                        twait(t, evWdH[mega], lastGu)
                                    else:
                                        twait(t, evWd[(mega, oc)], lastGu)
                                    if g_write:
                                        twait(t, evAdd.get(oc - 4))
                                    else:
                                        twait(t, evFEvac.get(oc - 4))
                                    pj = pf[oc % 4]
                                    last = None
                                    for li in range(nIC):
                                        if oc == 0 and li == nIC // 2:
                                            twait(t, evWd[(mega, 0)])
                                        for half in (0, 1):
                                            last = t.matmul(
                                                pj[:, half * 512:(half + 1) * 512],
                                                wd_db[:, sl * CW + li * 128: sl * CW + (li + 1) * 128],
                                                gu[:, li * T + half * 512: li * T + half * 512 + 512],
                                                start=(li == 0), stop=(li == nIC - 1))
                                    evF[(mega, oc)] = sPE.inc(last, 1)

                                @GP
                                def _(g, oc=oc):
                                    if oc + 2 < HC:
                                        wait(g, evF[(mega, oc)])
                                        s2 = (oc + 2) % 2
                                        evWd[(mega, oc + 2)] = ssem(oc).inc(g.dma_start(
                                            out=wd_db[:, s2 * CW:(s2 + 1) * CW],
                                            in_=wd[(oc + 2) * 128:(oc + 3) * 128,
                                                   mega * ICA * 128: mega * ICA * 128 + CW]), 16)

                                if not g_write:
                                    @VV
                                    def _(v, oc=oc, sl=sl):
                                        if first[1]:
                                            wait(v, *BB)
                                            first[1] = False
                                        wait(v, evF[(0, oc)], evFSt.get(oc - 2))
                                        evFEvac[oc] = sDVE.inc(v.tensor_copy(
                                            f_st[:, sl * T:(sl + 1) * T], pf[oc % 4][:, :]), 1)

                                    @SY
                                    def _(q, oc=oc, sl=sl):
                                        if first[3]:
                                            wait(q, *BB)
                                            first[3] = False
                                        wait(q, evFEvac[oc])
                                        evFSt[oc] = sSY.inc(q.dma_start(
                                            out=f_sc[oc * 128:(oc + 1) * 128, :],
                                            in_=f_st[:, sl * T:(sl + 1) * T]), 16)
                                else:
                                    @VV
                                    def _(v, oc=oc, sl=sl):
                                        if first[1]:
                                            wait(v, *BB)
                                            first[1] = False
                                        wait(v, evF[(1, oc)], evFRead[oc], evG.get(oc - 2))
                                        evAdd[oc] = sDVE.inc(v.tensor_add(
                                            tmpf[:, sl * T:(sl + 1) * T],
                                            pf[oc % 4][:, :], frd[:, sl * T:(sl + 1) * T]), 1)

                                    @AA
                                    def _(a, oc=oc, sl=sl):
                                        if first[2]:
                                            wait(a, *BB)
                                            first[2] = False
                                        wait(a, evAdd[oc], evGSt.get(oc - 2))
                                        evG[oc] = sACT.inc(a.activation(
                                            out=g_st[:, sl * T:(sl + 1) * T],
                                            in_=tmpf[:, sl * T:(sl + 1) * T], func=AF.Silu), 1)

                                    @SY
                                    def _(q, oc=oc, sl=sl):
                                        if first[3]:
                                            wait(q, *BB)
                                            first[3] = False
                                        wait(q, evG[oc])
                                        evGSt[oc] = sSY.inc(q.dma_start(
                                            out=g_sc[oc * 128:(oc + 1) * 128, :],
                                            in_=g_st[:, sl * T:(sl + 1) * T]), 16)
                                        if oc + 2 < HC:
                                            wait(q, evAdd[oc])
                                            evFRead[oc + 2] = ssem2(oc).inc(q.dma_start(
                                                out=frd[:, sl * T:(sl + 1) * T],
                                                in_=f_sc[(oc + 2) * 128:(oc + 3) * 128, :]), 16)

                    gateup(0)
                    if dbg:
                        @SY
                        def _(q):
                            wait(q, evGuEvac[ICA - 1])
                            sSY.inc(q.dma_start(out=gu_dump, in_=gu_t[:, :]), 16)
                    down_phase(0, g_write=False)
                    gateup(1)
                    down_phase(1, g_write=True)

                evQmm = {}
                evWqH = {}
                evWoH = {}
                evTsQ = {}
                evActQ = {}
                evQbf = {}
                evWq = {}
                evKvg = [None]
                evRec = [None]
                evAmm = {}
                evAsc = {}
                evWo = {}
                evSmm = {}
                evP = {}
                evPSt = {}
                evGload = {}

                p_tm4 = nc.alloc_sbuf_tensor("p_tm4", [128, 5 * H], BF16, side="right")
                evPTw1 = [None]

                with nc.sbuf_tensor("q_bf", [128, NH * T], BF16) as q_bf_t:
                    q_bf = q_bf_t[:, :]
                    BQ = barrier()
                    with ExitStack() as ctq:
                        wq_db = ctq.enter_context(nc.sbuf_tensor("wq_db", [128, 2 * H], BF16))
                        qtmin = ctq.enter_context(nc.sbuf_tensor("qtmin", [128, 1024], F32))
                        qe = ctq.enter_context(nc.sbuf_tensor("qe", [128, 1024], F32))
                        qr = ctq.enter_context(nc.sbuf_tensor("qr", [128, 1024], F32))
                        pq = [ctq.enter_context(nc.psum_tensor("pq%d" % i, [128, 512], F32))
                              for i in range(4)]

                        @GP
                        def _(g):
                            wait(g, *BQ)
                            evWqH[0] = ssem(0).inc(g.dma_start(
                                out=wq_db[:, 0:H // 2],
                                in_=wq[0:128, 0:H // 2]), 16)
                            evWq[0] = sW.inc(g.dma_start(
                                out=wq_db[:, H // 2:H],
                                in_=wq[0:128, H // 2:H]), 16)
                            evWq[1] = ssem(1).inc(g.dma_start(
                                out=wq_db[:, H:2 * H],
                                in_=wq[128:256, :]), 16)

                        first = [True, True, True]
                        for oc in range(NH):
                            sl = oc % 2

                            @TT
                            def _(t, oc=oc, sl=sl):
                                if first[0]:
                                    twait(t, *BQ)
                                    first[0] = False
                                twait(t, evWqH[0] if oc == 0 else evWq[oc])
                                for b in (0, 1):
                                    twait(t, evActQ.get((oc - 2, b)))
                                lastmm = {}
                                for hc in range(HC):
                                    if oc == 0 and hc == HC // 2:
                                        twait(t, evWq[0])
                                    for half in (0, 1):
                                        bank = (oc % 2) * 2 + half
                                        lastmm[half] = t.matmul(
                                            pq[bank][:, :],
                                            wq_db[:, sl * H + hc * 128: sl * H + (hc + 1) * 128],
                                            xn_fm[:, hc * T + half * 512: hc * T + half * 512 + 512],
                                            start=(hc == 0), stop=(hc == HC - 1))
                                evQmm[(oc, 0)] = sPE.inc(lastmm[0], 1)
                                evQmm[(oc, 1)] = sPE.inc(lastmm[1], 1)

                            @GP
                            def _(g, oc=oc):
                                if oc + 2 < NH:
                                    wait(g, evQmm[(oc, 1)])
                                    s2 = (oc + 2) % 2
                                    evWq[oc + 2] = ssem(oc).inc(g.dma_start(
                                        out=wq_db[:, s2 * H:(s2 + 1) * H],
                                        in_=wq[(oc + 2) * 128:(oc + 3) * 128, :]), 16)

                            @VV
                            def _(v, oc=oc):
                                if first[1]:
                                    wait(v, *BQ)
                                    first[1] = False
                                for half in (0, 1):
                                    bank = (oc % 2) * 2 + half
                                    wait(v, evQmm[(oc, half)], evActQ.get((oc - 1, half)))
                                    evTsQ[(oc, half)] = sDVE.inc(v.tensor_scalar(
                                        qtmin[:, half * 512:(half + 1) * 512], pq[bank][:, :],
                                        bq_sb[:, oc:oc + 1], 0.0, op0=ALU.add, op1=ALU.min), 1)

                            @AA
                            def _(a, oc=oc):
                                if first[2]:
                                    wait(a, *BQ)
                                    first[2] = False
                                for half in (0, 1):
                                    bank = (oc % 2) * 2 + half
                                    wait(a, evTsQ[(oc, half)], evQbf.get((oc - 1, half)))
                                    a.activation(out=qe[:, half * 512:(half + 1) * 512],
                                                 in_=qtmin[:, half * 512:(half + 1) * 512],
                                                 func=AF.Exp)
                                    evActQ[(oc, half)] = sACT.inc(a.activation(
                                        out=qr[:, half * 512:(half + 1) * 512],
                                        in_=pq[bank][:, :], func=AF.Relu,
                                        bias=bq_sb[:, oc:oc + 1]), 1)

                            @VV
                            def _(v, oc=oc):
                                for half in (0, 1):
                                    wait(v, evActQ[(oc, half)])
                                    evQbf[(oc, half)] = sDVE.inc(v.tensor_add(
                                        q_bf[:, oc * T + half * 512: oc * T + half * 512 + 512],
                                        qr[:, half * 512:(half + 1) * 512],
                                        qe[:, half * 512:(half + 1) * 512]), 1)

                    @GP
                    def _(g):
                        g.wait_ge(sCC.h, 1)
                        evKvg[0] = sSC.inc(g.dma_start(out=kvg_sb[:, :], in_=cc_out), 16)

                    @VV
                    def _(v):
                        wait(v, evKvg[0])
                        v.tensor_copy(kv_bf[:, :], kvg_sb[:, 0:KD])
                        vch(v, v.tensor_single_scalar(kvg_sb[:, KD:KD + NKV],
                                                      kvg_sb[:, KD:KD + NKV],
                                                      EPS_ATT, op=ALU.add))
                        evRec[0] = vch(v, v.reciprocal(rec_sb[:, :], kvg_sb[:, KD:KD + NKV]))

                    if dbg:
                        @SY
                        def _(q):
                            wait(q, evRec[0])
                            sSY.inc(q.dma_start(out=kvg_dump, in_=kvg_sb[:, :]), 16)
                            wait(q, evKvpack[0])
                            sSY.inc(q.dma_start(out=cc_dump, in_=kvpack[:, :]), 16)

                    BA = barrier()
                    with ExitStack() as cta:
                        pa = [cta.enter_context(nc.psum_tensor("pa%d" % i, [128, 512], F32))
                              for i in range(4)]
                        first = [True, True]
                        for h in range(NH):
                            hk = h // 4

                            @TT
                            def _(t, h=h, hk=hk):
                                if first[0]:
                                    twait(t, *BA)
                                    first[0] = False
                                twait(t, evRec[0], evQbf.get((NH - 1, 1)))
                                for half in (0, 1):
                                    bank = (h % 2) * 2 + half
                                    twait(t, evAsc.get((h - 2, half)))
                                    evAmm[(h, half)] = sPE.inc(t.matmul(
                                        pa[bank][:, :],
                                        kv_bf[:, hk * 128:(hk + 1) * 128],
                                        q_bf[:, h * T + half * 512: h * T + half * 512 + 512],
                                        start=True, stop=True), 1)

                            @VV
                            def _(v, h=h, hk=hk):
                                if first[1]:
                                    wait(v, *BA)
                                    first[1] = False
                                for half in (0, 1):
                                    bank = (h % 2) * 2 + half
                                    wait(v, evAmm[(h, half)])
                                    evAsc[(h, half)] = sDVE.inc(v.tensor_scalar(
                                        q_bf[:, h * T + half * 512: h * T + half * 512 + 512],
                                        pa[bank][:, :], rec_sb[:, hk:hk + 1], None,
                                        op0=ALU.mult), 1)

                    if dbg:
                        @SY
                        def _(q):
                            wait(q, evAsc[(NH - 1, 1)])
                            sSY.inc(q.dma_start(out=q_dump, in_=q_bf), 16)

                    BO = barrier()
                    with ExitStack() as cto:
                        wo_db = cto.enter_context(nc.sbuf_tensor("wo_db", [128, 2 * H], BF16))
                        g_buf = cto.enter_context(nc.sbuf_tensor("g_buf", [128, 2 * T], BF16))
                        p_st = cto.enter_context(nc.sbuf_tensor("p_st", [128, 2 * T], BF16))
                        ps = [cto.enter_context(nc.psum_tensor("ps%d" % i, [128, 512], F32))
                              for i in range(4)]

                        @GP
                        def _(g):
                            wait(g, *BO)
                            evWoH[0] = ssem(0).inc(g.dma_start(
                                out=wo_db[:, 0:H // 2],
                                in_=wo[0:128, 0:H // 2]), 16)
                            evWo[0] = sW.inc(g.dma_start(
                                out=wo_db[:, H // 2:H],
                                in_=wo[0:128, H // 2:H]), 16)
                            evWo[1] = ssem(1).inc(g.dma_start(
                                out=wo_db[:, H:2 * H],
                                in_=wo[128:256, :]), 16)

                        @SY
                        def _(q):
                            wait(q, *BO)
                            for oc in (0, 1):
                                wait(q, evGSt[oc])
                                evGload[oc] = ssem2(oc).inc(q.dma_start(
                                    out=g_buf[:, (oc % 2) * T:(oc % 2 + 1) * T],
                                    in_=g_sc[oc * 128:(oc + 1) * 128, :]), 16)

                        first = [True, True, True]
                        for oc in range(HC):
                            sl = oc % 2

                            @TT
                            def _(t, oc=oc, sl=sl):
                                if first[0]:
                                    twait(t, *BO)
                                    first[0] = False
                                twait(t, evWoH[0] if oc == 0 else evWo[oc],
                                      evAsc.get((NH - 1, 1)))
                                for b in (0, 1):
                                    twait(t, evP.get((oc - 2, b)))
                                lastmm = {}
                                for vc in range(HC):
                                    if oc == 0 and vc == HC // 2:
                                        twait(t, evWo[0])
                                    for half in (0, 1):
                                        bank = (oc % 2) * 2 + half
                                        lastmm[half] = t.matmul(
                                            ps[bank][:, :],
                                            wo_db[:, sl * H + vc * 128: sl * H + (vc + 1) * 128],
                                            q_bf[:, vc * T + half * 512: vc * T + half * 512 + 512],
                                            start=(vc == 0), stop=(vc == HC - 1))
                                evSmm[(oc, 0)] = sPE.inc(lastmm[0], 1)
                                evSmm[(oc, 1)] = sPE.inc(lastmm[1], 1)

                            @GP
                            def _(g, oc=oc):
                                if oc + 2 < HC:
                                    wait(g, evSmm[(oc, 1)])
                                    s2 = (oc + 2) % 2
                                    evWo[oc + 2] = ssem(oc).inc(g.dma_start(
                                        out=wo_db[:, s2 * H:(s2 + 1) * H],
                                        in_=wo[(oc + 2) * 128:(oc + 3) * 128, :]), 16)

                            @VV
                            def _(v, oc=oc, sl=sl):
                                if first[1]:
                                    wait(v, *BO)
                                    first[1] = False
                                for half in (0, 1):
                                    bank = (oc % 2) * 2 + half
                                    wait(v, evSmm[(oc, half)], evGload[oc], evPSt.get(oc - 2))
                                    evP[(oc, half)] = sDVE.inc(v.scalar_tensor_tensor(
                                        p_st[:, sl * T + half * 512: sl * T + half * 512 + 512],
                                        ps[bank][:, :], bo_sb[:, oc:oc + 1],
                                        g_buf[:, sl * T + half * 512: sl * T + half * 512 + 512],
                                        op0=ALU.add, op1=ALU.mult), 1)

                            @SY
                            def _(q, oc=oc, sl=sl):
                                if first[2]:
                                    wait(q, *BO)
                                    first[2] = False
                                wait(q, evP[(oc, 1)])
                                evPSt[oc] = sSY.inc(q.dma_start(
                                    out=p_sc[oc * 128:(oc + 1) * 128, :],
                                    in_=p_st[:, sl * T:(sl + 1) * T]), 16)
                                if oc + 2 < HC:
                                    wait(q, evGSt.get(oc + 2))
                                    s2 = (oc + 2) % 2
                                    evGload[oc + 2] = ssem2(oc).inc(q.dma_start(
                                        out=g_buf[:, s2 * T:(s2 + 1) * T],
                                        in_=g_sc[(oc + 2) * 128:(oc + 3) * 128, :]), 16)
                                if oc % 4 == 3:
                                    wait(q, evPSt[oc])
                                    lo = (oc - 3) * 128
                                    hi = (oc + 1) * 128
                                    for tc4 in range(5):
                                        sSY.inc(q.dma_start(
                                            out=p_tm4[:, tc4 * H + lo: tc4 * H + hi],
                                            in_=p_sc[lo:hi, tc4 * 128:(tc4 + 1) * 128],
                                            transpose=True), 16)
                                    if oc == HC - 1:
                                        evPTw1[0] = (sSY, sSY.n)

            BE = barrier()
            evPTw1v = evPTw1[0]
            with ExitStack() as cte:
                x2 = cte.enter_context(nc.sbuf_tensor("x2", [128, 2 * H], F32))
                outb = cte.enter_context(nc.sbuf_tensor("outb", [128, 2 * H], F32))
                evPT = {}
                evPT2 = {}
                evX2 = {}
                evEpi = {}
                evYSt = {}

                @SY
                def _(q):
                    wait(q, *BE)

                @VV
                def _(v):
                    wait(v, *BE)

                @GP
                def _(g):
                    wait(g, *BE)
                    for tc in (0, 1):
                        evX2[tc] = ssem(tc).inc(g.dma_start(
                            out=x2[:, (tc % 2) * H:((tc % 2) + 1) * H],
                            in_=x_in[tc * 128:(tc + 1) * 128, :]), 16)

                for tc in range(NTC):
                    sl = tc % 2
                    slot = tc % 5
                    if tc >= 5:
                        @SY
                        def _(q, tc=tc, slot=slot):
                            wait(q, evEpi[tc - 5])
                            wsem = [sCp, sDp, sSC][slot]
                            evPT[tc] = wsem.inc(q.dma_start(
                                out=p_tm4[:, slot * H:(slot + 1) * H],
                                in_=p_sc[:, tc * 128:(tc + 1) * 128],
                                transpose=True), 16)

                    @VV
                    def _(v, tc=tc, sl=sl, slot=slot):
                        wait(v, evX2[tc], evPT.get(tc, evPTw1v), evYSt.get(tc - 2))
                        evEpi[tc] = sDVE.inc(v.scalar_tensor_tensor(
                            outb[:, sl * H:(sl + 1) * H],
                            p_tm4[:, slot * H:(slot + 1) * H], 1.0,
                            x2[:, sl * H:(sl + 1) * H],
                            op0=ALU.mult, op1=ALU.add), 1)

                    @SY
                    def _(q, tc=tc, sl=sl):
                        wait(q, evEpi[tc])
                        evYSt[tc] = (sYa if tc % 2 == 0 else sYb).inc(q.dma_start(
                            out=y_out[tc * 128:(tc + 1) * 128, :],
                            in_=outb[:, sl * H:(sl + 1) * H]), 16)

                    @GP
                    def _(g, tc=tc):
                        if tc + 2 < NTC:
                            wait(g, evEpi[tc])
                            s2 = (tc + 2) % 2
                            evX2[tc + 2] = ssem(tc).inc(g.dma_start(
                                out=x2[:, s2 * H:(s2 + 1) * H],
                                in_=x_in[(tc + 2) * 128:(tc + 3) * 128, :]), 16)

                @SY
                def _(q):
                    wait(q, evYSt[NTC - 2], evYSt[NTC - 1])

    return nc



def _wtile(w):
    O, Hin = w.shape
    wt = np.ascontiguousarray(w.T)
    wt = wt.reshape(Hin // 128, 128, O // 128, 128)
    wt = wt.transpose(2, 1, 0, 3)
    return np.ascontiguousarray(wt).reshape(O, Hin)


def _btile(b):
    return np.ascontiguousarray(np.asarray(b, np.float32).reshape(-1, 128).T)


def kernel(L0_attn_hidden_states, L0_attn_Wq, L0_attn_bq, L0_attn_Wk, L0_attn_bk,
           L0_attn_Wv, L0_attn_bv, L0_attn_Wo, L0_attn_bo,
           L0_ffn_w_gate, L0_ffn_w_up, L0_ffn_w_down):
    x = np.ascontiguousarray(np.asarray(L0_attn_hidden_states, np.float32).reshape(8192, 4096))
    w_host = {
        "wq": _wtile(np.asarray(L0_attn_Wq, np.float32)),
        "wk": _wtile(np.asarray(L0_attn_Wk, np.float32)),
        "wv": _wtile(np.asarray(L0_attn_Wv, np.float32)),
        "wo": _wtile(np.asarray(L0_attn_Wo, np.float32)),
        "wg": _wtile(np.asarray(L0_ffn_w_gate, np.float32)),
        "wu": _wtile(np.asarray(L0_ffn_w_up, np.float32)),
        "wd": _wtile(np.asarray(L0_ffn_w_down, np.float32)),
        "bq": _btile(L0_attn_bq),
        "bk": _btile(L0_attn_bk),
        "bv": _btile(L0_attn_bv),
        "bo": _btile(L0_attn_bo),
        "ident_in": np.eye(128, dtype=np.float32),
    }
    nc = build_nc(dbg=DBG)
    in_maps = []
    for c in range(NCORES):
        m = {"x": np.ascontiguousarray(x[c * T:(c + 1) * T])}
        m.update(w_host)
        in_maps.append(m)
    res = run_bass_kernel_spmd(nc, in_maps, core_ids=list(range(NCORES)))
    out = np.concatenate([r["y"] for r in res.results], axis=0)
    if DBG:
        kernel.last_results = res.results
    return out.reshape(1, 8192, 4096)


if __name__ == "__main__":
    import time
    t0 = time.time()
    build_nc()
    print("build ok in %.1fs" % (time.time() - t0))


# version 28
# speedup vs baseline: 1.0136x; 1.0058x over previous
import numpy as np
import concourse.bass as bass
import concourse.mybir as mybir
import concourse.bass_utils as _bu
from concourse.bass_utils import run_bass_kernel_spmd
from contextlib import ExitStack

LDWOPT = True

if LDWOPT and not getattr(_bu, "_ldwopt_patched", False):
    _orig_run_command = _bu.run_command

    def _run_command_ldwopt(argv, **kw):
        argv = ["--enable-ldw-opt=true" if a == "--enable-ldw-opt=false" else a
                for a in argv]
        return _orig_run_command(argv, **kw)

    _bu.run_command = _run_command_ldwopt
    _bu._ldwopt_patched = True

F32 = mybir.dt.float32
BF16 = mybir.dt.bfloat16
AF = mybir.ActivationFunctionType
ALU = mybir.AluOpType
AX = mybir.AxisListType

NCORES = 8
T = 1024
H = 4096
HC = H // 128
NH, NKV, HD = 32, 8, 128
KD = NKV * HD
II = 11008
ICN = II // 128
ICA = 43
NTC = T // 128
EPS_LN = 1e-5
EPS_ATT = 1e-10

DBG = False


class Sem:
    __slots__ = ("h", "n")

    def __init__(self, h):
        self.h = h
        self.n = 0

    def inc(self, instr, k=1):
        instr.then_inc(self.h, k)
        self.n += k
        return (self, self.n)


def wait(eng, *evts):
    best = {}
    for e in evts:
        if e is None:
            continue
        s, v = e
        if v > 0 and v > best.get(id(s), (s, 0))[1]:
            best[id(s)] = (s, v)
    for s, v in best.values():
        eng.wait_ge(s.h, v)


def build_nc(dbg=False):
    nc = bass.Bass("TRN2", target_bir_lowering=False)

    x_in = nc.dram_tensor("x", [T, H], F32, kind="ExternalInput").ap()
    wq = nc.dram_tensor("wq", [H, H], F32, kind="ExternalInput").ap()
    wk = nc.dram_tensor("wk", [KD, H], F32, kind="ExternalInput").ap()
    wv = nc.dram_tensor("wv", [KD, H], F32, kind="ExternalInput").ap()
    wo = nc.dram_tensor("wo", [H, H], F32, kind="ExternalInput").ap()
    wg = nc.dram_tensor("wg", [II, H], F32, kind="ExternalInput").ap()
    wu = nc.dram_tensor("wu", [II, H], F32, kind="ExternalInput").ap()
    wd = nc.dram_tensor("wd", [H, II], F32, kind="ExternalInput").ap()
    bq = nc.dram_tensor("bq", [128, HC], F32, kind="ExternalInput").ap()
    bk = nc.dram_tensor("bk", [128, NKV], F32, kind="ExternalInput").ap()
    bv = nc.dram_tensor("bv", [128, NKV], F32, kind="ExternalInput").ap()
    bo = nc.dram_tensor("bo", [128, HC], F32, kind="ExternalInput").ap()
    ident_in = nc.dram_tensor("ident_in", [128, 128], F32, kind="ExternalInput").ap()
    y_out = nc.dram_tensor("y", [T, H], F32, kind="ExternalOutput").ap()

    sck = "ExternalOutput" if dbg else "Internal"
    f_sc = nc.dram_tensor("f_sc", [H, T], F32, kind=sck).ap()
    g_sc = nc.dram_tensor("g_sc", [H, T], BF16, kind=sck).ap()
    p_sc = nc.dram_tensor("p_sc", [H, T], BF16, kind=sck).ap()
    xn_sc = nc.dram_tensor("xn_sc", [T, H], BF16, kind="Internal").ap()
    cc_in = nc.dram_tensor("cc_in", [128, KD + NKV], F32, kind="Internal").ap()
    cc_out = nc.dram_tensor("cc_out", [128, KD + NKV], F32, kind="Internal").ap()
    if dbg:
        cc_dump = nc.dram_tensor("cc_dump", [128, KD + NKV], F32, kind="ExternalOutput").ap()
        gu_dump = nc.dram_tensor("gu_dump", [128, ICA * T], BF16, kind="ExternalOutput").ap()
        xn_dump = nc.dram_tensor("xn_dump", [128, HC * T], BF16, kind="ExternalOutput").ap()
        q_dump = nc.dram_tensor("q_dump", [128, NH * T], BF16, kind="ExternalOutput").ap()
        ktm_dump = nc.dram_tensor("ktm_dump", [128, NTC * KD], BF16, kind="ExternalOutput").ap()
        vtm_dump = nc.dram_tensor("vtm_dump", [128, NTC * KD], BF16, kind="ExternalOutput").ap()
        kvg_dump = nc.dram_tensor("kvg_dump", [128, KD + NKV], F32, kind="ExternalOutput").ap()

    with ExitStack() as stack:
        sb_outer = lambda name, shape, dt: stack.enter_context(nc.sbuf_tensor(name, shape, dt))
        ident = sb_outer("ident", [128, 128], BF16)
        bq_sb = sb_outer("bq_sb", [128, HC], F32)
        bk_sb = sb_outer("bk_sb", [128, NKV], F32)
        bv_sb = sb_outer("bv_sb", [128, NKV], F32)
        bo_sb = sb_outer("bo_sb", [128, HC], F32)
        kvg_sb = sb_outer("kvg_sb", [128, KD + NKV], F32)
        kv_bf = sb_outer("kv_bf", [128, KD], BF16)
        rec_sb = sb_outer("rec_sb", [128, NKV], F32)
        kvpack = sb_outer("kvpack", [128, KD + NKV], F32)
        sb_outer("cachebust_ldwopt%d" % int(LDWOPT), [128, 1], F32)

        sW = Sem(stack.enter_context(nc.semaphore("sW")))
        sSY = Sem(stack.enter_context(nc.semaphore("sSY")))
        sPE = Sem(stack.enter_context(nc.semaphore("sPE")))
        sDVE = Sem(stack.enter_context(nc.semaphore("sDVE")))
        sACT = Sem(stack.enter_context(nc.semaphore("sACT")))
        sPL = Sem(stack.enter_context(nc.semaphore("sPL")))
        sCC = Sem(stack.enter_context(nc.semaphore("sCC")))
        sSC = Sem(stack.enter_context(nc.semaphore("sSC")))
        sAp = Sem(stack.enter_context(nc.semaphore("sAp")))
        sBp = Sem(stack.enter_context(nc.semaphore("sBp")))
        sCp = Sem(stack.enter_context(nc.semaphore("sCp")))

        sDp = Sem(stack.enter_context(nc.semaphore("sDp")))
        sYa = Sem(stack.enter_context(nc.semaphore("sYa")))
        sYb = Sem(stack.enter_context(nc.semaphore("sYb")))
        sXNa = Sem(stack.enter_context(nc.semaphore("sXNa")))
        sXNb = Sem(stack.enter_context(nc.semaphore("sXNb")))
        sXa = Sem(stack.enter_context(nc.semaphore("sXa")))
        sXb = Sem(stack.enter_context(nc.semaphore("sXb")))

        def ssem(i):
            return sAp if i % 2 == 0 else sBp

        def ssem2(i):
            return sCp if i % 2 == 0 else sDp
        ALLSEMS = [sW, sSY, sPE, sDVE, sACT, sPL, sSC, sAp, sBp, sCp, sDp, sYa, sYb,
                   sXNa, sXNb, sXa, sXb]

        def barrier():
            return [(s, s.n) for s in ALLSEMS]

        def vch(v, ins):
            e = sDVE.inc(ins, 1)
            v.wait_ge(sDVE.h, e[1])
            return e

        def ach(a, ins):
            e = sACT.inc(ins, 1)
            a.wait_ge(sACT.h, e[1])
            return e

        def twait(t, *evts):
            wait(t, *evts)
            t.nop(nofuse=True)

        with nc.Block() as block:
            TT, VV, AA, SY, GP = block.tensor, block.vector, block.scalar, block.sync, block.gpsimd

            evIdent = [None]

            @GP
            def _(g):
                evIdent[0] = sPL.inc(g.dma_start(out=ident[:, :], in_=ident_in), 16)
                sW.inc(g.dma_start(out=bq_sb[:, :], in_=bq), 16)
                sW.inc(g.dma_start(out=bk_sb[:, :], in_=bk), 16)
                sW.inc(g.dma_start(out=bv_sb[:, :], in_=bv), 16)
                sW.inc(g.dma_start(out=bo_sb[:, :], in_=bo), 16)

            with nc.sbuf_tensor("xn_fm", [128, HC * T], BF16) as xn_fm_t:
                xn_fm = xn_fm_t[:, :]
                evXnDone = [None]
                evXnH0 = [None]
                evXnH1 = [None]
                evXnFree = {}
                slot_of = {}
                evPtW = [None] * 4
                evPtR = [None] * 4
                tr = [0]

                with ExitStack() as st_tr:
                    pt = [st_tr.enter_context(nc.psum_tensor("pt%d" % i, [128, 1024], BF16))
                          for i in range(4)]
                    xn_st = st_tr.enter_context(nc.sbuf_tensor("xn_st", [128, 4 * H], BF16))
                    wk_db = st_tr.enter_context(nc.sbuf_tensor("wk_db", [128, 2 * H], BF16))
                    wv_db = st_tr.enter_context(nc.sbuf_tensor("wv_db", [128, 2 * H], BF16))

                    with ExitStack() as st0:
                        st0_keep = []
                        x_db = st0.enter_context(nc.sbuf_tensor("x_db", [128, 4 * H], F32))
                        bnst = st0.enter_context(nc.sbuf_tensor("bnst", [128, 48], F32))
                        p0t = st0.enter_context(nc.sbuf_tensor("p0t", [128, 8], F32))
                        rstd_all = st0.enter_context(nc.sbuf_tensor("rstd_all", [128, NTC], F32))
                        evX = {}
                        evFreeX = {}
                        evXsq = {}
                        evVar = {}
                        evRstd = {}
                        evXnStage = {}

                        xn_fm3 = xn_fm_t[:, :].rearrange("p (hcx tt) -> p hcx tt", tt=T)

                        def p0_transposes(tc):
                            sl = tc % 4
                            for gq in range(8):
                                @TT
                                def _(t, tc=tc, sl=sl, gq=gq):
                                    j = tr[0] % 4
                                    twait(t, evIdent[0], evXnStage[tc], evPtR[j])
                                    ins = None
                                    for k in range(4):
                                        hc = gq * 4 + k
                                        ins = t.transpose(
                                            pt[j][:, k * 128:(k + 1) * 128],
                                            xn_st[:, sl * H + hc * 128: sl * H + (hc + 1) * 128],
                                            ident[:, :])
                                    evPtW[j] = sPE.inc(ins, 1)
                                    slot_of[("p0g", tc, gq)] = j
                                    tr[0] += 1

                                @VV
                                def _(v, tc=tc, gq=gq):
                                    j = slot_of[("p0g", tc, gq)]
                                    wait(v, evPtW[j])
                                    evPtR[j] = sDVE.inc(v.tensor_copy(
                                        xn_fm3[:, gq * 4:(gq + 1) * 4, tc * 128:(tc + 1) * 128],
                                        pt[j].rearrange("p (k f) -> p k f", f=128)[:, 0:4, :]), 1)

                        @SY
                        def _(q):
                            evX[0] = sXa.inc(q.dma_start(
                                out=x_db[:, 0:H], in_=x_in[0:128, :]), 16)
                            evX[3] = sYb.inc(q.dma_start(
                                out=x_db[:, 3 * H:4 * H], in_=x_in[384:512, :]), 16)

                        @AA
                        def _(a):
                            evX[1] = sXNa.inc(a.dma_start(
                                out=x_db[:, H:2 * H], in_=x_in[128:256, :]), 16)
                            evX[2] = sYa.inc(a.dma_start(
                                out=x_db[:, 2 * H:3 * H], in_=x_in[256:384, :]), 16)

                        for tc in range(NTC):
                            sl = tc % 4

                            @VV
                            def _(v, tc=tc, sl=sl):
                                wait(v, evX[tc])
                                for gq in range(8):
                                    ins = v.bn_stats(bnst[:, gq * 6:(gq + 1) * 6],
                                                     x_db[:, sl * H + gq * 512: sl * H + (gq + 1) * 512])
                                vch(v, ins)
                                vch(v, v.bn_aggr(p0t[:, 0:2], bnst[:, :]))
                                evVar[tc] = vch(v, v.tensor_single_scalar(
                                    p0t[:, 2:3], p0t[:, 1:2], EPS_LN, op=ALU.add))

                            @AA
                            def _(a, tc=tc):
                                wait(a, evVar[tc])
                                ach(a, a.activation(out=p0t[:, 3:4], in_=p0t[:, 2:3], func=AF.Ln))
                                evRstd[tc] = sACT.inc(a.activation(
                                    out=rstd_all[:, tc:tc + 1], in_=p0t[:, 3:4],
                                    func=AF.Exp, scale=-0.5), 1)

                            @VV
                            def _(v, tc=tc, sl=sl):
                                wait(v, evRstd[tc], evXnFree.get(tc - 4))
                                e = sDVE.inc(v.tensor_scalar(
                                    xn_st[:, sl * H:(sl + 1) * H], x_db[:, sl * H:(sl + 1) * H],
                                    p0t[:, 0:1], rstd_all[:, tc:tc + 1],
                                    op0=ALU.subtract, op1=ALU.mult), 1)
                                evXnStage[tc] = e
                                evFreeX[tc] = e

                            @SY
                            def _(q, tc=tc, sl=sl):
                                if tc + 4 < NTC:
                                    wait(q, evFreeX[tc])
                                    evX[tc + 4] = (sXa if tc % 2 else sXb).inc(q.dma_start(
                                        out=x_db[:, sl * H:(sl + 1) * H],
                                        in_=x_in[(tc + 4) * 128:(tc + 5) * 128, :]), 16)

                            if tc < 4:
                                p0_transposes(tc)
                                evXnFree[tc] = (sPE, sPE.n)
                            if tc == 3:
                                evXnH0[0] = (sDVE, sDVE.n)



                    evKvpack = [None]
                    evP0stage = evXnStage[NTC - 1]
                    with ExitStack() as st1:
                        k_tm = st1.enter_context(nc.sbuf_tensor("k_tm", [128, NTC * KD], BF16))
                        v_tm = st1.enter_context(nc.sbuf_tensor("v_tm", [128, NTC * KD], BF16))
                        ksum = st1.enter_context(nc.sbuf_tensor("ksum", [128, NKV], F32))
                        with ExitStack() as st1b:
                            kst = st1b.enter_context(nc.sbuf_tensor("kst", [128, 1024], F32))
                            est = st1b.enter_context(nc.sbuf_tensor("est", [128, 1024], F32))
                            rst = st1b.enter_context(nc.sbuf_tensor("rst", [128, 1024], F32))
                            kphiK = st1b.enter_context(nc.sbuf_tensor("kphiK", [128, 1024], BF16))
                            kphiV = st1b.enter_context(nc.sbuf_tensor("kphiV", [128, 1024], BF16))
                            ksum_h = st1b.enter_context(nc.sbuf_tensor("ksum_h", [128, 2 * NKV], F32))
                            pkK = [st1b.enter_context(nc.psum_tensor("pkK%d" % i, [128, 512], F32))
                                   for i in range(2)]
                            pkV = [st1b.enter_context(nc.psum_tensor("pkV%d" % i, [128, 512], F32))
                                   for i in range(2)]
                            evKmm = {}
                            evTsK = {}
                            evActK = {}
                            evPhi = {}
                            wbuf = [wk_db, wv_db]
                            wdr = [wk, wv]

                            k_tm3 = k_tm[:, :].rearrange("p (lc f) -> p lc f", f=KD)
                            v_tm3 = v_tm[:, :].rearrange("p (lc f) -> p lc f", f=KD)

                            def p1_transposes(oc, half):
                                for kvi, src, dst3 in ((0, kphiK, k_tm3), (1, kphiV, v_tm3)):
                                    @TT
                                    def _(t, oc=oc, kvi=kvi, src=src, half=half):
                                        j = tr[0] % 4
                                        twait(t, evPhi[(kvi, oc, half)], evPtR[j])
                                        ins = None
                                        for j4 in range(4):
                                            ins = t.transpose(
                                                pt[j][:, j4 * 128:(j4 + 1) * 128],
                                                src[:, half * 512 + j4 * 128: half * 512 + (j4 + 1) * 128],
                                                ident[:, :])
                                        evPtW[j] = sPE.inc(ins, 1)
                                        slot_of[("p1g", kvi, oc, half)] = j
                                        tr[0] += 1

                                    @VV
                                    def _(v, oc=oc, kvi=kvi, half=half, dst3=dst3):
                                        j = slot_of[("p1g", kvi, oc, half)]
                                        wait(v, evPtW[j])
                                        evPtR[j] = sDVE.inc(v.tensor_copy(
                                            dst3[:, half * 4:(half + 1) * 4, oc * 128:(oc + 1) * 128],
                                            pt[j].rearrange("p (k f) -> p k f", f=128)[:, 0:4, :]), 1)

                            first_pe = [True]
                            lastK = {}
                            lastV = {}
                            evWkvA = {}

                            @GP
                            def _(g):
                                for oc in (0, 1):
                                    for kvi in (0, 1):
                                        evWkvA[(kvi, 0, oc)] = (ssem(oc) if kvi == 0 else ssem2(oc)).inc(
                                            g.dma_start(
                                                out=wbuf[kvi][:, (oc % 2) * H:(oc % 2 + 1) * H],
                                                in_=wdr[kvi][oc * 128:(oc + 1) * 128, :]), 16)

                            for half in (0, 1):
                                for oc in range(NKV):
                                    gidx = half * NKV + oc
                                    sl = gidx % 2

                                    if oc > 0:
                                        p1_transposes(oc - 1, half)
                                    elif half == 1:
                                        p1_transposes(NKV - 1, 0)
                                    if half == 0 and 4 <= oc <= 7:
                                        p0_transposes(oc)
                                        if oc == 7:
                                            evXnH1[0] = (sDVE, sDVE.n)

                                    @TT
                                    def _(t, oc=oc, sl=sl, half=half):
                                        bank = oc % 2
                                        twait(t, evWkvA[(0, half, oc)],
                                              evXnH0[0] if half == 0 else evXnH1[0],
                                              lastK.get(bank))
                                        last = None
                                        for hc in range(HC):
                                            last = t.matmul(
                                                pkK[bank][:, :],
                                                wk_db[:, sl * H + hc * 128: sl * H + (hc + 1) * 128],
                                                xn_fm[:, hc * T + half * 512: hc * T + half * 512 + 512],
                                                start=(hc == 0), stop=(hc == HC - 1))
                                        evKmm[(0, oc, half)] = sPE.inc(last, 1)

                                    @TT
                                    def _(t, oc=oc, sl=sl, half=half):
                                        bank = oc % 2
                                        twait(t, evWkvA[(1, half, oc)], lastV.get(bank))
                                        last = None
                                        for hc in range(HC):
                                            last = t.matmul(
                                                pkV[bank][:, :],
                                                wv_db[:, sl * H + hc * 128: sl * H + (hc + 1) * 128],
                                                xn_fm[:, hc * T + half * 512: hc * T + half * 512 + 512],
                                                start=(hc == 0), stop=(hc == HC - 1))
                                        evKmm[(1, oc, half)] = sPE.inc(last, 1)

                                    @VV
                                    def _(v, oc=oc, half=half):
                                        bank = oc % 2
                                        if half == 0 and oc == 0:
                                            wait(v, evP0stage)
                                        wait(v, evKmm[(0, oc, half)], evActK.get((0, oc - 2, half)))
                                        evTsK[(oc, half)] = sDVE.inc(v.tensor_scalar(
                                            kst[:, half * 512:(half + 1) * 512],
                                            pkK[bank][:, :],
                                            bk_sb[:, oc:oc + 1], 0.0,
                                            op0=ALU.add, op1=ALU.min), 1)

                                    @AA
                                    def _(a, oc=oc, half=half):
                                        bank = oc % 2
                                        wait(a, evTsK[(oc, half)],
                                             evPhi.get((0, oc - 2, half)))
                                        a.activation(out=est[:, half * 512:(half + 1) * 512],
                                                     in_=kst[:, half * 512:(half + 1) * 512],
                                                     func=AF.Exp)
                                        evActK[(0, oc, half)] = sACT.inc(a.activation(
                                            out=rst[:, half * 512:(half + 1) * 512],
                                            in_=pkK[bank][:, :],
                                            func=AF.Relu, bias=bk_sb[:, oc:oc + 1]), 1)
                                    lastK[oc % 2] = (sACT, sACT.n)

                                    @VV
                                    def _(v, oc=oc, half=half):
                                        wait(v, evActK[(0, oc, half)])
                                        evPhi[(0, oc, half)] = vch(v, v.tensor_add(
                                            kphiK[:, half * 512:(half + 1) * 512],
                                            rst[:, half * 512:(half + 1) * 512],
                                            est[:, half * 512:(half + 1) * 512]))
                                        vch(v, v.reduce_sum(
                                            ksum_h[:, half * NKV + oc: half * NKV + oc + 1],
                                            kphiK[:, half * 512:(half + 1) * 512],
                                            axis=AX.X))
                                        if half == 1:
                                            vch(v, v.tensor_add(
                                                ksum[:, oc:oc + 1],
                                                ksum_h[:, oc:oc + 1],
                                                ksum_h[:, NKV + oc:NKV + oc + 1]))

                                    @VV
                                    def _(v, oc=oc, half=half):
                                        bank = oc % 2
                                        wait(v, evKmm[(1, oc, half)])
                                        evPhi[(1, oc, half)] = sDVE.inc(v.tensor_scalar(
                                            kphiV[:, half * 512:(half + 1) * 512],
                                            pkV[bank][:, :],
                                            bv_sb[:, oc:oc + 1], None, op0=ALU.add), 1)
                                    lastV[oc % 2] = (sDVE, sDVE.n)

                                    @GP
                                    def _(g, oc=oc, half=half, gidx=gidx):
                                        if gidx + 2 < 2 * NKV:
                                            g2 = gidx + 2
                                            h2, oc2 = divmod(g2, NKV)
                                            for kvi in (0, 1):
                                                wait(g, evKmm[(kvi, oc, half)])
                                                evWkvA[(kvi, h2, oc2)] = (ssem(g2) if kvi == 0 else ssem2(g2)).inc(
                                                    g.dma_start(
                                                        out=wbuf[kvi][:, (g2 % 2) * H:(g2 % 2 + 1) * H],
                                                        in_=wdr[kvi][oc2 * 128:(oc2 + 1) * 128, :]), 16)

                            p1_transposes(NKV - 1, 1)

                            evKVtmDone = (sDVE, sDVE.n)

                        with nc.psum_tensor("pkv0", [128, 512], F32) as pkv0, \
                             nc.psum_tensor("pkv1", [128, 512], F32) as pkv1:
                            pkv = [pkv0, pkv1]
                            evKVp = [None]

                            @TT
                            def _(t):
                                twait(t, evKVtmDone)
                                last = None
                                for h in range(NKV):
                                    for lc in range(NTC):
                                        last = t.matmul(
                                            pkv[h // 4][:, (h % 4) * 128:(h % 4 + 1) * 128],
                                            k_tm[:, lc * KD + h * 128:lc * KD + (h + 1) * 128],
                                            v_tm[:, lc * KD + h * 128:lc * KD + (h + 1) * 128],
                                            start=(lc == 0), stop=(lc == NTC - 1))
                                evKVp[0] = sPE.inc(last, 1)

                            @VV
                            def _(v):
                                wait(v, evKVp[0])
                                v.tensor_copy(kvpack[:, 0:512], pkv0[:, :])
                                v.tensor_copy(kvpack[:, 512:1024], pkv1[:, :])
                                evKvpack[0] = sDVE.inc(v.tensor_copy(
                                    kvpack[:, 1024:1024 + NKV], ksum[:, :]), 1)

                            if dbg:
                                @SY
                                def _(q):
                                    wait(q, evXnH1[0])
                                    sSY.inc(q.dma_start(out=xn_dump, in_=xn_fm), 16)
                                    wait(q, evKVtmDone)
                                    sSY.inc(q.dma_start(out=ktm_dump, in_=k_tm[:, :]), 16)
                                    sSY.inc(q.dma_start(out=vtm_dump, in_=v_tm[:, :]), 16)

                @GP
                def _(g):
                    wait(g, evKvpack[0])
                    sW.inc(g.dma_start(out=cc_in, in_=kvpack[:, :]), 16)
                    g.wait_ge(sW.h, sW.n)
                    ins = g.collective_compute(
                        "AllReduce", ALU.add,
                        replica_groups=[list(range(NCORES))],
                        ins=[cc_in], outs=[cc_out])
                    ins.then_inc(sCC.h, 1)
                    sCC.n += 1

                evGate = {}
                evWgH = {}
                evWdH = {}
                evUp = {}
                evSilu = {}
                evGuEvac = {}
                evWg = {}
                evWu = {}
                evF = {}
                evFEvac = {}
                evFSt = {}
                evFRead = {}
                evAdd = {}
                evG = {}
                evGSt = {}
                evWd = {}

                with nc.sbuf_tensor("gu", [128, ICA * T], BF16) as gu_t:
                    gu = gu_t[:, :]

                    def gateup(mega):
                        icl = list(range(0, ICA)) if mega == 0 else list(range(ICA, ICN))
                        BB = barrier()
                        with ExitStack() as ctx:
                            wg_db = ctx.enter_context(nc.sbuf_tensor("wg_db%d" % mega, [128, 2 * H], BF16))
                            wu_db = ctx.enter_context(nc.sbuf_tensor("wu_db%d" % mega, [128, 2 * H], BF16))
                            sg_st = ctx.enter_context(nc.sbuf_tensor("sg_st%d" % mega, [128, T], F32))
                            pg = [ctx.enter_context(nc.psum_tensor("pg%d_%d" % (i, mega), [128, 1024], F32))
                                  for i in range(2)]
                            pu = [ctx.enter_context(nc.psum_tensor("pu%d_%d" % (i, mega), [128, 1024], F32))
                                  for i in range(2)]

                            @GP
                            def _(g):
                                wait(g, *BB)
                                ic0 = icl[0]
                                sl0 = ic0 % 2
                                evWgH[ic0] = ssem(ic0).inc(g.dma_start(
                                    out=wg_db[:, sl0 * H: sl0 * H + H // 2],
                                    in_=wg[ic0 * 128:(ic0 + 1) * 128, 0:H // 2]), 16)
                                evWg[ic0] = sW.inc(g.dma_start(
                                    out=wg_db[:, sl0 * H + H // 2: sl0 * H + H],
                                    in_=wg[ic0 * 128:(ic0 + 1) * 128, H // 2:H]), 16)
                                evWu[ic0] = ssem2(ic0).inc(g.dma_start(
                                    out=wu_db[:, sl0 * H:(sl0 + 1) * H],
                                    in_=wu[ic0 * 128:(ic0 + 1) * 128, :]), 16)
                                for ic in icl[1:2]:
                                    evWg[ic] = ssem(ic).inc(g.dma_start(
                                        out=wg_db[:, (ic % 2) * H:(ic % 2 + 1) * H],
                                        in_=wg[ic * 128:(ic + 1) * 128, :]), 16)
                                    evWu[ic] = ssem2(ic).inc(g.dma_start(
                                        out=wu_db[:, (ic % 2) * H:(ic % 2 + 1) * H],
                                        in_=wu[ic * 128:(ic + 1) * 128, :]), 16)

                            first = [True, True, True]
                            for idx, ic in enumerate(icl):
                                sl = ic % 2

                                @TT
                                def _(t, ic=ic, sl=sl, idx=idx):
                                    if first[0]:
                                        twait(t, *BB)
                                        first[0] = False
                                    if idx == 0:
                                        twait(t, evWgH[ic])
                                    else:
                                        twait(t, evWg[ic])
                                    if idx >= 2:
                                        prev = icl[idx - 2]
                                        twait(t, evSilu.get(prev), evGuEvac.get(prev))
                                    last = None
                                    for hc in range(HC):
                                        if idx == 0 and hc == HC // 2:
                                            twait(t, evWg[ic])
                                        for half in (0, 1):
                                            last = t.matmul(
                                                pg[sl][:, half * 512:(half + 1) * 512],
                                                wg_db[:, sl * H + hc * 128: sl * H + (hc + 1) * 128],
                                                xn_fm[:, hc * T + half * 512: hc * T + half * 512 + 512],
                                                start=(hc == 0), stop=(hc == HC - 1))
                                    evGate[ic] = sPE.inc(last, 1)
                                    twait(t, evWu[ic])
                                    for hc in range(HC):
                                        for half in (0, 1):
                                            last = t.matmul(
                                                pu[sl][:, half * 512:(half + 1) * 512],
                                                wu_db[:, sl * H + hc * 128: sl * H + (hc + 1) * 128],
                                                xn_fm[:, hc * T + half * 512: hc * T + half * 512 + 512],
                                                start=(hc == 0), stop=(hc == HC - 1))
                                    evUp[ic] = sPE.inc(last, 1)

                                @GP
                                def _(g, idx=idx):
                                    if idx + 2 < len(icl):
                                        ic2 = icl[idx + 2]
                                        wait(g, evGate[icl[idx]], evUp[icl[idx]])
                                        evWg[ic2] = ssem(ic2).inc(g.dma_start(
                                            out=wg_db[:, (ic2 % 2) * H:(ic2 % 2 + 1) * H],
                                            in_=wg[ic2 * 128:(ic2 + 1) * 128, :]), 16)
                                        evWu[ic2] = ssem2(ic2).inc(g.dma_start(
                                            out=wu_db[:, (ic2 % 2) * H:(ic2 % 2 + 1) * H],
                                            in_=wu[ic2 * 128:(ic2 + 1) * 128, :]), 16)

                                @AA
                                def _(a, ic=ic, sl=sl, idx=idx):
                                    if first[1]:
                                        wait(a, *BB)
                                        first[1] = False
                                    wait(a, evGate[ic])
                                    if idx >= 1:
                                        wait(a, evGuEvac.get(icl[idx - 1]))
                                    evSilu[ic] = sACT.inc(a.activation(
                                        out=sg_st[:, :], in_=pg[sl][:, :], func=AF.Silu), 1)

                                @VV
                                def _(v, ic=ic, sl=sl, idx=idx):
                                    if first[2]:
                                        wait(v, *BB)
                                        first[2] = False
                                    wait(v, evSilu[ic], evUp[ic])
                                    evGuEvac[ic] = sDVE.inc(v.tensor_mul(
                                        gu[:, idx * T:(idx + 1) * T], sg_st[:, :], pu[sl][:, :]), 1)

                    def down_phase(mega, g_write):
                        icl = list(range(0, ICA)) if mega == 0 else list(range(ICA, ICN))
                        nIC = len(icl)
                        CW = nIC * 128
                        lastGu = evGuEvac[icl[-1]]
                        BB = barrier()
                        with ExitStack() as ctx:
                            wd_db = ctx.enter_context(nc.sbuf_tensor("wd_db%d" % mega, [128, 2 * CW], BF16))
                            if g_write:
                                frd = ctx.enter_context(nc.sbuf_tensor("frd", [128, 2 * T], F32))
                                tmpf = ctx.enter_context(nc.sbuf_tensor("tmpf", [128, 2 * T], F32))
                                g_st = ctx.enter_context(nc.sbuf_tensor("g_st", [128, 2 * T], BF16))
                            else:
                                f_st = ctx.enter_context(nc.sbuf_tensor("f_st%d" % mega, [128, 2 * T], F32))
                            pf = [ctx.enter_context(nc.psum_tensor("pf%d_%d" % (i, mega), [128, 1024], F32))
                                  for i in range(4)]

                            @GP
                            def _(g):
                                wait(g, *BB)
                                cw2 = CW // 2
                                evWdH[mega] = ssem(0).inc(g.dma_start(
                                    out=wd_db[:, 0:cw2],
                                    in_=wd[0:128, mega * ICA * 128: mega * ICA * 128 + cw2]), 16)
                                evWd[(mega, 0)] = sW.inc(g.dma_start(
                                    out=wd_db[:, cw2:CW],
                                    in_=wd[0:128, mega * ICA * 128 + cw2: mega * ICA * 128 + CW]), 16)
                                evWd[(mega, 1)] = ssem(1).inc(g.dma_start(
                                    out=wd_db[:, CW:2 * CW],
                                    in_=wd[128:256, mega * ICA * 128: mega * ICA * 128 + CW]), 16)

                            if g_write:
                                @SY
                                def _(q):
                                    wait(q, *BB)
                                    for oc in (0, 1):
                                        wait(q, evFSt[oc])
                                        evFRead[oc] = ssem2(oc).inc(q.dma_start(
                                            out=frd[:, (oc % 2) * T:(oc % 2 + 1) * T],
                                            in_=f_sc[oc * 128:(oc + 1) * 128, :]), 16)

                            first = [True, True, True, True]
                            for oc in range(HC):
                                sl = oc % 2

                                @TT
                                def _(t, oc=oc, sl=sl):
                                    if first[0]:
                                        twait(t, *BB)
                                        first[0] = False
                                    if oc == 0:
                                        twait(t, evWdH[mega], lastGu)
                                    else:
                                        twait(t, evWd[(mega, oc)], lastGu)
                                    if g_write:
                                        twait(t, evAdd.get(oc - 4))
                                    else:
                                        twait(t, evFEvac.get(oc - 4))
                                    pj = pf[oc % 4]
                                    last = None
                                    for li in range(nIC):
                                        if oc == 0 and li == nIC // 2:
                                            twait(t, evWd[(mega, 0)])
                                        for half in (0, 1):
                                            last = t.matmul(
                                                pj[:, half * 512:(half + 1) * 512],
                                                wd_db[:, sl * CW + li * 128: sl * CW + (li + 1) * 128],
                                                gu[:, li * T + half * 512: li * T + half * 512 + 512],
                                                start=(li == 0), stop=(li == nIC - 1))
                                    evF[(mega, oc)] = sPE.inc(last, 1)

                                @GP
                                def _(g, oc=oc):
                                    if oc + 2 < HC:
                                        wait(g, evF[(mega, oc)])
                                        s2 = (oc + 2) % 2
                                        evWd[(mega, oc + 2)] = ssem(oc).inc(g.dma_start(
                                            out=wd_db[:, s2 * CW:(s2 + 1) * CW],
                                            in_=wd[(oc + 2) * 128:(oc + 3) * 128,
                                                   mega * ICA * 128: mega * ICA * 128 + CW]), 16)

                                if not g_write:
                                    @VV
                                    def _(v, oc=oc, sl=sl):
                                        if first[1]:
                                            wait(v, *BB)
                                            first[1] = False
                                        wait(v, evF[(0, oc)], evFSt.get(oc - 2))
                                        evFEvac[oc] = sDVE.inc(v.tensor_copy(
                                            f_st[:, sl * T:(sl + 1) * T], pf[oc % 4][:, :]), 1)

                                    @SY
                                    def _(q, oc=oc, sl=sl):
                                        if first[3]:
                                            wait(q, *BB)
                                            first[3] = False
                                        wait(q, evFEvac[oc])
                                        evFSt[oc] = sSY.inc(q.dma_start(
                                            out=f_sc[oc * 128:(oc + 1) * 128, :],
                                            in_=f_st[:, sl * T:(sl + 1) * T]), 16)
                                else:
                                    @VV
                                    def _(v, oc=oc, sl=sl):
                                        if first[1]:
                                            wait(v, *BB)
                                            first[1] = False
                                        wait(v, evF[(1, oc)], evFRead[oc], evG.get(oc - 2))
                                        evAdd[oc] = sDVE.inc(v.tensor_add(
                                            tmpf[:, sl * T:(sl + 1) * T],
                                            pf[oc % 4][:, :], frd[:, sl * T:(sl + 1) * T]), 1)

                                    @AA
                                    def _(a, oc=oc, sl=sl):
                                        if first[2]:
                                            wait(a, *BB)
                                            first[2] = False
                                        wait(a, evAdd[oc], evGSt.get(oc - 2))
                                        evG[oc] = sACT.inc(a.activation(
                                            out=g_st[:, sl * T:(sl + 1) * T],
                                            in_=tmpf[:, sl * T:(sl + 1) * T], func=AF.Silu), 1)

                                    @SY
                                    def _(q, oc=oc, sl=sl):
                                        if first[3]:
                                            wait(q, *BB)
                                            first[3] = False
                                        wait(q, evG[oc])
                                        evGSt[oc] = sSY.inc(q.dma_start(
                                            out=g_sc[oc * 128:(oc + 1) * 128, :],
                                            in_=g_st[:, sl * T:(sl + 1) * T]), 16)
                                        if oc + 2 < HC:
                                            wait(q, evAdd[oc])
                                            evFRead[oc + 2] = ssem2(oc).inc(q.dma_start(
                                                out=frd[:, sl * T:(sl + 1) * T],
                                                in_=f_sc[(oc + 2) * 128:(oc + 3) * 128, :]), 16)

                    gateup(0)
                    if dbg:
                        @SY
                        def _(q):
                            wait(q, evGuEvac[ICA - 1])
                            sSY.inc(q.dma_start(out=gu_dump, in_=gu_t[:, :]), 16)
                    down_phase(0, g_write=False)
                    gateup(1)
                    down_phase(1, g_write=True)

                evQmm = {}
                evWqH = {}
                evWoH = {}
                evTsQ = {}
                evActQ = {}
                evQbf = {}
                evWq = {}
                evKvg = [None]
                evRec = [None]
                evAmm = {}
                evAsc = {}
                evWo = {}
                evSmm = {}
                evP = {}
                evPSt = {}
                evGload = {}

                p_tm4 = nc.alloc_sbuf_tensor("p_tm4", [128, 5 * H], BF16, side="right")
                evPTw1 = [None]

                with nc.sbuf_tensor("q_bf", [128, NH * T], BF16) as q_bf_t:
                    q_bf = q_bf_t[:, :]
                    BQ = barrier()
                    with ExitStack() as ctq:
                        wq_db = ctq.enter_context(nc.sbuf_tensor("wq_db", [128, 2 * H], BF16))
                        qtmin = ctq.enter_context(nc.sbuf_tensor("qtmin", [128, 1024], F32))
                        qe = ctq.enter_context(nc.sbuf_tensor("qe", [128, 1024], F32))
                        qr = ctq.enter_context(nc.sbuf_tensor("qr", [128, 1024], F32))
                        pq = [ctq.enter_context(nc.psum_tensor("pq%d" % i, [128, 512], F32))
                              for i in range(4)]

                        @GP
                        def _(g):
                            wait(g, *BQ)
                            evWqH[0] = ssem(0).inc(g.dma_start(
                                out=wq_db[:, 0:H // 2],
                                in_=wq[0:128, 0:H // 2]), 16)
                            evWq[0] = sW.inc(g.dma_start(
                                out=wq_db[:, H // 2:H],
                                in_=wq[0:128, H // 2:H]), 16)
                            evWq[1] = ssem(1).inc(g.dma_start(
                                out=wq_db[:, H:2 * H],
                                in_=wq[128:256, :]), 16)

                        first = [True, True, True]
                        for oc in range(NH):
                            sl = oc % 2

                            @TT
                            def _(t, oc=oc, sl=sl):
                                if first[0]:
                                    twait(t, *BQ)
                                    first[0] = False
                                twait(t, evWqH[0] if oc == 0 else evWq[oc])
                                for b in (0, 1):
                                    twait(t, evActQ.get((oc - 2, b)))
                                lastmm = {}
                                for hc in range(HC):
                                    if oc == 0 and hc == HC // 2:
                                        twait(t, evWq[0])
                                    for half in (0, 1):
                                        bank = (oc % 2) * 2 + half
                                        lastmm[half] = t.matmul(
                                            pq[bank][:, :],
                                            wq_db[:, sl * H + hc * 128: sl * H + (hc + 1) * 128],
                                            xn_fm[:, hc * T + half * 512: hc * T + half * 512 + 512],
                                            start=(hc == 0), stop=(hc == HC - 1))
                                evQmm[(oc, 0)] = sPE.inc(lastmm[0], 1)
                                evQmm[(oc, 1)] = sPE.inc(lastmm[1], 1)

                            @GP
                            def _(g, oc=oc):
                                if oc + 2 < NH:
                                    wait(g, evQmm[(oc, 1)])
                                    s2 = (oc + 2) % 2
                                    evWq[oc + 2] = ssem(oc).inc(g.dma_start(
                                        out=wq_db[:, s2 * H:(s2 + 1) * H],
                                        in_=wq[(oc + 2) * 128:(oc + 3) * 128, :]), 16)

                            @VV
                            def _(v, oc=oc):
                                if first[1]:
                                    wait(v, *BQ)
                                    first[1] = False
                                for half in (0, 1):
                                    bank = (oc % 2) * 2 + half
                                    wait(v, evQmm[(oc, half)], evActQ.get((oc - 1, half)))
                                    evTsQ[(oc, half)] = sDVE.inc(v.tensor_scalar(
                                        qtmin[:, half * 512:(half + 1) * 512], pq[bank][:, :],
                                        bq_sb[:, oc:oc + 1], 0.0, op0=ALU.add, op1=ALU.min), 1)

                            @AA
                            def _(a, oc=oc):
                                if first[2]:
                                    wait(a, *BQ)
                                    first[2] = False
                                for half in (0, 1):
                                    bank = (oc % 2) * 2 + half
                                    wait(a, evTsQ[(oc, half)], evQbf.get((oc - 1, half)))
                                    a.activation(out=qe[:, half * 512:(half + 1) * 512],
                                                 in_=qtmin[:, half * 512:(half + 1) * 512],
                                                 func=AF.Exp)
                                    evActQ[(oc, half)] = sACT.inc(a.activation(
                                        out=qr[:, half * 512:(half + 1) * 512],
                                        in_=pq[bank][:, :], func=AF.Relu,
                                        bias=bq_sb[:, oc:oc + 1]), 1)

                            @VV
                            def _(v, oc=oc):
                                for half in (0, 1):
                                    wait(v, evActQ[(oc, half)])
                                    evQbf[(oc, half)] = sDVE.inc(v.tensor_add(
                                        q_bf[:, oc * T + half * 512: oc * T + half * 512 + 512],
                                        qr[:, half * 512:(half + 1) * 512],
                                        qe[:, half * 512:(half + 1) * 512]), 1)

                    @GP
                    def _(g):
                        g.wait_ge(sCC.h, 1)
                        evKvg[0] = sSC.inc(g.dma_start(out=kvg_sb[:, :], in_=cc_out), 16)

                    @VV
                    def _(v):
                        wait(v, evKvg[0])
                        v.tensor_copy(kv_bf[:, :], kvg_sb[:, 0:KD])
                        vch(v, v.tensor_single_scalar(kvg_sb[:, KD:KD + NKV],
                                                      kvg_sb[:, KD:KD + NKV],
                                                      EPS_ATT, op=ALU.add))
                        evRec[0] = vch(v, v.reciprocal(rec_sb[:, :], kvg_sb[:, KD:KD + NKV]))

                    if dbg:
                        @SY
                        def _(q):
                            wait(q, evRec[0])
                            sSY.inc(q.dma_start(out=kvg_dump, in_=kvg_sb[:, :]), 16)
                            wait(q, evKvpack[0])
                            sSY.inc(q.dma_start(out=cc_dump, in_=kvpack[:, :]), 16)

                    BA = barrier()
                    with ExitStack() as cta:
                        pa = [cta.enter_context(nc.psum_tensor("pa%d" % i, [128, 512], F32))
                              for i in range(4)]
                        first = [True, True]
                        for h in range(NH):
                            hk = h // 4

                            @TT
                            def _(t, h=h, hk=hk):
                                if first[0]:
                                    twait(t, *BA)
                                    first[0] = False
                                twait(t, evRec[0], evQbf.get((NH - 1, 1)))
                                for half in (0, 1):
                                    bank = (h % 2) * 2 + half
                                    twait(t, evAsc.get((h - 2, half)))
                                    evAmm[(h, half)] = sPE.inc(t.matmul(
                                        pa[bank][:, :],
                                        kv_bf[:, hk * 128:(hk + 1) * 128],
                                        q_bf[:, h * T + half * 512: h * T + half * 512 + 512],
                                        start=True, stop=True), 1)

                            @VV
                            def _(v, h=h, hk=hk):
                                if first[1]:
                                    wait(v, *BA)
                                    first[1] = False
                                for half in (0, 1):
                                    bank = (h % 2) * 2 + half
                                    wait(v, evAmm[(h, half)])
                                    evAsc[(h, half)] = sDVE.inc(v.tensor_scalar(
                                        q_bf[:, h * T + half * 512: h * T + half * 512 + 512],
                                        pa[bank][:, :], rec_sb[:, hk:hk + 1], None,
                                        op0=ALU.mult), 1)

                    if dbg:
                        @SY
                        def _(q):
                            wait(q, evAsc[(NH - 1, 1)])
                            sSY.inc(q.dma_start(out=q_dump, in_=q_bf), 16)

                    BO = barrier()
                    with ExitStack() as cto:
                        wo_db = cto.enter_context(nc.sbuf_tensor("wo_db", [128, 2 * H], BF16))
                        g_buf = cto.enter_context(nc.sbuf_tensor("g_buf", [128, 2 * T], BF16))
                        p_st = cto.enter_context(nc.sbuf_tensor("p_st", [128, 2 * T], BF16))
                        ps = [cto.enter_context(nc.psum_tensor("ps%d" % i, [128, 512], F32))
                              for i in range(4)]
                        pt2 = [cto.enter_context(nc.psum_tensor("pt2_%d" % i, [128, 384], BF16))
                               for i in range(2)]
                        ptm_ext3 = xn_fm_t[:, :].rearrange("p (s f) -> p s f", f=H)
                        evPtW2 = {}
                        evPtR2 = [None, None]

                        def wo_tp(oc):
                            sl2 = oc % 2

                            @TT
                            def _(t, oc=oc, sl2=sl2):
                                twait(t, evP[(oc, 1)], evPtR2[sl2])
                                ins = None
                                for k in range(3):
                                    ins = t.transpose(
                                        pt2[sl2][:, k * 128:(k + 1) * 128],
                                        p_st[:, sl2 * T + (5 + k) * 128: sl2 * T + (6 + k) * 128],
                                        ident[:, :])
                                evPtW2[oc] = sPE.inc(ins, 1)

                            @VV
                            def _(v, oc=oc, sl2=sl2):
                                wait(v, evPtW2[oc])
                                evPtR2[sl2] = sDVE.inc(v.tensor_copy(
                                    ptm_ext3[:, 0:3, oc * 128:(oc + 1) * 128],
                                    pt2[sl2].rearrange("p (k f) -> p k f", f=128)[:, 0:3, :]), 1)

                        @GP
                        def _(g):
                            wait(g, *BO)
                            evWoH[0] = ssem(0).inc(g.dma_start(
                                out=wo_db[:, 0:H // 2],
                                in_=wo[0:128, 0:H // 2]), 16)
                            evWo[0] = sW.inc(g.dma_start(
                                out=wo_db[:, H // 2:H],
                                in_=wo[0:128, H // 2:H]), 16)
                            evWo[1] = ssem(1).inc(g.dma_start(
                                out=wo_db[:, H:2 * H],
                                in_=wo[128:256, :]), 16)

                        @SY
                        def _(q):
                            wait(q, *BO)
                            for oc in (0, 1):
                                wait(q, evGSt[oc])
                                evGload[oc] = ssem2(oc).inc(q.dma_start(
                                    out=g_buf[:, (oc % 2) * T:(oc % 2 + 1) * T],
                                    in_=g_sc[oc * 128:(oc + 1) * 128, :]), 16)

                        first = [True, True, True]
                        for oc in range(HC):
                            sl = oc % 2

                            @TT
                            def _(t, oc=oc, sl=sl):
                                if first[0]:
                                    twait(t, *BO)
                                    first[0] = False
                                twait(t, evWoH[0] if oc == 0 else evWo[oc],
                                      evAsc.get((NH - 1, 1)))
                                for b in (0, 1):
                                    twait(t, evP.get((oc - 2, b)))
                                lastmm = {}
                                for vc in range(HC):
                                    if oc == 0 and vc == HC // 2:
                                        twait(t, evWo[0])
                                    for half in (0, 1):
                                        bank = (oc % 2) * 2 + half
                                        lastmm[half] = t.matmul(
                                            ps[bank][:, :],
                                            wo_db[:, sl * H + vc * 128: sl * H + (vc + 1) * 128],
                                            q_bf[:, vc * T + half * 512: vc * T + half * 512 + 512],
                                            start=(vc == 0), stop=(vc == HC - 1))
                                evSmm[(oc, 0)] = sPE.inc(lastmm[0], 1)
                                evSmm[(oc, 1)] = sPE.inc(lastmm[1], 1)

                            @GP
                            def _(g, oc=oc):
                                if oc + 2 < HC:
                                    wait(g, evSmm[(oc, 1)])
                                    s2 = (oc + 2) % 2
                                    evWo[oc + 2] = ssem(oc).inc(g.dma_start(
                                        out=wo_db[:, s2 * H:(s2 + 1) * H],
                                        in_=wo[(oc + 2) * 128:(oc + 3) * 128, :]), 16)

                            if oc >= 1:
                                wo_tp(oc - 1)

                            @VV
                            def _(v, oc=oc, sl=sl):
                                if first[1]:
                                    wait(v, *BO)
                                    first[1] = False
                                for half in (0, 1):
                                    bank = (oc % 2) * 2 + half
                                    wait(v, evSmm[(oc, half)], evGload[oc], evPSt.get(oc - 2),
                                         evPtW2.get(oc - 2))
                                    evP[(oc, half)] = sDVE.inc(v.scalar_tensor_tensor(
                                        p_st[:, sl * T + half * 512: sl * T + half * 512 + 512],
                                        ps[bank][:, :], bo_sb[:, oc:oc + 1],
                                        g_buf[:, sl * T + half * 512: sl * T + half * 512 + 512],
                                        op0=ALU.add, op1=ALU.mult), 1)

                            @SY
                            def _(q, oc=oc, sl=sl):
                                if first[2]:
                                    wait(q, *BO)
                                    first[2] = False
                                wait(q, evP[(oc, 1)])
                                evPSt[oc] = sSY.inc(q.dma_start(
                                    out=p_sc[oc * 128:(oc + 1) * 128, 0:640],
                                    in_=p_st[:, sl * T: sl * T + 640]), 16)
                                if oc + 2 < HC:
                                    wait(q, evGSt.get(oc + 2))
                                    s2 = (oc + 2) % 2
                                    evGload[oc + 2] = ssem2(oc).inc(q.dma_start(
                                        out=g_buf[:, s2 * T:(s2 + 1) * T],
                                        in_=g_sc[(oc + 2) * 128:(oc + 3) * 128, :]), 16)

                            @AA
                            def _(a, oc=oc):
                                if oc % 4 == 3:
                                    wait(a, evPSt[oc])
                                    lo = (oc - 3) * 128
                                    hi = (oc + 1) * 128
                                    for tc4 in range(5):
                                        sSC.inc(a.dma_start(
                                            out=p_tm4[:, tc4 * H + lo: tc4 * H + hi],
                                            in_=p_sc[lo:hi, tc4 * 128:(tc4 + 1) * 128],
                                            transpose=True), 16)
                                    if oc == HC - 1:
                                        evPTw1[0] = (sSC, sSC.n)

                        wo_tp(HC - 1)
                        evPTX = (sDVE, sDVE.n)

                BE2 = [(s, s.n) for s in ALLSEMS if s is not sSC]
                evPTw1v = evPTw1[0]
                ORDER = [5, 6, 7, 0, 1, 2, 3, 4]
                with ExitStack() as cte:
                    x2 = cte.enter_context(nc.sbuf_tensor("x2", [128, 2 * H], F32))
                    outb = cte.enter_context(nc.sbuf_tensor("outb", [128, 2 * H], F32))
                    evX2 = {}
                    evEpi = {}
                    evYSt = {}
                    evYStI = {}

                    @SY
                    def _(q):
                        wait(q, *BE2)

                    @VV
                    def _(v):
                        wait(v, *BE2)

                    @GP
                    def _(g):
                        wait(g, *BE2)
                        for idx in (0, 1):
                            tc = ORDER[idx]
                            evX2[tc] = sAp.inc(g.dma_start(
                                out=x2[:, (idx % 2) * H:(idx % 2 + 1) * H],
                                in_=x_in[tc * 128:(tc + 1) * 128, :]), 16)

                    for idx, tc in enumerate(ORDER):
                        sl = idx % 2

                        @VV
                        def _(v, tc=tc, sl=sl, idx=idx):
                            src = (p_tm4[:, tc * H:(tc + 1) * H] if tc < 5
                                   else xn_fm[:, (tc - 5) * H:(tc - 4) * H])
                            wait(v, evX2[tc], evPTw1v if tc < 5 else evPTX,
                                 evYStI.get(idx - 2))
                            evEpi[tc] = sDVE.inc(v.scalar_tensor_tensor(
                                outb[:, sl * H:(sl + 1) * H], src, 1.0,
                                x2[:, sl * H:(sl + 1) * H],
                                op0=ALU.mult, op1=ALU.add), 1)

                        @(SY if idx % 2 == 0 else AA)
                        def _(q, tc=tc, sl=sl, idx=idx):
                            wait(q, evEpi[tc])
                            ev = (sYa if idx % 2 == 0 else sXNb).inc(q.dma_start(
                                out=y_out[tc * 128:(tc + 1) * 128, :],
                                in_=outb[:, sl * H:(sl + 1) * H]), 16)
                            evYSt[tc] = ev
                            evYStI[idx] = ev

                        if idx + 2 < NTC:
                            @GP
                            def _(g, tc=tc, idx=idx):
                                wait(g, evEpi[tc])
                                tc2 = ORDER[idx + 2]
                                evX2[tc2] = sAp.inc(g.dma_start(
                                    out=x2[:, (idx % 2) * H:(idx % 2 + 1) * H],
                                    in_=x_in[tc2 * 128:(tc2 + 1) * 128, :]), 16)

                    @SY
                    def _(q):
                        wait(q, evYSt[3], evYSt[4])

    return nc



def _wtile(w):
    O, Hin = w.shape
    wt = np.ascontiguousarray(w.T)
    wt = wt.reshape(Hin // 128, 128, O // 128, 128)
    wt = wt.transpose(2, 1, 0, 3)
    return np.ascontiguousarray(wt).reshape(O, Hin)


def _btile(b):
    return np.ascontiguousarray(np.asarray(b, np.float32).reshape(-1, 128).T)


def kernel(L0_attn_hidden_states, L0_attn_Wq, L0_attn_bq, L0_attn_Wk, L0_attn_bk,
           L0_attn_Wv, L0_attn_bv, L0_attn_Wo, L0_attn_bo,
           L0_ffn_w_gate, L0_ffn_w_up, L0_ffn_w_down):
    x = np.ascontiguousarray(np.asarray(L0_attn_hidden_states, np.float32).reshape(8192, 4096))
    w_host = {
        "wq": _wtile(np.asarray(L0_attn_Wq, np.float32)),
        "wk": _wtile(np.asarray(L0_attn_Wk, np.float32)),
        "wv": _wtile(np.asarray(L0_attn_Wv, np.float32)),
        "wo": _wtile(np.asarray(L0_attn_Wo, np.float32)),
        "wg": _wtile(np.asarray(L0_ffn_w_gate, np.float32)),
        "wu": _wtile(np.asarray(L0_ffn_w_up, np.float32)),
        "wd": _wtile(np.asarray(L0_ffn_w_down, np.float32)),
        "bq": _btile(L0_attn_bq),
        "bk": _btile(L0_attn_bk),
        "bv": _btile(L0_attn_bv),
        "bo": _btile(L0_attn_bo),
        "ident_in": np.eye(128, dtype=np.float32),
    }
    nc = build_nc(dbg=DBG)
    in_maps = []
    for c in range(NCORES):
        m = {"x": np.ascontiguousarray(x[c * T:(c + 1) * T])}
        m.update(w_host)
        in_maps.append(m)
    res = run_bass_kernel_spmd(nc, in_maps, core_ids=list(range(NCORES)))
    out = np.concatenate([r["y"] for r in res.results], axis=0)
    if DBG:
        kernel.last_results = res.results
    return out.reshape(1, 8192, 4096)


if __name__ == "__main__":
    import time
    t0 = time.time()
    build_nc()
    print("build ok in %.1fs" % (time.time() - t0))


# version 29
# speedup vs baseline: 1.0176x; 1.0039x over previous
import numpy as np
import concourse.bass as bass
import concourse.mybir as mybir
import concourse.bass_utils as _bu
from concourse.bass_utils import run_bass_kernel_spmd
from contextlib import ExitStack

LDWOPT = True

if LDWOPT and not getattr(_bu, "_ldwopt_patched", False):
    _orig_run_command = _bu.run_command

    def _run_command_ldwopt(argv, **kw):
        argv = ["--enable-ldw-opt=true" if a == "--enable-ldw-opt=false" else a
                for a in argv]
        return _orig_run_command(argv, **kw)

    _bu.run_command = _run_command_ldwopt
    _bu._ldwopt_patched = True

F32 = mybir.dt.float32
BF16 = mybir.dt.bfloat16
AF = mybir.ActivationFunctionType
ALU = mybir.AluOpType
AX = mybir.AxisListType

NCORES = 8
T = 1024
H = 4096
HC = H // 128
NH, NKV, HD = 32, 8, 128
KD = NKV * HD
II = 11008
ICN = II // 128
ICA = 43
NTC = T // 128
EPS_LN = 1e-5
EPS_ATT = 1e-10

DBG = False


class Sem:
    __slots__ = ("h", "n")

    def __init__(self, h):
        self.h = h
        self.n = 0

    def inc(self, instr, k=1):
        instr.then_inc(self.h, k)
        self.n += k
        return (self, self.n)


def wait(eng, *evts):
    best = {}
    for e in evts:
        if e is None:
            continue
        s, v = e
        if v > 0 and v > best.get(id(s), (s, 0))[1]:
            best[id(s)] = (s, v)
    for s, v in best.values():
        eng.wait_ge(s.h, v)


def build_nc(dbg=False):
    nc = bass.Bass("TRN2", target_bir_lowering=False)

    x_in = nc.dram_tensor("x", [T, H], F32, kind="ExternalInput").ap()
    wq = nc.dram_tensor("wq", [H, H], F32, kind="ExternalInput").ap()
    wk = nc.dram_tensor("wk", [KD, H], F32, kind="ExternalInput").ap()
    wv = nc.dram_tensor("wv", [KD, H], F32, kind="ExternalInput").ap()
    wo = nc.dram_tensor("wo", [H, H], F32, kind="ExternalInput").ap()
    wg = nc.dram_tensor("wg", [II, H], F32, kind="ExternalInput").ap()
    wu = nc.dram_tensor("wu", [II, H], F32, kind="ExternalInput").ap()
    wd = nc.dram_tensor("wd", [H, II], F32, kind="ExternalInput").ap()
    bq = nc.dram_tensor("bq", [128, HC], F32, kind="ExternalInput").ap()
    bk = nc.dram_tensor("bk", [128, NKV], F32, kind="ExternalInput").ap()
    bv = nc.dram_tensor("bv", [128, NKV], F32, kind="ExternalInput").ap()
    bo = nc.dram_tensor("bo", [128, HC], F32, kind="ExternalInput").ap()
    ident_in = nc.dram_tensor("ident_in", [128, 128], F32, kind="ExternalInput").ap()
    y_out = nc.dram_tensor("y", [T, H], F32, kind="ExternalOutput").ap()

    sck = "ExternalOutput" if dbg else "Internal"
    f_sc = nc.dram_tensor("f_sc", [H, T], F32, kind=sck).ap()
    g_sc = nc.dram_tensor("g_sc", [H, T], BF16, kind=sck).ap()
    p_sc = nc.dram_tensor("p_sc", [H, T], BF16, kind=sck).ap()
    xn_sc = nc.dram_tensor("xn_sc", [T, H], BF16, kind="Internal").ap()
    cc_in = nc.dram_tensor("cc_in", [128, KD + NKV], F32, kind="Internal").ap()
    cc_out = nc.dram_tensor("cc_out", [128, KD + NKV], F32, kind="Internal").ap()
    if dbg:
        cc_dump = nc.dram_tensor("cc_dump", [128, KD + NKV], F32, kind="ExternalOutput").ap()
        gu_dump = nc.dram_tensor("gu_dump", [128, ICA * T], BF16, kind="ExternalOutput").ap()
        xn_dump = nc.dram_tensor("xn_dump", [128, HC * T], BF16, kind="ExternalOutput").ap()
        q_dump = nc.dram_tensor("q_dump", [128, NH * T], BF16, kind="ExternalOutput").ap()
        ktm_dump = nc.dram_tensor("ktm_dump", [128, NTC * KD], BF16, kind="ExternalOutput").ap()
        vtm_dump = nc.dram_tensor("vtm_dump", [128, NTC * KD], BF16, kind="ExternalOutput").ap()
        kvg_dump = nc.dram_tensor("kvg_dump", [128, KD + NKV], F32, kind="ExternalOutput").ap()

    with ExitStack() as stack:
        sb_outer = lambda name, shape, dt: stack.enter_context(nc.sbuf_tensor(name, shape, dt))
        ident = sb_outer("ident", [128, 128], BF16)
        bq_sb = sb_outer("bq_sb", [128, HC], F32)
        bk_sb = sb_outer("bk_sb", [128, NKV], F32)
        bv_sb = sb_outer("bv_sb", [128, NKV], F32)
        bo_sb = sb_outer("bo_sb", [128, HC], F32)
        kvg_sb = sb_outer("kvg_sb", [128, KD + NKV], F32)
        kv_bf = sb_outer("kv_bf", [128, KD], BF16)
        rec_sb = sb_outer("rec_sb", [128, NKV], F32)
        kvpack = sb_outer("kvpack", [128, KD + NKV], F32)
        sb_outer("cachebust_ldwopt%d" % int(LDWOPT), [128, 1], F32)

        sW = Sem(stack.enter_context(nc.semaphore("sW")))
        sSY = Sem(stack.enter_context(nc.semaphore("sSY")))
        sPE = Sem(stack.enter_context(nc.semaphore("sPE")))
        sDVE = Sem(stack.enter_context(nc.semaphore("sDVE")))
        sACT = Sem(stack.enter_context(nc.semaphore("sACT")))
        sPL = Sem(stack.enter_context(nc.semaphore("sPL")))
        sCC = Sem(stack.enter_context(nc.semaphore("sCC")))
        sSC = Sem(stack.enter_context(nc.semaphore("sSC")))
        sAp = Sem(stack.enter_context(nc.semaphore("sAp")))
        sBp = Sem(stack.enter_context(nc.semaphore("sBp")))
        sCp = Sem(stack.enter_context(nc.semaphore("sCp")))

        sDp = Sem(stack.enter_context(nc.semaphore("sDp")))
        sYa = Sem(stack.enter_context(nc.semaphore("sYa")))
        sYb = Sem(stack.enter_context(nc.semaphore("sYb")))
        sXNa = Sem(stack.enter_context(nc.semaphore("sXNa")))
        sXNb = Sem(stack.enter_context(nc.semaphore("sXNb")))
        sXa = Sem(stack.enter_context(nc.semaphore("sXa")))
        sXb = Sem(stack.enter_context(nc.semaphore("sXb")))

        def ssem(i):
            return sAp if i % 2 == 0 else sBp

        def ssem2(i):
            return sCp if i % 2 == 0 else sDp
        ALLSEMS = [sW, sSY, sPE, sDVE, sACT, sPL, sSC, sAp, sBp, sCp, sDp, sYa, sYb,
                   sXNa, sXNb, sXa, sXb]

        def barrier():
            return [(s, s.n) for s in ALLSEMS]

        def vch(v, ins):
            e = sDVE.inc(ins, 1)
            v.wait_ge(sDVE.h, e[1])
            return e

        def ach(a, ins):
            e = sACT.inc(ins, 1)
            a.wait_ge(sACT.h, e[1])
            return e

        def twait(t, *evts):
            wait(t, *evts)
            t.nop(nofuse=True)

        with nc.Block() as block:
            TT, VV, AA, SY, GP = block.tensor, block.vector, block.scalar, block.sync, block.gpsimd

            evIdent = [None]

            @GP
            def _(g):
                evIdent[0] = sPL.inc(g.dma_start(out=ident[:, :], in_=ident_in), 16)
                sW.inc(g.dma_start(out=bq_sb[:, :], in_=bq), 16)
                sW.inc(g.dma_start(out=bk_sb[:, :], in_=bk), 16)
                sW.inc(g.dma_start(out=bv_sb[:, :], in_=bv), 16)
                sW.inc(g.dma_start(out=bo_sb[:, :], in_=bo), 16)

            with nc.sbuf_tensor("xn_fm", [128, HC * T], BF16) as xn_fm_t:
                xn_fm = xn_fm_t[:, :]
                evXnDone = [None]
                evXnH0 = [None]
                evXnH1 = [None]
                evXnFree = {}
                slot_of = {}
                evPtW = [None] * 4
                evPtR = [None] * 4
                tr = [0]

                with ExitStack() as st_tr:
                    pt = [st_tr.enter_context(nc.psum_tensor("pt%d" % i, [128, 1024], BF16))
                          for i in range(4)]
                    xn_st = st_tr.enter_context(nc.sbuf_tensor("xn_st", [128, 4 * H], BF16))
                    wk_db = st_tr.enter_context(nc.sbuf_tensor("wk_db", [128, 2 * H], BF16))
                    wv_db = st_tr.enter_context(nc.sbuf_tensor("wv_db", [128, 2 * H], BF16))

                    with ExitStack() as st0:
                        st0_keep = []
                        x_db = st0.enter_context(nc.sbuf_tensor("x_db", [128, 4 * H], F32))
                        bnst = st0.enter_context(nc.sbuf_tensor("bnst", [128, 48], F32))
                        p0t = st0.enter_context(nc.sbuf_tensor("p0t", [128, 8], F32))
                        rstd_all = st0.enter_context(nc.sbuf_tensor("rstd_all", [128, NTC], F32))
                        evX = {}
                        evFreeX = {}
                        evXsq = {}
                        evVar = {}
                        evRstd = {}
                        evXnStage = {}

                        xn_fm3 = xn_fm_t[:, :].rearrange("p (hcx tt) -> p hcx tt", tt=T)

                        def p0_transposes(tc):
                            sl = tc % 4
                            for gq in range(8):
                                @TT
                                def _(t, tc=tc, sl=sl, gq=gq):
                                    j = tr[0] % 4
                                    twait(t, evIdent[0], evXnStage[tc], evPtR[j])
                                    ins = None
                                    for k in range(4):
                                        hc = gq * 4 + k
                                        ins = t.transpose(
                                            pt[j][:, k * 128:(k + 1) * 128],
                                            xn_st[:, sl * H + hc * 128: sl * H + (hc + 1) * 128],
                                            ident[:, :])
                                    evPtW[j] = sPE.inc(ins, 1)
                                    slot_of[("p0g", tc, gq)] = j
                                    tr[0] += 1

                                @VV
                                def _(v, tc=tc, gq=gq):
                                    j = slot_of[("p0g", tc, gq)]
                                    wait(v, evPtW[j])
                                    evPtR[j] = sDVE.inc(v.tensor_copy(
                                        xn_fm3[:, gq * 4:(gq + 1) * 4, tc * 128:(tc + 1) * 128],
                                        pt[j].rearrange("p (k f) -> p k f", f=128)[:, 0:4, :]), 1)

                        @SY
                        def _(q):
                            evX[0] = sXa.inc(q.dma_start(
                                out=x_db[:, 0:H], in_=x_in[0:128, :]), 16)
                            evX[3] = sYb.inc(q.dma_start(
                                out=x_db[:, 3 * H:4 * H], in_=x_in[384:512, :]), 16)

                        @AA
                        def _(a):
                            evX[1] = sXNa.inc(a.dma_start(
                                out=x_db[:, H:2 * H], in_=x_in[128:256, :]), 16)
                            evX[2] = sYa.inc(a.dma_start(
                                out=x_db[:, 2 * H:3 * H], in_=x_in[256:384, :]), 16)

                        for tc in range(NTC):
                            sl = tc % 4

                            @VV
                            def _(v, tc=tc, sl=sl):
                                wait(v, evX[tc])
                                for gq in range(8):
                                    ins = v.bn_stats(bnst[:, gq * 6:(gq + 1) * 6],
                                                     x_db[:, sl * H + gq * 512: sl * H + (gq + 1) * 512])
                                vch(v, ins)
                                vch(v, v.bn_aggr(p0t[:, 0:2], bnst[:, :]))
                                evVar[tc] = vch(v, v.tensor_single_scalar(
                                    p0t[:, 2:3], p0t[:, 1:2], EPS_LN, op=ALU.add))

                            @AA
                            def _(a, tc=tc):
                                wait(a, evVar[tc])
                                ach(a, a.activation(out=p0t[:, 3:4], in_=p0t[:, 2:3], func=AF.Ln))
                                evRstd[tc] = sACT.inc(a.activation(
                                    out=rstd_all[:, tc:tc + 1], in_=p0t[:, 3:4],
                                    func=AF.Exp, scale=-0.5), 1)

                            @VV
                            def _(v, tc=tc, sl=sl):
                                wait(v, evRstd[tc], evXnFree.get(tc - 4))
                                e = sDVE.inc(v.tensor_scalar(
                                    xn_st[:, sl * H:(sl + 1) * H], x_db[:, sl * H:(sl + 1) * H],
                                    p0t[:, 0:1], rstd_all[:, tc:tc + 1],
                                    op0=ALU.subtract, op1=ALU.mult), 1)
                                evXnStage[tc] = e
                                evFreeX[tc] = e

                            @SY
                            def _(q, tc=tc, sl=sl):
                                if tc + 4 < NTC:
                                    wait(q, evFreeX[tc])
                                    evX[tc + 4] = (sXa if tc % 2 else sXb).inc(q.dma_start(
                                        out=x_db[:, sl * H:(sl + 1) * H],
                                        in_=x_in[(tc + 4) * 128:(tc + 5) * 128, :]), 16)

                            if tc < 4:
                                p0_transposes(tc)
                                evXnFree[tc] = (sPE, sPE.n)
                            if tc == 3:
                                evXnH0[0] = (sDVE, sDVE.n)



                    evKvpack = [None]
                    evP0stage = evXnStage[NTC - 1]
                    with ExitStack() as st1:
                        k_tm = st1.enter_context(nc.sbuf_tensor("k_tm", [128, NTC * KD], BF16))
                        v_tm = st1.enter_context(nc.sbuf_tensor("v_tm", [128, NTC * KD], BF16))
                        ksum = st1.enter_context(nc.sbuf_tensor("ksum", [128, NKV], F32))
                        with ExitStack() as st1b:
                            kst = st1b.enter_context(nc.sbuf_tensor("kst", [128, 1024], F32))
                            est = st1b.enter_context(nc.sbuf_tensor("est", [128, 1024], F32))
                            rst = st1b.enter_context(nc.sbuf_tensor("rst", [128, 1024], F32))
                            kphiK = st1b.enter_context(nc.sbuf_tensor("kphiK", [128, 1024], BF16))
                            kphiV = st1b.enter_context(nc.sbuf_tensor("kphiV", [128, 1024], BF16))
                            ksum_h = st1b.enter_context(nc.sbuf_tensor("ksum_h", [128, 2 * NKV], F32))
                            pkK = [st1b.enter_context(nc.psum_tensor("pkK%d" % i, [128, 512], F32))
                                   for i in range(2)]
                            pkV = [st1b.enter_context(nc.psum_tensor("pkV%d" % i, [128, 512], F32))
                                   for i in range(2)]
                            evKmm = {}
                            evTsK = {}
                            evActK = {}
                            evPhi = {}
                            wbuf = [wk_db, wv_db]
                            wdr = [wk, wv]

                            k_tm3 = k_tm[:, :].rearrange("p (lc f) -> p lc f", f=KD)
                            v_tm3 = v_tm[:, :].rearrange("p (lc f) -> p lc f", f=KD)

                            def p1_transposes(oc, half):
                                for kvi, src, dst3 in ((0, kphiK, k_tm3), (1, kphiV, v_tm3)):
                                    @TT
                                    def _(t, oc=oc, kvi=kvi, src=src, half=half):
                                        j = tr[0] % 4
                                        twait(t, evPhi[(kvi, oc, half)], evPtR[j])
                                        ins = None
                                        for j4 in range(4):
                                            ins = t.transpose(
                                                pt[j][:, j4 * 128:(j4 + 1) * 128],
                                                src[:, half * 512 + j4 * 128: half * 512 + (j4 + 1) * 128],
                                                ident[:, :])
                                        evPtW[j] = sPE.inc(ins, 1)
                                        slot_of[("p1g", kvi, oc, half)] = j
                                        tr[0] += 1

                                    @VV
                                    def _(v, oc=oc, kvi=kvi, half=half, dst3=dst3):
                                        j = slot_of[("p1g", kvi, oc, half)]
                                        wait(v, evPtW[j])
                                        evPtR[j] = sDVE.inc(v.tensor_copy(
                                            dst3[:, half * 4:(half + 1) * 4, oc * 128:(oc + 1) * 128],
                                            pt[j].rearrange("p (k f) -> p k f", f=128)[:, 0:4, :]), 1)

                            first_pe = [True]
                            lastK = {}
                            lastV = {}
                            evWkvA = {}

                            @GP
                            def _(g):
                                for oc in (0, 1):
                                    for kvi in (0, 1):
                                        evWkvA[(kvi, 0, oc)] = (ssem(oc) if kvi == 0 else ssem2(oc)).inc(
                                            g.dma_start(
                                                out=wbuf[kvi][:, (oc % 2) * H:(oc % 2 + 1) * H],
                                                in_=wdr[kvi][oc * 128:(oc + 1) * 128, :]), 16)

                            for half in (0, 1):
                                for oc in range(NKV):
                                    gidx = half * NKV + oc
                                    sl = gidx % 2

                                    if oc > 0:
                                        p1_transposes(oc - 1, half)
                                    elif half == 1:
                                        p1_transposes(NKV - 1, 0)
                                    if half == 0 and 4 <= oc <= 7:
                                        p0_transposes(oc)
                                        if oc == 7:
                                            evXnH1[0] = (sDVE, sDVE.n)

                                    @TT
                                    def _(t, oc=oc, sl=sl, half=half):
                                        bank = oc % 2
                                        twait(t, evWkvA[(0, half, oc)],
                                              evXnH0[0] if half == 0 else evXnH1[0],
                                              lastK.get(bank))
                                        last = None
                                        for hc in range(HC):
                                            last = t.matmul(
                                                pkK[bank][:, :],
                                                wk_db[:, sl * H + hc * 128: sl * H + (hc + 1) * 128],
                                                xn_fm[:, hc * T + half * 512: hc * T + half * 512 + 512],
                                                start=(hc == 0), stop=(hc == HC - 1))
                                        evKmm[(0, oc, half)] = sPE.inc(last, 1)

                                    @TT
                                    def _(t, oc=oc, sl=sl, half=half):
                                        bank = oc % 2
                                        twait(t, evWkvA[(1, half, oc)], lastV.get(bank))
                                        last = None
                                        for hc in range(HC):
                                            last = t.matmul(
                                                pkV[bank][:, :],
                                                wv_db[:, sl * H + hc * 128: sl * H + (hc + 1) * 128],
                                                xn_fm[:, hc * T + half * 512: hc * T + half * 512 + 512],
                                                start=(hc == 0), stop=(hc == HC - 1))
                                        evKmm[(1, oc, half)] = sPE.inc(last, 1)

                                    @VV
                                    def _(v, oc=oc, half=half):
                                        bank = oc % 2
                                        if half == 0 and oc == 0:
                                            wait(v, evP0stage)
                                        wait(v, evKmm[(0, oc, half)], evActK.get((0, oc - 2, half)))
                                        evTsK[(oc, half)] = sDVE.inc(v.tensor_scalar(
                                            kst[:, half * 512:(half + 1) * 512],
                                            pkK[bank][:, :],
                                            bk_sb[:, oc:oc + 1], 0.0,
                                            op0=ALU.add, op1=ALU.min), 1)

                                    @AA
                                    def _(a, oc=oc, half=half):
                                        bank = oc % 2
                                        wait(a, evTsK[(oc, half)],
                                             evPhi.get((0, oc - 2, half)))
                                        a.activation(out=est[:, half * 512:(half + 1) * 512],
                                                     in_=kst[:, half * 512:(half + 1) * 512],
                                                     func=AF.Exp)
                                        evActK[(0, oc, half)] = sACT.inc(a.activation(
                                            out=rst[:, half * 512:(half + 1) * 512],
                                            in_=pkK[bank][:, :],
                                            func=AF.Relu, bias=bk_sb[:, oc:oc + 1]), 1)
                                    lastK[oc % 2] = (sACT, sACT.n)

                                    @VV
                                    def _(v, oc=oc, half=half):
                                        wait(v, evActK[(0, oc, half)])
                                        evPhi[(0, oc, half)] = vch(v, v.tensor_add(
                                            kphiK[:, half * 512:(half + 1) * 512],
                                            rst[:, half * 512:(half + 1) * 512],
                                            est[:, half * 512:(half + 1) * 512]))
                                        vch(v, v.reduce_sum(
                                            ksum_h[:, half * NKV + oc: half * NKV + oc + 1],
                                            kphiK[:, half * 512:(half + 1) * 512],
                                            axis=AX.X))
                                        if half == 1:
                                            vch(v, v.tensor_add(
                                                ksum[:, oc:oc + 1],
                                                ksum_h[:, oc:oc + 1],
                                                ksum_h[:, NKV + oc:NKV + oc + 1]))

                                    @VV
                                    def _(v, oc=oc, half=half):
                                        bank = oc % 2
                                        wait(v, evKmm[(1, oc, half)])
                                        evPhi[(1, oc, half)] = sDVE.inc(v.tensor_scalar(
                                            kphiV[:, half * 512:(half + 1) * 512],
                                            pkV[bank][:, :],
                                            bv_sb[:, oc:oc + 1], None, op0=ALU.add), 1)
                                    lastV[oc % 2] = (sDVE, sDVE.n)

                                    @GP
                                    def _(g, oc=oc, half=half, gidx=gidx):
                                        if gidx + 2 < 2 * NKV:
                                            g2 = gidx + 2
                                            h2, oc2 = divmod(g2, NKV)
                                            for kvi in (0, 1):
                                                wait(g, evKmm[(kvi, oc, half)])
                                                evWkvA[(kvi, h2, oc2)] = (ssem(g2) if kvi == 0 else ssem2(g2)).inc(
                                                    g.dma_start(
                                                        out=wbuf[kvi][:, (g2 % 2) * H:(g2 % 2 + 1) * H],
                                                        in_=wdr[kvi][oc2 * 128:(oc2 + 1) * 128, :]), 16)

                            p1_transposes(NKV - 1, 1)

                            evKVtmDone = (sDVE, sDVE.n)

                        with nc.psum_tensor("pkv0", [128, 512], F32) as pkv0, \
                             nc.psum_tensor("pkv1", [128, 512], F32) as pkv1:
                            pkv = [pkv0, pkv1]
                            evKVp = [None]

                            @TT
                            def _(t):
                                twait(t, evKVtmDone)
                                last = None
                                for h in range(NKV):
                                    for lc in range(NTC):
                                        last = t.matmul(
                                            pkv[h // 4][:, (h % 4) * 128:(h % 4 + 1) * 128],
                                            k_tm[:, lc * KD + h * 128:lc * KD + (h + 1) * 128],
                                            v_tm[:, lc * KD + h * 128:lc * KD + (h + 1) * 128],
                                            start=(lc == 0), stop=(lc == NTC - 1))
                                evKVp[0] = sPE.inc(last, 1)

                            @VV
                            def _(v):
                                wait(v, evKVp[0])
                                v.tensor_copy(kvpack[:, 0:512], pkv0[:, :])
                                v.tensor_copy(kvpack[:, 512:1024], pkv1[:, :])
                                evKvpack[0] = sDVE.inc(v.tensor_copy(
                                    kvpack[:, 1024:1024 + NKV], ksum[:, :]), 1)

                            if dbg:
                                @SY
                                def _(q):
                                    wait(q, evXnH1[0])
                                    sSY.inc(q.dma_start(out=xn_dump, in_=xn_fm), 16)
                                    wait(q, evKVtmDone)
                                    sSY.inc(q.dma_start(out=ktm_dump, in_=k_tm[:, :]), 16)
                                    sSY.inc(q.dma_start(out=vtm_dump, in_=v_tm[:, :]), 16)

                @GP
                def _(g):
                    wait(g, evKvpack[0])
                    sW.inc(g.dma_start(out=cc_in, in_=kvpack[:, :]), 16)
                    g.wait_ge(sW.h, sW.n)
                    ins = g.collective_compute(
                        "AllReduce", ALU.add,
                        replica_groups=[list(range(NCORES))],
                        ins=[cc_in], outs=[cc_out])
                    ins.then_inc(sCC.h, 1)
                    sCC.n += 1

                evGate = {}
                evWgH = {}
                evWdH = {}
                evUp = {}
                evSilu = {}
                evGuEvac = {}
                evWg = {}
                evWu = {}
                evF = {}
                evFEvac = {}
                evFSt = {}
                evFRead = {}
                evAdd = {}
                evG = {}
                evGSt = {}
                evWd = {}

                with nc.sbuf_tensor("gu", [128, ICA * T], BF16) as gu_t:
                    gu = gu_t[:, :]

                    def gateup(mega):
                        icl = list(range(0, ICA)) if mega == 0 else list(range(ICA, ICN))
                        BB = barrier()
                        with ExitStack() as ctx:
                            wg_db = ctx.enter_context(nc.sbuf_tensor("wg_db%d" % mega, [128, 2 * H], BF16))
                            wu_db = ctx.enter_context(nc.sbuf_tensor("wu_db%d" % mega, [128, 2 * H], BF16))
                            sg_st = ctx.enter_context(nc.sbuf_tensor("sg_st%d" % mega, [128, T], F32))
                            pg = [ctx.enter_context(nc.psum_tensor("pg%d_%d" % (i, mega), [128, 1024], F32))
                                  for i in range(2)]
                            pu = [ctx.enter_context(nc.psum_tensor("pu%d_%d" % (i, mega), [128, 1024], F32))
                                  for i in range(2)]

                            @GP
                            def _(g):
                                wait(g, *BB)
                                ic0 = icl[0]
                                sl0 = ic0 % 2
                                q4 = H // 4
                                evWgH[ic0] = [
                                    (ssem(ic0) if k == 0 else sW).inc(g.dma_start(
                                        out=wg_db[:, sl0 * H + k * q4: sl0 * H + (k + 1) * q4],
                                        in_=wg[ic0 * 128:(ic0 + 1) * 128, k * q4:(k + 1) * q4]), 16)
                                    for k in range(4)]
                                evWg[ic0] = evWgH[ic0][3]
                                evWu[ic0] = ssem2(ic0).inc(g.dma_start(
                                    out=wu_db[:, sl0 * H:(sl0 + 1) * H],
                                    in_=wu[ic0 * 128:(ic0 + 1) * 128, :]), 16)
                                for ic in icl[1:2]:
                                    evWg[ic] = ssem(ic).inc(g.dma_start(
                                        out=wg_db[:, (ic % 2) * H:(ic % 2 + 1) * H],
                                        in_=wg[ic * 128:(ic + 1) * 128, :]), 16)
                                    evWu[ic] = ssem2(ic).inc(g.dma_start(
                                        out=wu_db[:, (ic % 2) * H:(ic % 2 + 1) * H],
                                        in_=wu[ic * 128:(ic + 1) * 128, :]), 16)

                            first = [True, True, True]
                            for idx, ic in enumerate(icl):
                                sl = ic % 2

                                @TT
                                def _(t, ic=ic, sl=sl, idx=idx):
                                    if first[0]:
                                        twait(t, *BB)
                                        first[0] = False
                                    if idx == 0:
                                        twait(t, evWgH[ic][0])
                                    else:
                                        twait(t, evWg[ic])
                                    if idx >= 2:
                                        prev = icl[idx - 2]
                                        twait(t, evSilu.get(prev), evGuEvac.get(prev))
                                    last = None
                                    for hc in range(HC):
                                        if idx == 0 and hc > 0 and hc % (HC // 4) == 0:
                                            twait(t, evWgH[ic][hc // (HC // 4)])
                                        for half in (0, 1):
                                            last = t.matmul(
                                                pg[sl][:, half * 512:(half + 1) * 512],
                                                wg_db[:, sl * H + hc * 128: sl * H + (hc + 1) * 128],
                                                xn_fm[:, hc * T + half * 512: hc * T + half * 512 + 512],
                                                start=(hc == 0), stop=(hc == HC - 1))
                                    evGate[ic] = sPE.inc(last, 1)
                                    twait(t, evWu[ic])
                                    for hc in range(HC):
                                        for half in (0, 1):
                                            last = t.matmul(
                                                pu[sl][:, half * 512:(half + 1) * 512],
                                                wu_db[:, sl * H + hc * 128: sl * H + (hc + 1) * 128],
                                                xn_fm[:, hc * T + half * 512: hc * T + half * 512 + 512],
                                                start=(hc == 0), stop=(hc == HC - 1))
                                    evUp[ic] = sPE.inc(last, 1)

                                @GP
                                def _(g, idx=idx):
                                    if idx + 2 < len(icl):
                                        ic2 = icl[idx + 2]
                                        wait(g, evGate[icl[idx]], evUp[icl[idx]])
                                        evWg[ic2] = ssem(ic2).inc(g.dma_start(
                                            out=wg_db[:, (ic2 % 2) * H:(ic2 % 2 + 1) * H],
                                            in_=wg[ic2 * 128:(ic2 + 1) * 128, :]), 16)
                                        evWu[ic2] = ssem2(ic2).inc(g.dma_start(
                                            out=wu_db[:, (ic2 % 2) * H:(ic2 % 2 + 1) * H],
                                            in_=wu[ic2 * 128:(ic2 + 1) * 128, :]), 16)

                                @AA
                                def _(a, ic=ic, sl=sl, idx=idx):
                                    if first[1]:
                                        wait(a, *BB)
                                        first[1] = False
                                    wait(a, evGate[ic])
                                    if idx >= 1:
                                        wait(a, evGuEvac.get(icl[idx - 1]))
                                    evSilu[ic] = sACT.inc(a.activation(
                                        out=sg_st[:, :], in_=pg[sl][:, :], func=AF.Silu), 1)

                                @VV
                                def _(v, ic=ic, sl=sl, idx=idx):
                                    if first[2]:
                                        wait(v, *BB)
                                        first[2] = False
                                    wait(v, evSilu[ic], evUp[ic])
                                    evGuEvac[ic] = sDVE.inc(v.tensor_mul(
                                        gu[:, idx * T:(idx + 1) * T], sg_st[:, :], pu[sl][:, :]), 1)

                    def down_phase(mega, g_write):
                        icl = list(range(0, ICA)) if mega == 0 else list(range(ICA, ICN))
                        nIC = len(icl)
                        CW = nIC * 128
                        lastGu = evGuEvac[icl[-1]]
                        BB = barrier()
                        with ExitStack() as ctx:
                            wd_db = ctx.enter_context(nc.sbuf_tensor("wd_db%d" % mega, [128, 2 * CW], BF16))
                            if g_write:
                                frd = ctx.enter_context(nc.sbuf_tensor("frd", [128, 2 * T], F32))
                                tmpf = ctx.enter_context(nc.sbuf_tensor("tmpf", [128, 2 * T], F32))
                                g_st = ctx.enter_context(nc.sbuf_tensor("g_st", [128, 2 * T], BF16))
                            else:
                                f_st = ctx.enter_context(nc.sbuf_tensor("f_st%d" % mega, [128, 2 * T], F32))
                            pf = [ctx.enter_context(nc.psum_tensor("pf%d_%d" % (i, mega), [128, 1024], F32))
                                  for i in range(4)]

                            @GP
                            def _(g):
                                wait(g, *BB)
                                cq = CW // 4
                                evWdH[mega] = [
                                    (ssem(0) if k == 0 else sW).inc(g.dma_start(
                                        out=wd_db[:, k * cq:(k + 1) * cq],
                                        in_=wd[0:128, mega * ICA * 128 + k * cq:
                                               mega * ICA * 128 + (k + 1) * cq]), 16)
                                    for k in range(4)]
                                evWd[(mega, 0)] = evWdH[mega][3]
                                evWd[(mega, 1)] = ssem(1).inc(g.dma_start(
                                    out=wd_db[:, CW:2 * CW],
                                    in_=wd[128:256, mega * ICA * 128: mega * ICA * 128 + CW]), 16)

                            if g_write:
                                @SY
                                def _(q):
                                    wait(q, *BB)
                                    for oc in (0, 1):
                                        wait(q, evFSt[oc])
                                        evFRead[oc] = ssem2(oc).inc(q.dma_start(
                                            out=frd[:, (oc % 2) * T:(oc % 2 + 1) * T],
                                            in_=f_sc[oc * 128:(oc + 1) * 128, :]), 16)

                            first = [True, True, True, True]
                            for oc in range(HC):
                                sl = oc % 2

                                @TT
                                def _(t, oc=oc, sl=sl):
                                    if first[0]:
                                        twait(t, *BB)
                                        first[0] = False
                                    if oc == 0:
                                        twait(t, evWdH[mega][0], lastGu)
                                    else:
                                        twait(t, evWd[(mega, oc)], lastGu)
                                    if g_write:
                                        twait(t, evAdd.get(oc - 4))
                                    else:
                                        twait(t, evFEvac.get(oc - 4))
                                    pj = pf[oc % 4]
                                    last = None
                                    got_q = 1
                                    for li in range(nIC):
                                        if oc == 0:
                                            need = ((li + 1) * 128 + CW // 4 - 1) // (CW // 4)
                                            if need > got_q:
                                                twait(t, evWdH[mega][need - 1])
                                                got_q = need
                                        for half in (0, 1):
                                            last = t.matmul(
                                                pj[:, half * 512:(half + 1) * 512],
                                                wd_db[:, sl * CW + li * 128: sl * CW + (li + 1) * 128],
                                                gu[:, li * T + half * 512: li * T + half * 512 + 512],
                                                start=(li == 0), stop=(li == nIC - 1))
                                    evF[(mega, oc)] = sPE.inc(last, 1)

                                @GP
                                def _(g, oc=oc):
                                    if oc + 2 < HC:
                                        wait(g, evF[(mega, oc)])
                                        s2 = (oc + 2) % 2
                                        evWd[(mega, oc + 2)] = ssem(oc).inc(g.dma_start(
                                            out=wd_db[:, s2 * CW:(s2 + 1) * CW],
                                            in_=wd[(oc + 2) * 128:(oc + 3) * 128,
                                                   mega * ICA * 128: mega * ICA * 128 + CW]), 16)

                                if not g_write:
                                    @VV
                                    def _(v, oc=oc, sl=sl):
                                        if first[1]:
                                            wait(v, *BB)
                                            first[1] = False
                                        wait(v, evF[(0, oc)], evFSt.get(oc - 2))
                                        evFEvac[oc] = sDVE.inc(v.tensor_copy(
                                            f_st[:, sl * T:(sl + 1) * T], pf[oc % 4][:, :]), 1)

                                    @SY
                                    def _(q, oc=oc, sl=sl):
                                        if first[3]:
                                            wait(q, *BB)
                                            first[3] = False
                                        wait(q, evFEvac[oc])
                                        evFSt[oc] = sSY.inc(q.dma_start(
                                            out=f_sc[oc * 128:(oc + 1) * 128, :],
                                            in_=f_st[:, sl * T:(sl + 1) * T]), 16)
                                else:
                                    @VV
                                    def _(v, oc=oc, sl=sl):
                                        if first[1]:
                                            wait(v, *BB)
                                            first[1] = False
                                        wait(v, evF[(1, oc)], evFRead[oc], evG.get(oc - 2))
                                        evAdd[oc] = sDVE.inc(v.tensor_add(
                                            tmpf[:, sl * T:(sl + 1) * T],
                                            pf[oc % 4][:, :], frd[:, sl * T:(sl + 1) * T]), 1)

                                    @AA
                                    def _(a, oc=oc, sl=sl):
                                        if first[2]:
                                            wait(a, *BB)
                                            first[2] = False
                                        wait(a, evAdd[oc], evGSt.get(oc - 2))
                                        evG[oc] = sACT.inc(a.activation(
                                            out=g_st[:, sl * T:(sl + 1) * T],
                                            in_=tmpf[:, sl * T:(sl + 1) * T], func=AF.Silu), 1)

                                    @SY
                                    def _(q, oc=oc, sl=sl):
                                        if first[3]:
                                            wait(q, *BB)
                                            first[3] = False
                                        wait(q, evG[oc])
                                        evGSt[oc] = sSY.inc(q.dma_start(
                                            out=g_sc[oc * 128:(oc + 1) * 128, :],
                                            in_=g_st[:, sl * T:(sl + 1) * T]), 16)
                                        if oc + 2 < HC:
                                            wait(q, evAdd[oc])
                                            evFRead[oc + 2] = ssem2(oc).inc(q.dma_start(
                                                out=frd[:, sl * T:(sl + 1) * T],
                                                in_=f_sc[(oc + 2) * 128:(oc + 3) * 128, :]), 16)

                    gateup(0)
                    if dbg:
                        @SY
                        def _(q):
                            wait(q, evGuEvac[ICA - 1])
                            sSY.inc(q.dma_start(out=gu_dump, in_=gu_t[:, :]), 16)
                    down_phase(0, g_write=False)
                    gateup(1)
                    down_phase(1, g_write=True)

                evQmm = {}
                evWqH = {}
                evWoH = {}
                evTsQ = {}
                evActQ = {}
                evQbf = {}
                evWq = {}
                evKvg = [None]
                evRec = [None]
                evAmm = {}
                evAsc = {}
                evWo = {}
                evSmm = {}
                evP = {}
                evPSt = {}
                evGload = {}

                p_tm4 = nc.alloc_sbuf_tensor("p_tm4", [128, 5 * H], BF16, side="right")
                evPTw1 = [None]

                with nc.sbuf_tensor("q_bf", [128, NH * T], BF16) as q_bf_t:
                    q_bf = q_bf_t[:, :]
                    BQ = barrier()
                    with ExitStack() as ctq:
                        wq_db = ctq.enter_context(nc.sbuf_tensor("wq_db", [128, 2 * H], BF16))
                        qtmin = ctq.enter_context(nc.sbuf_tensor("qtmin", [128, 1024], F32))
                        qe = ctq.enter_context(nc.sbuf_tensor("qe", [128, 1024], F32))
                        qr = ctq.enter_context(nc.sbuf_tensor("qr", [128, 1024], F32))
                        pq = [ctq.enter_context(nc.psum_tensor("pq%d" % i, [128, 512], F32))
                              for i in range(4)]

                        @GP
                        def _(g):
                            wait(g, *BQ)
                            evWqH[0] = [
                                (ssem(0) if k == 0 else sW).inc(g.dma_start(
                                    out=wq_db[:, k * (H // 4):(k + 1) * (H // 4)],
                                    in_=wq[0:128, k * (H // 4):(k + 1) * (H // 4)]), 16)
                                for k in range(4)]
                            evWq[0] = evWqH[0][3]
                            evWq[1] = ssem(1).inc(g.dma_start(
                                out=wq_db[:, H:2 * H],
                                in_=wq[128:256, :]), 16)

                        first = [True, True, True]
                        for oc in range(NH):
                            sl = oc % 2

                            @TT
                            def _(t, oc=oc, sl=sl):
                                if first[0]:
                                    twait(t, *BQ)
                                    first[0] = False
                                twait(t, evWqH[0][0] if oc == 0 else evWq[oc])
                                for b in (0, 1):
                                    twait(t, evActQ.get((oc - 2, b)))
                                lastmm = {}
                                for hc in range(HC):
                                    if oc == 0 and hc > 0 and hc % (HC // 4) == 0:
                                        twait(t, evWqH[0][hc // (HC // 4)])
                                    for half in (0, 1):
                                        bank = (oc % 2) * 2 + half
                                        lastmm[half] = t.matmul(
                                            pq[bank][:, :],
                                            wq_db[:, sl * H + hc * 128: sl * H + (hc + 1) * 128],
                                            xn_fm[:, hc * T + half * 512: hc * T + half * 512 + 512],
                                            start=(hc == 0), stop=(hc == HC - 1))
                                evQmm[(oc, 0)] = sPE.inc(lastmm[0], 1)
                                evQmm[(oc, 1)] = sPE.inc(lastmm[1], 1)

                            @GP
                            def _(g, oc=oc):
                                if oc + 2 < NH:
                                    wait(g, evQmm[(oc, 1)])
                                    s2 = (oc + 2) % 2
                                    evWq[oc + 2] = ssem(oc).inc(g.dma_start(
                                        out=wq_db[:, s2 * H:(s2 + 1) * H],
                                        in_=wq[(oc + 2) * 128:(oc + 3) * 128, :]), 16)

                            @VV
                            def _(v, oc=oc):
                                if first[1]:
                                    wait(v, *BQ)
                                    first[1] = False
                                for half in (0, 1):
                                    bank = (oc % 2) * 2 + half
                                    wait(v, evQmm[(oc, half)], evActQ.get((oc - 1, half)))
                                    evTsQ[(oc, half)] = sDVE.inc(v.tensor_scalar(
                                        qtmin[:, half * 512:(half + 1) * 512], pq[bank][:, :],
                                        bq_sb[:, oc:oc + 1], 0.0, op0=ALU.add, op1=ALU.min), 1)

                            @AA
                            def _(a, oc=oc):
                                if first[2]:
                                    wait(a, *BQ)
                                    first[2] = False
                                for half in (0, 1):
                                    bank = (oc % 2) * 2 + half
                                    wait(a, evTsQ[(oc, half)], evQbf.get((oc - 1, half)))
                                    a.activation(out=qe[:, half * 512:(half + 1) * 512],
                                                 in_=qtmin[:, half * 512:(half + 1) * 512],
                                                 func=AF.Exp)
                                    evActQ[(oc, half)] = sACT.inc(a.activation(
                                        out=qr[:, half * 512:(half + 1) * 512],
                                        in_=pq[bank][:, :], func=AF.Relu,
                                        bias=bq_sb[:, oc:oc + 1]), 1)

                            @VV
                            def _(v, oc=oc):
                                for half in (0, 1):
                                    wait(v, evActQ[(oc, half)])
                                    evQbf[(oc, half)] = sDVE.inc(v.tensor_add(
                                        q_bf[:, oc * T + half * 512: oc * T + half * 512 + 512],
                                        qr[:, half * 512:(half + 1) * 512],
                                        qe[:, half * 512:(half + 1) * 512]), 1)

                    @GP
                    def _(g):
                        g.wait_ge(sCC.h, 1)
                        evKvg[0] = sSC.inc(g.dma_start(out=kvg_sb[:, :], in_=cc_out), 16)

                    @VV
                    def _(v):
                        wait(v, evKvg[0])
                        v.tensor_copy(kv_bf[:, :], kvg_sb[:, 0:KD])
                        vch(v, v.tensor_single_scalar(kvg_sb[:, KD:KD + NKV],
                                                      kvg_sb[:, KD:KD + NKV],
                                                      EPS_ATT, op=ALU.add))
                        evRec[0] = vch(v, v.reciprocal(rec_sb[:, :], kvg_sb[:, KD:KD + NKV]))

                    if dbg:
                        @SY
                        def _(q):
                            wait(q, evRec[0])
                            sSY.inc(q.dma_start(out=kvg_dump, in_=kvg_sb[:, :]), 16)
                            wait(q, evKvpack[0])
                            sSY.inc(q.dma_start(out=cc_dump, in_=kvpack[:, :]), 16)

                    BA = barrier()
                    with ExitStack() as cta:
                        pa = [cta.enter_context(nc.psum_tensor("pa%d" % i, [128, 512], F32))
                              for i in range(4)]
                        first = [True, True]
                        for h in range(NH):
                            hk = h // 4

                            @TT
                            def _(t, h=h, hk=hk):
                                if first[0]:
                                    twait(t, *BA)
                                    first[0] = False
                                twait(t, evRec[0], evQbf.get((NH - 1, 1)))
                                for half in (0, 1):
                                    bank = (h % 2) * 2 + half
                                    twait(t, evAsc.get((h - 2, half)))
                                    evAmm[(h, half)] = sPE.inc(t.matmul(
                                        pa[bank][:, :],
                                        kv_bf[:, hk * 128:(hk + 1) * 128],
                                        q_bf[:, h * T + half * 512: h * T + half * 512 + 512],
                                        start=True, stop=True), 1)

                            @VV
                            def _(v, h=h, hk=hk):
                                if first[1]:
                                    wait(v, *BA)
                                    first[1] = False
                                for half in (0, 1):
                                    bank = (h % 2) * 2 + half
                                    wait(v, evAmm[(h, half)])
                                    evAsc[(h, half)] = sDVE.inc(v.tensor_scalar(
                                        q_bf[:, h * T + half * 512: h * T + half * 512 + 512],
                                        pa[bank][:, :], rec_sb[:, hk:hk + 1], None,
                                        op0=ALU.mult), 1)

                    if dbg:
                        @SY
                        def _(q):
                            wait(q, evAsc[(NH - 1, 1)])
                            sSY.inc(q.dma_start(out=q_dump, in_=q_bf), 16)

                    BO = barrier()
                    with ExitStack() as cto:
                        wo_db = cto.enter_context(nc.sbuf_tensor("wo_db", [128, 2 * H], BF16))
                        g_buf = cto.enter_context(nc.sbuf_tensor("g_buf", [128, 2 * T], BF16))
                        p_st = cto.enter_context(nc.sbuf_tensor("p_st", [128, 2 * T], BF16))
                        ps = [cto.enter_context(nc.psum_tensor("ps%d" % i, [128, 512], F32))
                              for i in range(4)]
                        pt2 = [cto.enter_context(nc.psum_tensor("pt2_%d" % i, [128, 384], BF16))
                               for i in range(2)]
                        ptm_ext3 = xn_fm_t[:, :].rearrange("p (s f) -> p s f", f=H)
                        evPtW2 = {}
                        evPtR2 = [None, None]

                        def wo_tp(oc):
                            sl2 = oc % 2

                            @TT
                            def _(t, oc=oc, sl2=sl2):
                                twait(t, evP[(oc, 1)], evPtR2[sl2])
                                ins = None
                                for k in range(3):
                                    ins = t.transpose(
                                        pt2[sl2][:, k * 128:(k + 1) * 128],
                                        p_st[:, sl2 * T + (5 + k) * 128: sl2 * T + (6 + k) * 128],
                                        ident[:, :])
                                evPtW2[oc] = sPE.inc(ins, 1)

                            @VV
                            def _(v, oc=oc, sl2=sl2):
                                wait(v, evPtW2[oc])
                                evPtR2[sl2] = sDVE.inc(v.tensor_copy(
                                    ptm_ext3[:, 0:3, oc * 128:(oc + 1) * 128],
                                    pt2[sl2].rearrange("p (k f) -> p k f", f=128)[:, 0:3, :]), 1)

                        @GP
                        def _(g):
                            wait(g, *BO)
                            evWoH[0] = [
                                (ssem(0) if k == 0 else sW).inc(g.dma_start(
                                    out=wo_db[:, k * (H // 4):(k + 1) * (H // 4)],
                                    in_=wo[0:128, k * (H // 4):(k + 1) * (H // 4)]), 16)
                                for k in range(4)]
                            evWo[0] = evWoH[0][3]
                            evWo[1] = ssem(1).inc(g.dma_start(
                                out=wo_db[:, H:2 * H],
                                in_=wo[128:256, :]), 16)

                        @SY
                        def _(q):
                            wait(q, *BO)
                            for oc in (0, 1):
                                wait(q, evGSt[oc])
                                evGload[oc] = ssem2(oc).inc(q.dma_start(
                                    out=g_buf[:, (oc % 2) * T:(oc % 2 + 1) * T],
                                    in_=g_sc[oc * 128:(oc + 1) * 128, :]), 16)

                        first = [True, True, True]
                        for oc in range(HC):
                            sl = oc % 2

                            @TT
                            def _(t, oc=oc, sl=sl):
                                if first[0]:
                                    twait(t, *BO)
                                    first[0] = False
                                twait(t, evWoH[0][0] if oc == 0 else evWo[oc],
                                      evAsc.get((NH - 1, 1)))
                                for b in (0, 1):
                                    twait(t, evP.get((oc - 2, b)))
                                lastmm = {}
                                for vc in range(HC):
                                    if oc == 0 and vc > 0 and vc % (HC // 4) == 0:
                                        twait(t, evWoH[0][vc // (HC // 4)])
                                    for half in (0, 1):
                                        bank = (oc % 2) * 2 + half
                                        lastmm[half] = t.matmul(
                                            ps[bank][:, :],
                                            wo_db[:, sl * H + vc * 128: sl * H + (vc + 1) * 128],
                                            q_bf[:, vc * T + half * 512: vc * T + half * 512 + 512],
                                            start=(vc == 0), stop=(vc == HC - 1))
                                evSmm[(oc, 0)] = sPE.inc(lastmm[0], 1)
                                evSmm[(oc, 1)] = sPE.inc(lastmm[1], 1)

                            @GP
                            def _(g, oc=oc):
                                if oc + 2 < HC:
                                    wait(g, evSmm[(oc, 1)])
                                    s2 = (oc + 2) % 2
                                    evWo[oc + 2] = ssem(oc).inc(g.dma_start(
                                        out=wo_db[:, s2 * H:(s2 + 1) * H],
                                        in_=wo[(oc + 2) * 128:(oc + 3) * 128, :]), 16)

                            if oc >= 1:
                                wo_tp(oc - 1)

                            @VV
                            def _(v, oc=oc, sl=sl):
                                if first[1]:
                                    wait(v, *BO)
                                    first[1] = False
                                for half in (0, 1):
                                    bank = (oc % 2) * 2 + half
                                    wait(v, evSmm[(oc, half)], evGload[oc], evPSt.get(oc - 2),
                                         evPtW2.get(oc - 2))
                                    evP[(oc, half)] = sDVE.inc(v.scalar_tensor_tensor(
                                        p_st[:, sl * T + half * 512: sl * T + half * 512 + 512],
                                        ps[bank][:, :], bo_sb[:, oc:oc + 1],
                                        g_buf[:, sl * T + half * 512: sl * T + half * 512 + 512],
                                        op0=ALU.add, op1=ALU.mult), 1)

                            @SY
                            def _(q, oc=oc, sl=sl):
                                if first[2]:
                                    wait(q, *BO)
                                    first[2] = False
                                wait(q, evP[(oc, 1)])
                                evPSt[oc] = sSY.inc(q.dma_start(
                                    out=p_sc[oc * 128:(oc + 1) * 128, 0:640],
                                    in_=p_st[:, sl * T: sl * T + 640]), 16)
                                if oc + 2 < HC:
                                    wait(q, evGSt.get(oc + 2))
                                    s2 = (oc + 2) % 2
                                    evGload[oc + 2] = ssem2(oc).inc(q.dma_start(
                                        out=g_buf[:, s2 * T:(s2 + 1) * T],
                                        in_=g_sc[(oc + 2) * 128:(oc + 3) * 128, :]), 16)

                            @AA
                            def _(a, oc=oc):
                                if oc % 4 == 3:
                                    wait(a, evPSt[oc])
                                    lo = (oc - 3) * 128
                                    hi = (oc + 1) * 128
                                    for tc4 in range(5):
                                        sSC.inc(a.dma_start(
                                            out=p_tm4[:, tc4 * H + lo: tc4 * H + hi],
                                            in_=p_sc[lo:hi, tc4 * 128:(tc4 + 1) * 128],
                                            transpose=True), 16)
                                    if oc == HC - 1:
                                        evPTw1[0] = (sSC, sSC.n)

                        wo_tp(HC - 1)
                        evPTX = (sDVE, sDVE.n)

                BE2 = [(s, s.n) for s in ALLSEMS if s is not sSC]
                evPTw1v = evPTw1[0]
                ORDER = [5, 6, 7, 0, 1, 2, 3, 4]
                with ExitStack() as cte:
                    x2 = cte.enter_context(nc.sbuf_tensor("x2", [128, 2 * H], F32))
                    outb = cte.enter_context(nc.sbuf_tensor("outb", [128, 2 * H], F32))
                    evX2 = {}
                    evEpi = {}
                    evYSt = {}
                    evYStI = {}

                    @SY
                    def _(q):
                        wait(q, *BE2)

                    @VV
                    def _(v):
                        wait(v, *BE2)

                    @GP
                    def _(g):
                        wait(g, *BE2)
                        for idx in (0, 1):
                            tc = ORDER[idx]
                            evX2[tc] = sAp.inc(g.dma_start(
                                out=x2[:, (idx % 2) * H:(idx % 2 + 1) * H],
                                in_=x_in[tc * 128:(tc + 1) * 128, :]), 16)

                    for idx, tc in enumerate(ORDER):
                        sl = idx % 2

                        @VV
                        def _(v, tc=tc, sl=sl, idx=idx):
                            src = (p_tm4[:, tc * H:(tc + 1) * H] if tc < 5
                                   else xn_fm[:, (tc - 5) * H:(tc - 4) * H])
                            wait(v, evX2[tc], evPTw1v if tc < 5 else evPTX,
                                 evYStI.get(idx - 2))
                            evEpi[tc] = sDVE.inc(v.scalar_tensor_tensor(
                                outb[:, sl * H:(sl + 1) * H], src, 1.0,
                                x2[:, sl * H:(sl + 1) * H],
                                op0=ALU.mult, op1=ALU.add), 1)

                        @(SY if idx % 2 == 0 else AA)
                        def _(q, tc=tc, sl=sl, idx=idx):
                            wait(q, evEpi[tc])
                            ev = (sYa if idx % 2 == 0 else sXNb).inc(q.dma_start(
                                out=y_out[tc * 128:(tc + 1) * 128, :],
                                in_=outb[:, sl * H:(sl + 1) * H]), 16)
                            evYSt[tc] = ev
                            evYStI[idx] = ev

                        if idx + 2 < NTC:
                            @GP
                            def _(g, tc=tc, idx=idx):
                                wait(g, evEpi[tc])
                                tc2 = ORDER[idx + 2]
                                evX2[tc2] = sAp.inc(g.dma_start(
                                    out=x2[:, (idx % 2) * H:(idx % 2 + 1) * H],
                                    in_=x_in[tc2 * 128:(tc2 + 1) * 128, :]), 16)

                    @SY
                    def _(q):
                        wait(q, evYSt[3], evYSt[4])

    return nc



def _wtile(w):
    O, Hin = w.shape
    wt = np.ascontiguousarray(w.T)
    wt = wt.reshape(Hin // 128, 128, O // 128, 128)
    wt = wt.transpose(2, 1, 0, 3)
    return np.ascontiguousarray(wt).reshape(O, Hin)


def _btile(b):
    return np.ascontiguousarray(np.asarray(b, np.float32).reshape(-1, 128).T)


def kernel(L0_attn_hidden_states, L0_attn_Wq, L0_attn_bq, L0_attn_Wk, L0_attn_bk,
           L0_attn_Wv, L0_attn_bv, L0_attn_Wo, L0_attn_bo,
           L0_ffn_w_gate, L0_ffn_w_up, L0_ffn_w_down):
    x = np.ascontiguousarray(np.asarray(L0_attn_hidden_states, np.float32).reshape(8192, 4096))
    w_host = {
        "wq": _wtile(np.asarray(L0_attn_Wq, np.float32)),
        "wk": _wtile(np.asarray(L0_attn_Wk, np.float32)),
        "wv": _wtile(np.asarray(L0_attn_Wv, np.float32)),
        "wo": _wtile(np.asarray(L0_attn_Wo, np.float32)),
        "wg": _wtile(np.asarray(L0_ffn_w_gate, np.float32)),
        "wu": _wtile(np.asarray(L0_ffn_w_up, np.float32)),
        "wd": _wtile(np.asarray(L0_ffn_w_down, np.float32)),
        "bq": _btile(L0_attn_bq),
        "bk": _btile(L0_attn_bk),
        "bv": _btile(L0_attn_bv),
        "bo": _btile(L0_attn_bo),
        "ident_in": np.eye(128, dtype=np.float32),
    }
    nc = build_nc(dbg=DBG)
    in_maps = []
    for c in range(NCORES):
        m = {"x": np.ascontiguousarray(x[c * T:(c + 1) * T])}
        m.update(w_host)
        in_maps.append(m)
    res = run_bass_kernel_spmd(nc, in_maps, core_ids=list(range(NCORES)))
    out = np.concatenate([r["y"] for r in res.results], axis=0)
    if DBG:
        kernel.last_results = res.results
    return out.reshape(1, 8192, 4096)


if __name__ == "__main__":
    import time
    t0 = time.time()
    build_nc()
    print("build ok in %.1fs" % (time.time() - t0))
